# version 16
# baseline (speedup 1.0000x reference)
import contextlib
import numpy as np
import ml_dtypes
import concourse.bass as bass
import concourse.mybir as mybir
from concourse.bass_utils import run_bass_kernel_spmd

F32 = mybir.dt.float32
BF16 = mybir.dt.bfloat16
AF = mybir.ActivationFunctionType
ALU = mybir.AluOpType

NCORES = 8
DM = 2048
NF = 16
NH = 16
S = 8192
TOK = 1024
NT = 8
II = 8192
EPS = 1e-5
SCALE = 1.0 / float(np.sqrt(128.0))
EXP_BIAS = -2.0


class Sem:
    def __init__(self, nc, es, name):
        self.h = es.enter_context(nc.semaphore(name))
        self.n = 0

    def inc(self, k=1):
        self.n += k
        return self.n


class Plan:
    def __init__(self):
        self.items = []

    def add(self, eng, fn):
        self.items.append((eng, fn))

    def run(self, name, engobj):
        for e, fn in self.items:
            if e == name:
                fn(engobj)


def build_program(debug=False):
    nc = bass.Bass("TRN2", target_bir_lowering=False)

    x_in = nc.dram_tensor("x", [DM, TOK], F32, kind="ExternalInput").ap()
    wq_in = nc.dram_tensor("wq", [128, NF * DM], F32, kind="ExternalInput").ap()
    wk_in = nc.dram_tensor("wk", [128, NF * DM], F32, kind="ExternalInput").ap()
    wv_in = nc.dram_tensor("wv", [128, NF * DM], F32, kind="ExternalInput").ap()
    wo_in = nc.dram_tensor("wo", [128, NF * DM], F32, kind="ExternalInput").ap()
    wg_in = nc.dram_tensor("wg", [128, NF * II], F32, kind="ExternalInput").ap()
    wu_in = nc.dram_tensor("wu", [128, NF * II], F32, kind="ExternalInput").ap()
    wd_in = nc.dram_tensor("wd", [128, 64 * DM], F32, kind="ExternalInput").ap()
    bq_in = nc.dram_tensor("bq", [128, 16], F32, kind="ExternalInput").ap()
    bk_in = nc.dram_tensor("bk", [128, 16], F32, kind="ExternalInput").ap()
    bo_in = nc.dram_tensor("bo", [128, 16], F32, kind="ExternalInput").ap()
    bv_in = nc.dram_tensor("bvr", [1, DM], F32, kind="ExternalInput").ap()
    lng_in = nc.dram_tensor("lng", [128, 16], F32, kind="ExternalInput").ap()
    lnb_in = nc.dram_tensor("lnb", [128, 16], F32, kind="ExternalInput").ap()
    masks_in = nc.dram_tensor("masks", [8, 128, 896], BF16, kind="ExternalInput").ap()
    ident_in = nc.dram_tensor("identw", [128, 128], BF16, kind="ExternalInput").ap()
    out_t = nc.dram_tensor("out", [DM, TOK], F32, kind="ExternalOutput").ap()

    cc_in = nc.dram_tensor("cc_in", [4096, 1024], BF16, kind="Internal")
    cc_out = nc.dram_tensor("cc_out", [NCORES * 4096, 1024], BF16, kind="Internal")
    q_dram = nc.dram_tensor("q_dram", [NH, 128, TOK], BF16, kind="Internal")
    h_dram = nc.dram_tensor("h_dram", [NT, 128, II], F32, kind="Internal")
    hs_fm_dram = nc.dram_tensor("hs_fm", [64, 128, TOK], BF16, kind="Internal")
    if debug:
        dbg_xn = nc.dram_tensor("dbg_xn", [128, NF * TOK], BF16, kind="ExternalOutput").ap()
        dbg_x2 = nc.dram_tensor("dbg_x2", [128, NF * TOK], F32, kind="ExternalOutput").ap()
        dbg_ccin = nc.dram_tensor("dbg_ccin", [4096, 1024], BF16, kind="ExternalOutput").ap()
        dbg_q = nc.dram_tensor("dbg_q", [NH, 128, TOK], BF16, kind="ExternalOutput").ap()
        dbg_h = nc.dram_tensor("dbg_h", [NT, 128, II], F32, kind="ExternalOutput").ap()
        dbg_hs = nc.dram_tensor("dbg_hs", [64, 128, TOK], BF16, kind="ExternalOutput").ap()
        dbg_y = nc.dram_tensor("dbg_y", [128, NF * TOK], F32, kind="ExternalOutput").ap()

    es = contextlib.ExitStack()
    with es:
        stream = es.enter_context(nc.sbuf_tensor("stream", [128, NF * TOK], F32))
        cpool = es.enter_context(nc.sbuf_tensor("cpool", [128, 8], F32))
        bq_sb = es.enter_context(nc.sbuf_tensor("bq_sb", [128, 16], F32))
        bk_sb = es.enter_context(nc.sbuf_tensor("bk_sb", [128, 16], F32))
        bo_sb = es.enter_context(nc.sbuf_tensor("bo_sb", [128, 16], F32))
        lng_sb = es.enter_context(nc.sbuf_tensor("lng_sb", [128, 16], F32))
        lnb_sb = es.enter_context(nc.sbuf_tensor("lnb_sb", [128, 16], F32))
        ones128 = es.enter_context(nc.sbuf_tensor("ones128", [128, 128], BF16))
        ident = es.enter_context(nc.sbuf_tensor("ident", [128, 128], BF16))
        onescol = es.enter_context(nc.sbuf_tensor("onescol", [128, 1], BF16))
        onesrow = es.enter_context(nc.sbuf_tensor("onesrow", [1, 128], F32))

        sx = Sem(nc, es, "sx")
        s_w = [Sem(nc, es, f"s_w{i}") for i in range(2)]
        s_st = [Sem(nc, es, f"s_st{i}") for i in range(2)]
        s_vst = [Sem(nc, es, f"s_vst{i}") for i in range(2)]
        s_qst = [Sem(nc, es, f"s_qst{i}") for i in range(2)]
        s_kv = [Sem(nc, es, f"s_kv{i}") for i in range(2)]
        s_hst = [Sem(nc, es, f"s_hst{i}") for i in range(4)]
        s_tp = [Sem(nc, es, f"s_tp{i}") for i in range(4)]
        s_wd = [Sem(nc, es, f"s_wd{i}") for i in range(3)]
        scast = Sem(nc, es, "scast")
        skvst = Sem(nc, es, "skvst")
        scc = Sem(nc, es, "scc")
        skvf = Sem(nc, es, "skvf")
        sh = Sem(nc, es, "sh")
        shs = Sem(nc, es, "shs")
        sout = Sem(nc, es, "sout")
        spe = Sem(nc, es, "spe")
        sact = Sem(nc, es, "sact")
        sdve = Sem(nc, es, "sdve")

        C_SCALE, C_EBIAS, C_EPS, C_NHALF, C_NONE = 0, 1, 2, 3, 4
        CVALS = {C_SCALE: SCALE, C_EBIAS: EXP_BIAS, C_EPS: EPS,
                 C_NHALF: -0.5, C_NONE: -1.0}

        def cp(i):
            return cpool[:, i:i + 1]

        def emit_layernorm(name, src_is_xb, xb_ap, with_gb, out_fn, pre_wait):
            with contextlib.ExitStack() as ps:
                s1_ps = ps.enter_context(nc.psum_tensor(name + "_s1", [128, TOK], F32))
                s2_ps = ps.enter_context(nc.psum_tensor(name + "_s2", [128, TOK], F32))
                dln_ps = ps.enter_context(nc.psum_tensor(name + "_dum", [32, 1], F32))
                xsq = ps.enter_context(nc.sbuf_tensor(name + "_xsq", [128, 2 * TOK], BF16))
                xbf = None
                if not src_is_xb:
                    xbf = ps.enter_context(nc.sbuf_tensor(name + "_xbf", [128, 2 * TOK], BF16))
                mu = ps.enter_context(nc.sbuf_tensor(name + "_mu", [128, TOK], F32))
                rstd = ps.enter_context(nc.sbuf_tensor(name + "_rstd", [128, TOK], F32))
                m2 = ps.enter_context(nc.sbuf_tensor(name + "_m2", [128, TOK], F32))
                ta = ps.enter_context(nc.sbuf_tensor(name + "_ta", [128, TOK], F32))

                plan = Plan()
                cast_dve = {}
                sq_act = {}
                mm_pe = {}
                for f in range(NF):
                    if xbf is not None:
                        def cast_fn(eng, ff=f):
                            if ff >= 2:
                                eng.wait_ge(spe.h, mm_pe[ff - 2])
                            eng.tensor_scalar(
                                out=xbf[:, (ff % 2) * TOK:(ff % 2 + 1) * TOK],
                                in0=stream[:, ff * TOK:(ff + 1) * TOK],
                                scalar1=1.0, scalar2=None,
                                op0=ALU.mult).then_inc(sdve.h, 1)
                        cast_dve[f] = sdve.inc()
                        plan.add("dve", cast_fn)

                    def sq_fn(eng, ff=f):
                        if xbf is not None:
                            eng.wait_ge(sdve.h, cast_dve[ff])
                        elif ff >= 2:
                            eng.wait_ge(spe.h, mm_pe[ff - 2])
                        src = (xbf[:, (ff % 2) * TOK:(ff % 2 + 1) * TOK]
                               if xbf is not None
                               else xb_ap[:, ff * TOK:(ff + 1) * TOK])
                        eng.activation(out=xsq[:, (ff % 2) * TOK:(ff % 2 + 1) * TOK],
                                       in_=src, func=AF.Square).then_inc(sact.h, 1)
                    sq_act[f] = sact.inc()
                    plan.add("act", sq_fn)

                    def mm_fn(eng, ff=f):
                        eng.wait_ge(sact.h, sq_act[ff])
                        src1 = (xbf[:, (ff % 2) * TOK:(ff % 2 + 1) * TOK]
                                if xbf is not None
                                else xb_ap[:, ff * TOK:(ff + 1) * TOK])
                        src2 = xsq[:, (ff % 2) * TOK:(ff % 2 + 1) * TOK]
                        st = (ff == 0)
                        sp = (ff == NF - 1)
                        eng.matmul(s1_ps[:, 0:512], ones128[:], src1[:, 0:512], start=st, stop=sp)
                        eng.matmul(s1_ps[:, 512:1024], ones128[:], src1[:, 512:1024], start=st, stop=sp)
                        eng.matmul(s2_ps[:, 0:512], ones128[:], src2[:, 0:512], start=st, stop=sp)
                        eng.matmul(s2_ps[:, 512:1024], ones128[:], src2[:, 512:1024],
                                   start=st, stop=sp).then_inc(spe.h, 1)
                        if sp:
                            eng.matmul(dln_ps[:], ones128[:, 0:32], onescol[:],
                                       start=True, stop=True).then_inc(spe.h, 1)
                    mm_pe[f] = spe.inc()
                    if f == NF - 1:
                        spe.inc()
                    plan.add("pe", mm_fn)

                stats_done = mm_pe[NF - 1] + 1

                def mu_fn(eng):
                    eng.wait_ge(spe.h, stats_done)
                    eng.tensor_scalar(out=mu[:], in0=s1_ps[:], scalar1=1.0 / 2048.0,
                                      scalar2=None, op0=ALU.mult).then_inc(sdve.h, 1)
                mu_dve = sdve.inc()
                plan.add("dve", mu_fn)

                def musq_fn(eng):
                    eng.wait_ge(sdve.h, mu_dve)
                    eng.activation(out=ta[:], in_=mu[:], func=AF.Square).then_inc(sact.h, 1)
                musq_act = sact.inc()
                plan.add("act", musq_fn)

                def var_fn(eng):
                    eng.wait_ge(sact.h, musq_act)
                    eng.scalar_tensor_tensor(
                        out=rstd[:], in0=s2_ps[:], scalar=1.0 / 2048.0, in1=ta[:],
                        op0=ALU.mult, op1=ALU.subtract).then_inc(sdve.h, 1)
                var_dve = sdve.inc()
                plan.add("dve", var_fn)

                rstd_act = sact.inc(2)

                def rstd_fn(eng):
                    eng.wait_ge(sdve.h, var_dve)
                    eng.activation(out=rstd[:], in_=rstd[:], func=AF.Ln,
                                   bias=cp(C_EPS)).then_inc(sact.h, 1)
                    eng.activation(out=rstd[:], in_=rstd[:], func=AF.Exp,
                                   scale=cp(C_NHALF)).then_inc(sact.h, 1)
                plan.add("act", rstd_fn)

                def m2_fn(eng):
                    eng.wait_ge(sact.h, rstd_act)
                    eng.tensor_mul(out=m2[:], in0=mu[:], in1=rstd[:]).then_inc(sdve.h, 1)
                m2_dve = sdve.inc()
                plan.add("dve", m2_fn)

                apply_dve = {}
                for f in range(NF):
                    def ap_fn(eng, ff=f):
                        if ff == 0:
                            eng.wait_ge(sdve.h, m2_dve)
                        eng.tensor_mul(out=ta[:], in0=stream[:, ff * TOK:(ff + 1) * TOK],
                                       in1=rstd[:])
                        if with_gb:
                            eng.tensor_sub(out=ta[:], in0=ta[:], in1=m2[:])
                            eng.tensor_scalar(out=out_fn(ff), in0=ta[:],
                                              scalar1=lng_sb[:, ff:ff + 1],
                                              scalar2=lnb_sb[:, ff:ff + 1],
                                              op0=ALU.mult, op1=ALU.add).then_inc(sdve.h, 1)
                        else:
                            eng.tensor_sub(out=out_fn(ff), in0=ta[:],
                                           in1=m2[:]).then_inc(sdve.h, 1)
                    apply_dve[f] = sdve.inc()
                    plan.add("dve", ap_fn)

                with nc.Block(name) as blk:
                    @blk.tensor
                    def _(eng):
                        for sem, val in pre_wait.get("pe", []):
                            eng.wait_ge(sem.h, val)
                        plan.run("pe", eng)

                    @blk.scalar
                    def _(eng):
                        for sem, val in pre_wait.get("act", []):
                            eng.wait_ge(sem.h, val)
                        plan.run("act", eng)

                    @blk.vector
                    def _(eng):
                        for sem, val in pre_wait.get("dve", []):
                            eng.wait_ge(sem.h, val)
                        plan.run("dve", eng)
                return apply_dve

        with nc.sbuf_tensor("xb", [128, NF * TOK], BF16) as xb:
            with nc.Block("p0") as blk:
                @blk.sync
                def _(eng):
                    src = x_in.rearrange("(f p) t -> p f t", p=128)
                    dst = stream[:].rearrange("p (f t) -> p f t", f=NF)
                    eng.dma_start(out=dst[:, 0:8], in_=src[:, 0:8]).then_inc(sx.h, 16)
                    eng.dma_start(out=dst[:, 8:16], in_=src[:, 8:16]).then_inc(sx.h, 16)
                    for ap_in, sb in ((bq_in, bq_sb), (bk_in, bk_sb), (bo_in, bo_sb),
                                      (lng_in, lng_sb), (lnb_in, lnb_sb), (ident_in, ident)):
                        eng.dma_start(out=sb[:], in_=ap_in).then_inc(sx.h, 16)

                @blk.gpsimd
                def _(eng):
                    src = x_in.rearrange("(f p) t -> p f t", p=128)
                    dst = xb[:].rearrange("p (f t) -> p f t", f=NF)
                    eng.dma_start(out=dst[:, 0:8], in_=src[:, 0:8]).then_inc(scast.h, 16)
                    eng.dma_start(out=dst[:, 8:16], in_=src[:, 8:16]).then_inc(scast.h, 16)
                    eng.memset(ones128[:], 1.0)
                    eng.memset(onescol[:], 1.0)
                    eng.memset(onesrow[:], 1.0)
                    for i, v in CVALS.items():
                        eng.memset(cpool[:, i:i + 1], v)
            sx.n += 8 * 16
            scast.n += 32
            p0_sx = sx.n
            p0_scast = scast.n

            emit_layernorm("p1", True, xb,
                           True, lambda f: xb[:, f * TOK:(f + 1) * TOK],
                           {"pe": [(scast, p0_scast)],
                            "act": [(scast, p0_scast)],
                            "dve": [(sx, p0_sx)]})

            with contextlib.ExitStack() as ps:
                bv_bc = ps.enter_context(nc.sbuf_tensor("bv_bc", [128, DM], F32))
                bv_row = ps.enter_context(nc.sbuf_tensor("bv_row", [1, DM], F32))
                wbuf = [ps.enter_context(nc.sbuf_tensor(f"w2_{i}", [128, NF * 512], BF16))
                        for i in range(2)]
                kstage = [ps.enter_context(nc.sbuf_tensor(f"kst_{i}", [128, TOK], BF16))
                          for i in range(2)]
                vstage = [ps.enter_context(nc.sbuf_tensor(f"vst_{i}", [128, 512], BF16))
                          for i in range(2)]
                pp = [ps.enter_context(nc.psum_tensor(f"p2_{i}", [128, 512], F32))
                      for i in range(2)]
                bv_ps = ps.enter_context(nc.psum_tensor("p2_bv", [128, DM], F32))
                dum2_ps = ps.enter_context(nc.psum_tensor("p2_dum", [32, 1], F32))

                def p2dummy(eng):
                    eng.matmul(dum2_ps[:], ones128[:, 0:32], onescol[:],
                               start=True, stop=True).then_inc(spe.h, 1)

                plan = Plan()
                cc_in_ap = cc_in.ap()
                cc_in_V = cc_in_ap[2048:4096, :].rearrange("(t two) n -> t (two n)", two=2)

                def bvload_fn(eng):
                    eng.dma_start(out=bv_row[:], in_=bv_in).then_inc(sx.h, 16)
                bv_sx = sx.inc(16)
                plan.add("sync", bvload_fn)

                def bvbc_fn(eng):
                    eng.wait_ge(sx.h, bv_sx)
                    for j in range(3):
                        eng.matmul(bv_ps[:, j * 512:(j + 1) * 512], onesrow[:],
                                   bv_row[:, j * 512:(j + 1) * 512], start=True, stop=True)
                    eng.matmul(bv_ps[:, 1536:2048], onesrow[:],
                               bv_row[:, 1536:2048], start=True, stop=True).then_inc(spe.h, 1)
                    p2dummy(eng)
                spe.inc()
                bv_pe = spe.inc()
                plan.add("pe", bvbc_fn)

                def bvcp_fn(eng):
                    eng.wait_ge(spe.h, bv_pe)
                    eng.tensor_copy(out=bv_bc[:], in_=bv_ps[:]).then_inc(sdve.h, 1)
                bv_dve = sdve.inc()
                plan.add("dve", bvcp_fn)

                kmm_pe, kevac_act, kdma = {}, {}, {}
                vmm_pe, vevac_dve, vdma = {}, {}, {}
                qmm_pe, qevac_act, qdma = {}, {}, {}
                wk_fetch, wv_fetch, wq_fetch = {}, {}, {}

                for g in range(16):
                    def wkf(eng, g=g):
                        if g >= 2:
                            eng.wait_ge(spe.h, kmm_pe[(g - 2, 1)])
                        eng.dma_start(out=wbuf[g % 2][:, 0:2048],
                                      in_=wk_in[:, g * 2048:(g + 1) * 2048]).then_inc(s_w[g % 2].h, 16)
                    plan.add("gpsimd", wkf)
                    wk_fetch[g] = s_w[g % 2].inc(16)

                    for half in range(2):
                        def kmm(eng, g=g, half=half):
                            eng.wait_ge(s_w[g % 2].h, wk_fetch[g])
                            idx = g * 2 + half
                            if idx >= 2:
                                pg, ph = divmod(idx - 2, 2)
                                eng.wait_ge(sact.h, kevac_act[(pg, ph)])
                            ins = None
                            for f in range(NF):
                                ins = eng.matmul(
                                    pp[half][:],
                                    wbuf[g % 2][:, f * 128:(f + 1) * 128],
                                    xb[:, f * TOK + half * 512: f * TOK + half * 512 + 512],
                                    start=(f == 0), stop=(f == NF - 1))
                            ins.then_inc(spe.h, 1)
                            p2dummy(eng)
                        spe.inc()
                        kmm_pe[(g, half)] = spe.inc()
                        plan.add("pe", kmm)

                        def kevac(eng, g=g, half=half):
                            eng.wait_ge(spe.h, kmm_pe[(g, half)])
                            if g >= 2 and half == 0:
                                eng.wait_ge(s_st[g % 2].h, kdma[g - 2])
                            eng.activation(out=kstage[g % 2][:, half * 512:half * 512 + 512],
                                           in_=pp[half][:], func=AF.Identity,
                                           bias=bk_sb[:, g:g + 1]).then_inc(sact.h, 1)
                        kevac_act[(g, half)] = sact.inc()
                        plan.add("act", kevac)

                    def kdma_fn(eng, g=g):
                        eng.wait_ge(sact.h, kevac_act[(g, 1)])
                        eng.dma_start(out=cc_in_ap[g * 128:(g + 1) * 128, :],
                                      in_=kstage[g % 2][:]).then_inc(s_st[g % 2].h, 16)
                    plan.add("sync", kdma_fn)
                    kdma[g] = s_st[g % 2].inc(16)

                wv_r = wv_in.rearrange("p (f n) -> p f n", f=NF)
                for j in range(4):
                    def wvf(eng, j=j):
                        if j >= 2:
                            eng.wait_ge(spe.h, vmm_pe[(j - 2, NT - 1)])
                        src = wv_r[:, :, j * 512:(j + 1) * 512]
                        dst = wbuf[j % 2][:].rearrange("p (f n) -> p f n", f=NF)
                        eng.dma_start(out=dst[:, 0:8], in_=src[:, 0:8]).then_inc(s_w[j % 2].h, 16)
                        eng.dma_start(out=dst[:, 8:16], in_=src[:, 8:16]).then_inc(s_w[j % 2].h, 16)
                    plan.add("gpsimd", wvf)
                    wv_fetch[j] = s_w[j % 2].inc(32)

                    for t in range(NT):
                        def vmm(eng, j=j, t=t):
                            eng.wait_ge(s_w[j % 2].h, wv_fetch[j])
                            idx = j * NT + t
                            if idx < 2:
                                eng.wait_ge(sact.h, kevac_act[(15, 1)])
                            else:
                                pj, pt = divmod(idx - 2, NT)
                                eng.wait_ge(sdve.h, vevac_dve[(pj, pt)])
                            ins = None
                            for f in range(NF):
                                ins = eng.matmul(
                                    pp[t % 2][:],
                                    xb[:, f * TOK + t * 128: f * TOK + t * 128 + 128],
                                    wbuf[j % 2][:, f * 512:(f + 1) * 512],
                                    start=(f == 0), stop=(f == NF - 1))
                            ins.then_inc(spe.h, 1)
                            p2dummy(eng)
                        spe.inc()
                        vmm_pe[(j, t)] = spe.inc()
                        plan.add("pe", vmm)

                        def vevac(eng, j=j, t=t):
                            eng.wait_ge(spe.h, vmm_pe[(j, t)])
                            idx = j * NT + t
                            if idx == 0:
                                eng.wait_ge(sdve.h, bv_dve)
                            if idx >= 2:
                                pj, pt = divmod(idx - 2, NT)
                                eng.wait_ge(s_vst[t % 2].h, vdma[(pj, pt)])
                            eng.tensor_add(out=vstage[t % 2][:], in0=pp[t % 2][:],
                                           in1=bv_bc[:, j * 512:(j + 1) * 512]).then_inc(sdve.h, 1)
                        vevac_dve[(j, t)] = sdve.inc()
                        plan.add("dve", vevac)

                        def vdma_fn(eng, j=j, t=t):
                            eng.wait_ge(sdve.h, vevac_dve[(j, t)])
                            eng.dma_start(
                                out=cc_in_V[t * 128:(t + 1) * 128, j * 512:(j + 1) * 512],
                                in_=vstage[t % 2][:]).then_inc(s_vst[t % 2].h, 16)
                        plan.add("sync", vdma_fn)
                        vdma[(j, t)] = s_vst[t % 2].inc(16)

                staged_vals = [(s_st[0], s_st[0].n), (s_st[1], s_st[1].n),
                               (s_vst[0], s_vst[0].n), (s_vst[1], s_vst[1].n)]

                def coll_fn(eng):
                    for sem, val in staged_vals:
                        eng.wait_ge(sem.h, val)
                    eng.collective_compute(
                        "AllGather", ALU.bypass,
                        replica_groups=[list(range(NCORES))],
                        ins=[cc_in.ap().opt()],
                        outs=[cc_out.ap().opt()],
                    ).then_inc(scc.h, 1)
                plan.add("gpsimd", coll_fn)
                scc.inc()

                for g in range(16):
                    def wqf(eng, g=g):
                        if g >= 2:
                            eng.wait_ge(spe.h, qmm_pe[(g - 2, 1)])
                        else:
                            eng.wait_ge(spe.h, vmm_pe[(3, NT - 1)])
                        eng.dma_start(out=wbuf[g % 2][:, 0:2048],
                                      in_=wq_in[:, g * 2048:(g + 1) * 2048]).then_inc(s_w[g % 2].h, 16)
                    plan.add("gpsimd", wqf)
                    wq_fetch[g] = s_w[g % 2].inc(16)

                    for half in range(2):
                        def qmm(eng, g=g, half=half):
                            eng.wait_ge(s_w[g % 2].h, wq_fetch[g])
                            idx = g * 2 + half
                            if idx < 2:
                                eng.wait_ge(sdve.h, vevac_dve[(3, NT - 1)])
                            else:
                                pg, ph = divmod(idx - 2, 2)
                                eng.wait_ge(sact.h, qevac_act[(pg, ph)])
                            ins = None
                            for f in range(NF):
                                ins = eng.matmul(
                                    pp[half][:],
                                    wbuf[g % 2][:, f * 128:(f + 1) * 128],
                                    xb[:, f * TOK + half * 512: f * TOK + half * 512 + 512],
                                    start=(f == 0), stop=(f == NF - 1))
                            ins.then_inc(spe.h, 1)
                            p2dummy(eng)
                        spe.inc()
                        qmm_pe[(g, half)] = spe.inc()
                        plan.add("pe", qmm)

                        def qevac(eng, g=g, half=half):
                            eng.wait_ge(spe.h, qmm_pe[(g, half)])
                            if half == 0:
                                if g < 2:
                                    eng.wait_ge(s_st[g % 2].h, kdma[14 + g])
                                else:
                                    eng.wait_ge(s_qst[g % 2].h, qdma[g - 2])
                            eng.activation(
                                out=kstage[g % 2][:, half * 512:half * 512 + 512],
                                in_=pp[half][:], func=AF.Identity,
                                bias=bq_sb[:, g:g + 1]).then_inc(sact.h, 1)
                        qevac_act[(g, half)] = sact.inc()
                        plan.add("act", qevac)

                    def qdma_fn(eng, g=g):
                        eng.wait_ge(sact.h, qevac_act[(g, 1)])
                        eng.dma_start(out=q_dram.ap()[g],
                                      in_=kstage[g % 2][:]).then_inc(s_qst[g % 2].h, 16)
                    plan.add("sync", qdma_fn)
                    qdma[g] = s_qst[g % 2].inc(16)

                with nc.Block("p2") as blk:
                    @blk.tensor
                    def _(eng):
                        plan.run("pe", eng)

                    @blk.scalar
                    def _(eng):
                        plan.run("act", eng)

                    @blk.vector
                    def _(eng):
                        plan.run("dve", eng)

                    @blk.sync
                    def _(eng):
                        plan.run("sync", eng)

                    @blk.gpsimd
                    def _(eng):
                        plan.run("gpsimd", eng)
            p2_qst = (s_qst[0].n, s_qst[1].n)

            if debug:
                with nc.Block("dbgxn") as blk:
                    @blk.sync
                    def _(eng):
                        eng.dma_start(out=dbg_xn, in_=xb[:]).then_inc(sout.h, 16)
                        sout.n += 16
                        eng.wait_ge(sout.h, sout.n)

        with contextlib.ExitStack() as ps:
            kb_sb = [ps.enter_context(nc.sbuf_tensor(f"kb_{i}", [128, 8192], BF16))
                     for i in range(2)]
            vb_sb = [ps.enter_context(nc.sbuf_tensor(f"vb_{i}", [128, 8192], BF16))
                     for i in range(2)]
            qcur = [ps.enter_context(nc.sbuf_tensor(f"qc_{i}", [128, 512], BF16))
                    for i in range(2)]
            abuf = [ps.enter_context(nc.sbuf_tensor(f"ab_{i}", [128, 1024], BF16))
                    for i in range(2)]
            av_sb = ps.enter_context(nc.sbuf_tensor("av_sb", [128, NH * 512], BF16))
            mask_sb = ps.enter_context(nc.sbuf_tensor("mask_sb", [128, 8 * 896], BF16))
            rdbc_sb = ps.enter_context(nc.sbuf_tensor("rdbc_sb", [128, 512], F32))
            wob = [ps.enter_context(nc.sbuf_tensor(f"wob_{i}", [128, 2048], BF16))
                   for i in range(2)]

            s_ps = [ps.enter_context(nc.psum_tensor(f"s_ps{i}", [128, 1024], F32))
                    for i in range(2)]
            av_ps = [ps.enter_context(nc.psum_tensor(f"av_ps{i}", [128, 512], F32))
                     for i in range(2)]
            den_ps = [ps.enter_context(nc.psum_tensor(f"den_ps{i}", [128, 512], F32))
                      for i in range(1)]
            dum3_ps = ps.enter_context(nc.psum_tensor("dum3_ps", [32, 1], F32))

            plan = Plan()

            def mload(eng):
                src = masks_in.transpose([1, 0, 2])
                dst = mask_sb[:].rearrange("p (r n) -> p r n", r=8)
                eng.dma_start(out=dst[:, 0:4], in_=src[:, 0:4]).then_inc(sx.h, 16)
                eng.dma_start(out=dst[:, 4:8], in_=src[:, 4:8]).then_inc(sx.h, 16)
            plan.add("sync", mload)
            mask_sx = sx.inc(32)

            ccr = cc_out.ap().rearrange("(r x) n -> r x n", x=4096)
            vsrc_all = ccr[:, 2048:4096, :].rearrange("r (t two) n -> r t (two n)", two=2)

            exp_act, mask_dve = {}, {}
            headdone_pe, headfinal_pe, rden_act = {}, {}, {}
            avevac_dve = {}
            mask_info = {}

            def emit_mask(mgt):
                if mgt not in mask_info:
                    return
                mr, mkbr, _ = mask_info[mgt]

                def maskf(eng, mgt=mgt, mr=mr, mkbr=mkbr):
                    if mgt == 0:
                        eng.wait_ge(sx.h, mask_sx)
                    eng.wait_ge(sact.h, exp_act[mgt // 2])
                    moff = mr * 896 + (384 - 128 * mkbr)
                    adst = abuf[(mgt // 2) % 2][:, (mgt % 2) * 512:(mgt % 2) * 512 + 512]
                    eng.tensor_mul(out=adst, in0=adst,
                                   in1=mask_sb[:, moff:moff + 512]).then_inc(sdve.h, 1)
                mask_dve[mgt] = sdve.inc()
                plan.add("dve", maskf)
            kvf_done, smm_pe = {}, {}
            wo_pe, woevac_dve, wof_cast, wo_cover = {}, {}, {}, {}

            def emit_kv_fetch(qb, h, hh):
                Tq = 512 * (qb + 1)
                ktn = 4 * (qb + 1)
                slot = hh % 2

                def qf(eng, qb=qb, h=h, slot=slot, hh=hh):
                    if hh >= 2:
                        eng.wait_ge(spe.h, headfinal_pe[hh - 2])
                    eng.dma_start(out=qcur[slot][:],
                                  in_=q_dram.ap()[h, :, qb * 512:(qb + 1) * 512]).then_inc(s_kv[slot].h, 16)
                plan.add("sync", qf)
                s_kv[slot].inc(16)

                def kf(eng, h=h, slot=slot, Tq=Tq):
                    ksrc = ccr[:, h * 128:(h + 1) * 128, 0:Tq].transpose([1, 0, 2])
                    kdst = kb_sb[slot][:, 0:8 * Tq].rearrange("p (r t) -> p r t", r=8)
                    eng.dma_start(out=kdst[:, 0:4], in_=ksrc[:, 0:4]).then_inc(s_kv[slot].h, 16)
                    eng.dma_start(out=kdst[:, 4:8], in_=ksrc[:, 4:8]).then_inc(s_kv[slot].h, 16)
                plan.add("sync", kf)
                s_kv[slot].inc(32)

                def vf(eng, h=h, slot=slot, Tq=Tq, ktn=ktn):
                    vsrc = vsrc_all[:, 0:Tq, h * 128:(h + 1) * 128]
                    vsrc2 = vsrc.rearrange("r (kt p) d -> p r kt d", p=128)
                    vdst = vb_sb[slot][:, 0:8 * Tq].rearrange("p (r kt d) -> p r kt d",
                                                              r=8, kt=ktn)
                    for r in range(8):
                        if ktn <= 4:
                            eng.dma_start(out=vdst[:, r:r + 1],
                                          in_=vsrc2[:, r:r + 1]).then_inc(s_kv[slot].h, 16)
                        else:
                            eng.dma_start(out=vdst[:, r:r + 1, 0:4],
                                          in_=vsrc2[:, r:r + 1, 0:4]).then_inc(s_kv[slot].h, 16)
                            eng.dma_start(out=vdst[:, r:r + 1, 4:ktn],
                                          in_=vsrc2[:, r:r + 1, 4:ktn]).then_inc(s_kv[slot].h, 16)
                plan.add("sync", vf)
                s_kv[slot].inc(16 * (8 if ktn <= 4 else 16))
                return s_kv[slot].n

            gt = 0
            avmm_den = {}

            for qb in range(2):
                ktn = 4 * (qb + 1)
                ntile = 8 * ktn
                for h in range(NH):
                    hh = qb * NH + h
                    kvf_done[hh] = emit_kv_fetch(qb, h, hh)

                    def mk_smm(hh, r, kt, ti, gt, sl, ktn):
                        pt = gt // 2
                        half = gt % 2

                        def smm(eng):
                            if ti == 0:
                                eng.wait_ge(s_kv[hh % 2].h, kvf_done[hh])
                                if hh >= 2:
                                    eng.wait_ge(sdve.h, avevac_dve[hh - 2])
                                if hh >= 1:
                                    eng.wait_ge(sact.h, rden_act[hh - 1])
                            if half == 0 and pt >= 2:
                                eng.wait_ge(sact.h, exp_act[pt - 1])
                            off = (r * ktn + kt) * 128
                            eng.matmul(s_ps[(pt % 2)][:, half * 512:half * 512 + 512],
                                       kb_sb[hh % 2][:, off:off + 128],
                                       qcur[hh % 2][:],
                                       start=True, stop=True).then_inc(spe.h, 1)
                            if half == 1:
                                eng.matmul(dum3_ps[:], ones128[:, 0:32], onescol[:],
                                           start=True, stop=True).then_inc(spe.h, 1)
                        return smm

                    def mk_avmm(hh, ti, gt, asl, ntile, band, r, kt, ktn):
                        def avmm(eng):
                            if band:
                                eng.wait_ge(sdve.h, mask_dve[gt])
                            else:
                                eng.wait_ge(sact.h, exp_act[gt // 2])
                            off = (r * ktn + kt) * 128
                            asrc = abuf[(gt // 2) % 2][:, (gt % 2) * 512:(gt % 2) * 512 + 512]
                            eng.matmul(av_ps[hh % 2][:], vb_sb[hh % 2][:, off:off + 128],
                                       asrc, start=(ti == 0),
                                       stop=(ti == ntile - 1)).then_inc(spe.h, 1)
                            eng.matmul(den_ps[0][:], ones128[:], asrc,
                                       start=(ti == 0),
                                       stop=(ti == ntile - 1)).then_inc(spe.h, 1)
                        return avmm

                    def add_avmm(args):
                        hh_, ti_, gt_, asl_, ntile_, band_, r_, kt_, ktn_ = args
                        spe.inc()
                        avmm_den[gt_] = spe.inc()
                        plan.add("pe", mk_avmm(hh_, ti_, gt_, asl_, ntile_, band_,
                                               r_, kt_, ktn_))

                    def mk_exp(pt, cover):
                        def expf(eng):
                            eng.wait_ge(spe.h, cover)
                            eng.activation(out=abuf[pt % 2][:], in_=s_ps[pt % 2][:],
                                           func=AF.Exp, bias=cp(C_EBIAS),
                                           scale=cp(C_SCALE)).then_inc(sact.h, 1)
                        return expf

                    pend_av = []
                    for r in range(8):
                        for kt in range(ktn):
                            ti = r * ktn + kt
                            band = (kt >= 4 * qb)
                            kbr = kt - 4 * qb
                            if band:
                                mask_info[gt] = (r, kbr, 0)

                            if gt % 2 == 0:
                                smm_pe[gt] = spe.inc()
                            else:
                                spe.inc()
                                smm_pe[gt] = spe.inc()
                            plan.add("pe", mk_smm(hh, r, kt, ti, gt, 0, ktn))

                            if gt % 2 == 1:
                                pt = gt // 2
                                exp_act[pt] = sact.inc()
                                plan.add("act", mk_exp(pt, smm_pe[gt]))
                                emit_mask(gt - 1)
                                emit_mask(gt)

                            pend_av.append((hh, ti, gt, 0, ntile, band, r, kt, ktn))
                            if len(pend_av) > 2:
                                add_avmm(pend_av.pop(0))
                            gt += 1

                    for a in pend_av:
                        add_avmm(a)

                    def headdummy(eng):
                        eng.matmul(dum3_ps[:], ones128[:, 0:32], onescol[:],
                                   start=True, stop=True).then_inc(spe.h, 1)
                    headfinal_pe[hh] = spe.inc()
                    plan.add("pe", headdummy)

                    rden_act[hh] = sact.inc(2)

                    def rdenf(eng, hh=hh):
                        eng.wait_ge(spe.h, headfinal_pe[hh])
                        if hh >= 1:
                            eng.wait_ge(sdve.h, avevac_dve[hh - 1])
                        eng.activation(out=rdbc_sb[:], in_=den_ps[0][:],
                                       func=AF.Ln).then_inc(sact.h, 1)
                        eng.activation(out=rdbc_sb[:], in_=rdbc_sb[:], func=AF.Exp,
                                       scale=cp(C_NONE)).then_inc(sact.h, 1)
                    plan.add("act", rdenf)

                    def avevac(eng, hh=hh, h=h, qb=qb):
                        eng.wait_ge(sact.h, rden_act[hh])
                        if qb == 1 and h == 0:
                            eng.wait_ge(spe.h, wo_pe[(0, 15)])
                        eng.tensor_mul(out=av_sb[:, h * 512:(h + 1) * 512],
                                       in0=av_ps[hh % 2][:],
                                       in1=rdbc_sb[:]).then_inc(sdve.h, 1)
                    avevac_dve[hh] = sdve.inc()
                    plan.add("dve", avevac)

                for g in range(16):
                    def wof(eng, qb=qb, g=g):
                        idx = qb * 16 + g
                        if idx >= 2:
                            pqb, pg = divmod(idx - 2, 16)
                            eng.wait_ge(spe.h, wo_pe[(pqb, pg)])
                        eng.dma_start(out=wob[g % 2][:],
                                      in_=wo_in[:, g * 2048:(g + 1) * 2048]).then_inc(s_w[g % 2].h, 16)
                    plan.add("gpsimd", wof)
                    wof_cast[(qb, g)] = s_w[g % 2].inc(16)

                    def womm(eng, qb=qb, g=g):
                        eng.wait_ge(s_w[g % 2].h, wof_cast[(qb, g)])
                        if g == 0:
                            eng.wait_ge(sdve.h, avevac_dve[qb * NH + NH - 1])
                        if g >= 2:
                            eng.wait_ge(sdve.h, woevac_dve[(qb, g - 2)])
                        ins = None
                        for f in range(NF):
                            ins = eng.matmul(s_ps[g % 2][:, 0:512],
                                             wob[g % 2][:, f * 128:(f + 1) * 128],
                                             av_sb[:, f * 512:(f + 1) * 512],
                                             start=(f == 0), stop=(f == NF - 1))
                        ins.then_inc(spe.h, 1)
                        eng.matmul(dum3_ps[:], ones128[:, 0:32], onescol[:],
                                   start=True, stop=True).then_inc(spe.h, 1)
                    wo_pe[(qb, g)] = spe.inc()
                    wo_cover[(qb, g)] = spe.inc()
                    plan.add("pe", womm)

                    def woevac(eng, qb=qb, g=g):
                        eng.wait_ge(spe.h, wo_cover[(qb, g)])
                        sl_ = stream[:, g * TOK + qb * 512: g * TOK + qb * 512 + 512]
                        eng.scalar_tensor_tensor(out=sl_, in0=s_ps[g % 2][:, 0:512],
                                                 scalar=bo_sb[:, g:g + 1], in1=sl_,
                                                 op0=ALU.add, op1=ALU.add).then_inc(sdve.h, 1)
                    woevac_dve[(qb, g)] = sdve.inc()
                    plan.add("dve", woevac)

            with nc.Block("p3") as blk:
                @blk.tensor
                def _(eng):
                    eng.wait_ge(scc.h, 1)
                    plan.run("pe", eng)

                @blk.scalar
                def _(eng):
                    plan.run("act", eng)

                @blk.vector
                def _(eng):
                    plan.run("dve", eng)

                @blk.sync
                def _(eng):
                    eng.wait_ge(scc.h, 1)
                    eng.wait_ge(s_qst[0].h, p2_qst[0])
                    eng.wait_ge(s_qst[1].h, p2_qst[1])
                    plan.run("sync", eng)

                @blk.gpsimd
                def _(eng):
                    plan.run("gpsimd", eng)

        if debug:
            with nc.Block("dbgx2") as blk:
                @blk.sync
                def _(eng):
                    eng.dma_start(out=dbg_x2, in_=stream[:]).then_inc(sout.h, 16)
                    sout.n += 16
                    eng.wait_ge(sout.h, sout.n)

        with nc.sbuf_tensor("xn2", [128, NF * TOK], BF16) as xn2:
            emit_layernorm("p4", False, None, False,
                           lambda f: xn2[:, f * TOK:(f + 1) * TOK], {})

            with contextlib.ExitStack() as ps:
                wgb = [ps.enter_context(nc.sbuf_tensor(f"wgb_{i}", [128, NF * 512], BF16))
                       for i in range(2)]
                wub = [ps.enter_context(nc.sbuf_tensor(f"wub_{i}", [128, NF * 512], BF16))
                       for i in range(2)]
                sgb = [ps.enter_context(nc.sbuf_tensor(f"sgb_{i}", [128, 512], F32))
                       for i in range(2)]
                hst = [ps.enter_context(nc.sbuf_tensor(f"hst_{i}", [128, 512], F32))
                       for i in range(4)]
                g_ps = [ps.enter_context(nc.psum_tensor(f"g_ps{i}", [128, 512], F32))
                        for i in range(2)]
                u_ps = [ps.enter_context(nc.psum_tensor(f"u_ps{i}", [128, 512], F32))
                        for i in range(2)]
                dum5_ps = ps.enter_context(nc.psum_tensor("p5_dum", [32, 1], F32))

                def p5dummy(eng):
                    eng.matmul(dum5_ps[:], ones128[:, 0:32], onescol[:],
                               start=True, stop=True).then_inc(spe.h, 1)

                plan = Plan()
                wgu_fetch, gmm_pe, umm_pe = {}, {}, {}
                silu_act, ht_dve, hw_sync = {}, {}, {}

                wg_r = wg_in.rearrange("p (f n) -> p f n", f=NF)
                wu_r = wu_in.rearrange("p (f n) -> p f n", f=NF)

                for js in range(16):
                    def wguf(eng, js=js):
                        if js >= 2:
                            eng.wait_ge(spe.h, umm_pe[(js - 2, NT - 1)])
                        srcg = wg_r[:, :, js * 512:(js + 1) * 512]
                        srcu = wu_r[:, :, js * 512:(js + 1) * 512]
                        dstg = wgb[js % 2][:].rearrange("p (f n) -> p f n", f=NF)
                        dstu = wub[js % 2][:].rearrange("p (f n) -> p f n", f=NF)
                        eng.dma_start(out=dstg[:, 0:8], in_=srcg[:, 0:8]).then_inc(s_w[js % 2].h, 16)
                        eng.dma_start(out=dstg[:, 8:16], in_=srcg[:, 8:16]).then_inc(s_w[js % 2].h, 16)
                        eng.dma_start(out=dstu[:, 0:8], in_=srcu[:, 0:8]).then_inc(s_w[js % 2].h, 16)
                        eng.dma_start(out=dstu[:, 8:16], in_=srcu[:, 8:16]).then_inc(s_w[js % 2].h, 16)
                    plan.add("gpsimd", wguf)
                    wgu_fetch[js] = s_w[js % 2].inc(64)

                    for t in range(NT):
                        idx = js * NT + t

                        def gmm(eng, js=js, t=t, idx=idx):
                            eng.wait_ge(s_w[js % 2].h, wgu_fetch[js])
                            if idx >= 2:
                                pj, pt = divmod(idx - 2, NT)
                                eng.wait_ge(sact.h, silu_act[(pj, pt)])
                            ins = None
                            for f in range(NF):
                                ins = eng.matmul(
                                    g_ps[t % 2][:],
                                    xn2[:, f * TOK + t * 128: f * TOK + t * 128 + 128],
                                    wgb[js % 2][:, f * 512:(f + 1) * 512],
                                    start=(f == 0), stop=(f == NF - 1))
                            ins.then_inc(spe.h, 1)
                            p5dummy(eng)
                        spe.inc()
                        gmm_pe[(js, t)] = spe.inc()
                        plan.add("pe", gmm)

                        def umm(eng, js=js, t=t, idx=idx):
                            if idx >= 2:
                                pj, pt = divmod(idx - 2, NT)
                                eng.wait_ge(sdve.h, ht_dve[(pj, pt)])
                            ins = None
                            for f in range(NF):
                                ins = eng.matmul(
                                    u_ps[t % 2][:],
                                    xn2[:, f * TOK + t * 128: f * TOK + t * 128 + 128],
                                    wub[js % 2][:, f * 512:(f + 1) * 512],
                                    start=(f == 0), stop=(f == NF - 1))
                            ins.then_inc(spe.h, 1)
                            p5dummy(eng)
                        spe.inc()
                        umm_pe[(js, t)] = spe.inc()
                        plan.add("pe", umm)

                        def siluf(eng, js=js, t=t, idx=idx):
                            eng.wait_ge(spe.h, gmm_pe[(js, t)])
                            if idx >= 2:
                                pj, pt = divmod(idx - 2, NT)
                                eng.wait_ge(sdve.h, ht_dve[(pj, pt)])
                            eng.activation(out=sgb[t % 2][:], in_=g_ps[t % 2][:],
                                           func=AF.Silu).then_inc(sact.h, 1)
                        silu_act[(js, t)] = sact.inc()
                        plan.add("act", siluf)

                        def htf(eng, js=js, t=t, idx=idx):
                            eng.wait_ge(spe.h, umm_pe[(js, t)])
                            eng.wait_ge(sact.h, silu_act[(js, t)])
                            if idx >= 4:
                                pj, pt = divmod(idx - 4, NT)
                                eng.wait_ge(s_hst[idx % 4].h, hw_sync[(pj, pt)])
                            eng.tensor_mul(out=hst[idx % 4][:], in0=sgb[t % 2][:],
                                           in1=u_ps[t % 2][:]).then_inc(sdve.h, 1)
                        ht_dve[(js, t)] = sdve.inc()
                        plan.add("dve", htf)

                        def hwf(eng, js=js, t=t, idx=idx):
                            eng.wait_ge(sdve.h, ht_dve[(js, t)])
                            eng.dma_start(out=h_dram.ap()[t, :, js * 512:(js + 1) * 512],
                                          in_=hst[idx % 4][:]).then_inc(s_hst[idx % 4].h, 16)
                        plan.add("sync", hwf)
                        hw_sync[(js, t)] = s_hst[idx % 4].inc(16)

                with nc.Block("p5a") as blk:
                    @blk.tensor
                    def _(eng):
                        plan.run("pe", eng)

                    @blk.scalar
                    def _(eng):
                        plan.run("act", eng)

                    @blk.vector
                    def _(eng):
                        plan.run("dve", eng)

                    @blk.sync
                    def _(eng):
                        plan.run("sync", eng)

                    @blk.gpsimd
                    def _(eng):
                        plan.run("gpsimd", eng)
                p5a_hst = [s.n for s in s_hst]

        with contextlib.ExitStack() as ps:
            htile2 = [ps.enter_context(nc.sbuf_tensor(f"htile{i}", [128, II], F32))
                      for i in range(2)]
            hbf2 = [ps.enter_context(nc.sbuf_tensor(f"hbf{i}", [128, II], BF16))
                    for i in range(2)]
            t1 = ps.enter_context(nc.sbuf_tensor("t1", [128, 2048], F32))
            t2 = ps.enter_context(nc.sbuf_tensor("t2", [128, 2048], F32))
            t3 = ps.enter_context(nc.sbuf_tensor("t3", [128, 2048], F32))
            tstage = ps.enter_context(nc.sbuf_tensor("tstage", [128, 2048], BF16))
            tp_ps = [ps.enter_context(nc.psum_tensor(f"tp_ps{i}", [128, 128], BF16))
                     for i in range(2)]
            dum5b_ps = ps.enter_context(nc.psum_tensor("p5b_dum", [32, 1], F32))

            plan = Plan()
            hr_sync, msk_dve, sq_act5, hsw_sync = {}, {}, {}, {}
            tp_pe, tev_n = {}, {}

            def q_(t, k):
                return htile2[t % 2][:, k * 2048:(k + 1) * 2048]

            def qb_(t, k):
                return hbf2[t % 2][:, k * 2048:(k + 1) * 2048]

            for t in range(NT):
                def hrf(eng, t=t):
                    if t == 0:
                        for i in range(4):
                            eng.wait_ge(s_hst[i].h, p5a_hst[i])
                    if t >= 2:
                        eng.wait_ge(spe.h, tp_pe[(t - 2, 63)])
                    eng.dma_start(out=htile2[t % 2][:], in_=h_dram.ap()[t]).then_inc(sh.h, 16)
                plan.add("sync", hrf)
                hr_sync[t] = sh.inc(16)

                def castf(eng, t=t):
                    eng.wait_ge(sh.h, hr_sync[t])
                    eng.tensor_copy(out=hbf2[t % 2][:], in_=htile2[t % 2][:]).then_inc(sdve.h, 1)
                cast_v = sdve.inc()
                plan.add("dve", castf)

                def sqf(eng, t=t, cast_v=cast_v):
                    eng.wait_ge(sdve.h, cast_v)
                    eng.activation(out=htile2[t % 2][:], in_=htile2[t % 2][:],
                                   func=AF.Square).then_inc(sact.h, 1)
                sq_act5[t] = sact.inc()
                plan.add("act", sqf)

                def mskf(eng, t=t):
                    eng.wait_ge(sact.h, sq_act5[t])
                    eng.tensor_tensor(out=t1[:], in0=q_(t, 0), in1=q_(t, 1), op=ALU.min)
                    eng.tensor_tensor(out=t2[:], in0=q_(t, 2), in1=q_(t, 3), op=ALU.min)
                    eng.tensor_max(out=t1[:], in0=t1[:], in1=t2[:])
                    eng.tensor_max(out=t2[:], in0=q_(t, 0), in1=q_(t, 1))
                    eng.tensor_max(out=t3[:], in0=q_(t, 2), in1=q_(t, 3))
                    eng.tensor_tensor(out=t2[:], in0=t2[:], in1=t3[:], op=ALU.min)
                    eng.tensor_max(out=t2[:], in0=t1[:], in1=t2[:])
                    for k in range(4):
                        eng.tensor_tensor(out=q_(t, k), in0=q_(t, k), in1=t2[:], op=ALU.is_ge)
                        ins = eng.tensor_mul(out=qb_(t, k), in0=qb_(t, k), in1=q_(t, k))
                    ins.then_inc(sdve.h, 1)
                msk_dve[t] = sdve.inc()
                plan.add("dve", mskf)

                for ic in range(64):
                    def tpf(eng, t=t, ic=ic):
                        if ic == 0:
                            eng.wait_ge(sdve.h, msk_dve[t])
                            if t >= 1:
                                eng.wait_ge(shs.h, hsw_sync[(t - 1, 7)])
                        if ic >= 2:
                            kind, val = tev_n[(t, ic - 2)]
                            eng.wait_ge(sdve.h if kind == "dve" else sact.h, val)
                        eng.transpose(tp_ps[ic % 2][:],
                                      hbf2[t % 2][:, ic * 128:(ic + 1) * 128],
                                      ident[:]).then_inc(spe.h, 1)
                    tp_pe[(t, ic)] = spe.inc()
                    plan.add("pe", tpf)

                    eng_name = "dve" if ic % 2 == 0 else "act"

                    def tevac(eng, t=t, ic=ic, eng_name=eng_name):
                        eng.wait_ge(spe.h, tp_pe[(t, ic)] + 1)
                        if ic >= 16:
                            eng.wait_ge(shs.h, hsw_sync[(t, ic // 8 - 2)])
                        dst = tstage[:, (ic % 16) * 128:(ic % 16) * 128 + 128]
                        if eng_name == "dve":
                            eng.tensor_copy(out=dst, in_=tp_ps[ic % 2][:]).then_inc(sdve.h, 1)
                        else:
                            eng.activation(out=dst, in_=tp_ps[ic % 2][:],
                                           func=AF.Identity).then_inc(sact.h, 1)
                    if eng_name == "dve":
                        tev_n[(t, ic)] = ("dve", sdve.inc())
                    else:
                        tev_n[(t, ic)] = ("act", sact.inc())
                    plan.add(eng_name, tevac)

                    if ic % 8 == 7:
                        def tstore(eng, t=t, ic8=ic // 8):
                            for icc in range(ic8 * 8, ic8 * 8 + 8):
                                kind, val = tev_n[(t, icc)]
                                eng.wait_ge(sdve.h if kind == "dve" else sact.h, val)
                            src_ = tstage[:, (ic8 % 2) * 1024:(ic8 % 2) * 1024 + 1024]
                            eng.dma_start(
                                out=hs_fm_dram.ap()[ic8 * 8:(ic8 + 1) * 8, :,
                                                    t * 128:(t + 1) * 128].transpose([1, 0, 2]),
                                in_=src_.rearrange("p (i n) -> p i n", i=8)).then_inc(shs.h, 16)
                        plan.add("sync", tstore)
                        hsw_sync[(t, ic // 8)] = shs.inc(16)

                def tpdum(eng, t=t):
                    eng.matmul(dum5b_ps[:], ones128[:, 0:32], onescol[:],
                               start=True, stop=True).then_inc(spe.h, 1)
                spe.inc()
                plan.add("pe", tpdum)
            p5b_shs = shs.n

            with nc.Block("p5b") as blk:
                @blk.tensor
                def _(eng):
                    plan.run("pe", eng)

                @blk.scalar
                def _(eng):
                    plan.run("act", eng)

                @blk.vector
                def _(eng):
                    plan.run("dve", eng)

                @blk.sync
                def _(eng):
                    plan.run("sync", eng)

        with contextlib.ExitStack() as ps:
            hfm = [ps.enter_context(nc.sbuf_tensor(f"hfm_{i}", [128, 512], BF16))
                   for i in range(4)]
            wdb = [ps.enter_context(nc.sbuf_tensor(f"wdb_{i}", [128, 2048], BF16))
                   for i in range(3)]
            y_ps = [ps.enter_context(nc.psum_tensor(f"y_ps{i}", [128, 512], F32))
                    for i in range(8)]

            plan = Plan()
            hs_ts, wd_fetch, dmm_pe, yev_dve = {}, {}, {}, {}
            wd_slot_user = {}

            for half in range(2):
                for gh in range(2):
                    pi = half * 2 + gh
                    for ic in range(64):
                        gi = pi * 64 + ic

                        def hstr(eng, half=half, ic=ic, gi=gi):
                            if gi == 0:
                                eng.wait_ge(shs.h, p5b_shs)
                            if gi >= 4:
                                ppi, pic = divmod(gi - 4, 64)
                                eng.wait_ge(spe.h, dmm_pe[(ppi, pic)])
                            eng.dma_start(
                                out=hfm[ic % 4][:],
                                in_=hs_fm_dram.ap()[ic, :, half * 512:(half + 1) * 512]
                            ).then_inc(s_tp[ic % 4].h, 16)
                        plan.add("sync", hstr)
                        hs_ts[(pi, ic)] = s_tp[ic % 4].inc(16)

                        prev_user = wd_slot_user.get(ic % 3)

                        def wdf(eng, ic=ic, gi=gi, prev_user=prev_user):
                            if prev_user is not None:
                                eng.wait_ge(spe.h, dmm_pe[prev_user])
                            eng.dma_start(out=wdb[ic % 3][:],
                                          in_=wd_in[:, ic * 2048:(ic + 1) * 2048]).then_inc(s_wd[ic % 3].h, 16)
                        plan.add("gpsimd", wdf)
                        wd_fetch[(pi, ic)] = s_wd[ic % 3].inc(16)
                        wd_slot_user[ic % 3] = (pi, ic)

                        def dmm(eng, gh=gh, ic=ic, pi=pi):
                            eng.wait_ge(s_tp[ic % 4].h, hs_ts[(pi, ic)])
                            eng.wait_ge(s_wd[ic % 3].h, wd_fetch[(pi, ic)])
                            if ic == 0 and pi >= 1:
                                eng.wait_ge(sdve.h, yev_dve[pi - 1])
                            ins = None
                            for gg in range(8):
                                g = gh * 8 + gg
                                ins = eng.matmul(y_ps[gg][:],
                                                 wdb[ic % 3][:, g * 128:(g + 1) * 128],
                                                 hfm[ic % 4][:],
                                                 start=(ic == 0), stop=(ic == 63))
                            ins.then_inc(spe.h, 1)
                        dmm_pe[(pi, ic)] = spe.inc()
                        plan.add("pe", dmm)

                    def yev(eng, half=half, gh=gh, pi=pi):
                        eng.wait_ge(spe.h, dmm_pe[(pi, 63)])
                        ins = None
                        for gg in range(8):
                            g = gh * 8 + gg
                            sl_ = stream[:, g * TOK + half * 512: g * TOK + half * 512 + 512]
                            ins = eng.tensor_add(out=sl_, in0=y_ps[gg][:], in1=sl_)
                        ins.then_inc(sdve.h, 1)
                    yev_dve[pi] = sdve.inc()
                    plan.add("dve", yev)

            with nc.Block("p5c") as blk:
                @blk.tensor
                def _(eng):
                    plan.run("pe", eng)

                @blk.vector
                def _(eng):
                    plan.run("dve", eng)

                @blk.sync
                def _(eng):
                    plan.run("sync", eng)

                @blk.gpsimd
                def _(eng):
                    plan.run("gpsimd", eng)

        if debug:
            with nc.Block("dbgy") as blk:
                @blk.sync
                def _(eng):
                    eng.dma_start(out=dbg_y, in_=stream[:]).then_inc(sout.h, 16)
                    sout.n += 16
                    eng.wait_ge(sout.h, sout.n)

        apply6 = emit_layernorm("p6", False, None, False,
                                lambda f: stream[:, f * TOK:(f + 1) * TOK], {})

        with nc.Block("p6s") as blk:
            @blk.sync
            def _(eng):
                for f in range(NF):
                    eng.wait_ge(sdve.h, apply6[f])
                    eng.dma_start(out=out_t[f * 128:(f + 1) * 128, :],
                                  in_=stream[:, f * TOK:(f + 1) * TOK]).then_inc(sout.h, 16)
                    sout.n += 16
                if debug:
                    eng.dma_start(out=dbg_ccin, in_=cc_in.ap()).then_inc(sout.h, 16)
                    eng.dma_start(out=dbg_q.rearrange("h p t -> (h p) t"),
                                  in_=q_dram.ap().rearrange("h p t -> (h p) t")).then_inc(sout.h, 16)
                    for t in range(NT):
                        eng.dma_start(out=dbg_h[t], in_=h_dram.ap()[t]).then_inc(sout.h, 16)
                    eng.dma_start(out=dbg_hs.rearrange("i p t -> (i p) t"),
                                  in_=hs_fm_dram.ap().rearrange("i p t -> (i p) t")).then_inc(sout.h, 16)
                    sout.n += 16 * (3 + NT)
                eng.wait_ge(sout.h, sout.n)

    return nc


_NC_CACHE = None


def _get_nc():
    global _NC_CACHE
    if _NC_CACHE is None:
        _NC_CACHE = build_program()
    return _NC_CACHE


def kernel(L0_attn_X, L0_attn_ln_g, L0_attn_ln_b, L0_attn_Wq, L0_attn_bq,
           L0_attn_Wk, L0_attn_bk, L0_attn_Wv, L0_attn_bv, L0_attn_Wo,
           L0_attn_bo, L0_ffn_w_gate, L0_ffn_w_up, L0_ffn_w_down):
    X = np.asarray(L0_attn_X)[0]
    Wq = np.asarray(L0_attn_Wq)
    Wk = np.asarray(L0_attn_Wk)
    Wv = np.asarray(L0_attn_Wv)
    Wo = np.asarray(L0_attn_Wo)
    Wg = np.asarray(L0_ffn_w_gate)
    Wu = np.asarray(L0_ffn_w_up)
    Wd = np.asarray(L0_ffn_w_down)

    def lhsT_layout(W):
        n_in = W.shape[0] // 128
        return np.ascontiguousarray(
            W.reshape(n_in, 128, 16, 128).transpose(1, 2, 0, 3).reshape(128, -1))

    def rhs_layout(W):
        return np.ascontiguousarray(
            W.reshape(16, 128, -1).transpose(1, 0, 2).reshape(128, -1))

    perm = 4 * (np.arange(II) % 2048) + np.arange(II) // 2048

    wq_dev = lhsT_layout(Wq)
    wk_dev = lhsT_layout(Wk)
    wo_dev = lhsT_layout(Wo)
    wv_dev = rhs_layout(Wv)
    wg_dev = rhs_layout(Wg[:, perm])
    wu_dev = rhs_layout(Wu[:, perm])
    wd_dev = np.ascontiguousarray(
        Wd[perm, :].reshape(64, 128, 16, 128).transpose(1, 0, 2, 3).reshape(128, -1))

    def colvec(b):
        return np.ascontiguousarray(np.asarray(b).reshape(16, 128).T)

    bq_dev = colvec(L0_attn_bq)
    bk_dev = colvec(L0_attn_bk)
    bo_dev = colvec(L0_attn_bo)
    lng_dev = colvec(L0_attn_ln_g)
    lnb_dev = colvec(L0_attn_ln_b)
    bv_dev = np.ascontiguousarray(np.asarray(L0_attn_bv)[None, :])

    p = np.arange(128)[:, None]
    v = np.arange(896)[None, :]
    in_maps = []
    for c in range(NCORES):
        Xc = np.ascontiguousarray(X[c::8, :].T)
        masks = np.zeros((8, 128, 896), dtype=ml_dtypes.bfloat16)
        for r in range(8):
            delta = 1 if r > c else 0
            masks[r] = ((v - 384 - p) >= delta).astype(ml_dtypes.bfloat16)
        in_maps.append({
            "x": Xc, "wq": wq_dev, "wk": wk_dev, "wv": wv_dev, "wo": wo_dev,
            "wg": wg_dev, "wu": wu_dev, "wd": wd_dev,
            "bq": bq_dev, "bk": bk_dev, "bo": bo_dev, "bvr": bv_dev,
            "lng": lng_dev, "lnb": lnb_dev, "masks": masks,
            "identw": np.eye(128, dtype=ml_dtypes.bfloat16),
        })

    nc = _get_nc()
    res = run_bass_kernel_spmd(nc, in_maps, core_ids=list(range(NCORES)))
    out = np.empty((S, DM), np.float32)
    for c in range(NCORES):
        out[c::8, :] = res.results[c]["out"].T
    return out[None]


# version 18
# speedup vs baseline: 1.1482x; 1.1482x over previous
import contextlib
import numpy as np
import ml_dtypes
import concourse.bass as bass
import concourse.mybir as mybir
from concourse.bass_utils import run_bass_kernel_spmd

F32 = mybir.dt.float32
BF16 = mybir.dt.bfloat16
AF = mybir.ActivationFunctionType
ALU = mybir.AluOpType

NCORES = 8
DM = 2048
NF = 16
NH = 16
S = 8192
TOK = 1024
NT = 8
II = 8192
EPS = 1e-5
SCALE = 1.0 / float(np.sqrt(128.0))
EXP_BIAS = -2.0


class Sem:
    def __init__(self, nc, es, name):
        self.h = es.enter_context(nc.semaphore(name))
        self.n = 0

    def inc(self, k=1):
        self.n += k
        return self.n


class Plan:
    def __init__(self):
        self.items = []

    def add(self, eng, fn):
        self.items.append((eng, fn))

    def run(self, name, engobj):
        for e, fn in self.items:
            if e == name:
                fn(engobj)


def build_program(debug=False):
    nc = bass.Bass("TRN2", target_bir_lowering=False)

    x_in = nc.dram_tensor("x", [DM, TOK], F32, kind="ExternalInput").ap()
    wq_in = nc.dram_tensor("wq", [128, NF * DM], F32, kind="ExternalInput").ap()
    wk_in = nc.dram_tensor("wk", [128, NF * DM], F32, kind="ExternalInput").ap()
    wv_in = nc.dram_tensor("wv", [128, NF * DM], F32, kind="ExternalInput").ap()
    wo_in = nc.dram_tensor("wo", [128, NF * DM], F32, kind="ExternalInput").ap()
    wg_in = nc.dram_tensor("wg", [128, NF * II], F32, kind="ExternalInput").ap()
    wu_in = nc.dram_tensor("wu", [128, NF * II], F32, kind="ExternalInput").ap()
    wd_in = nc.dram_tensor("wd", [128, 64 * DM], F32, kind="ExternalInput").ap()
    bq_in = nc.dram_tensor("bq", [128, 16], F32, kind="ExternalInput").ap()
    bk_in = nc.dram_tensor("bk", [128, 16], F32, kind="ExternalInput").ap()
    bo_in = nc.dram_tensor("bo", [128, 16], F32, kind="ExternalInput").ap()
    bv_in = nc.dram_tensor("bvr", [1, DM], F32, kind="ExternalInput").ap()
    lng_in = nc.dram_tensor("lng", [128, 16], F32, kind="ExternalInput").ap()
    lnb_in = nc.dram_tensor("lnb", [128, 16], F32, kind="ExternalInput").ap()
    masks_in = nc.dram_tensor("masks", [8, 128, 896], BF16, kind="ExternalInput").ap()
    ident_in = nc.dram_tensor("identw", [128, 128], BF16, kind="ExternalInput").ap()
    out_t = nc.dram_tensor("out", [DM, TOK], F32, kind="ExternalOutput").ap()

    cc_in = nc.dram_tensor("cc_in", [4096, 1024], BF16, kind="Internal")
    cc_out = nc.dram_tensor("cc_out", [NCORES * 4096, 1024], BF16, kind="Internal")
    q_dram = nc.dram_tensor("q_dram", [NH, 128, TOK], BF16, kind="Internal")
    h_dram = nc.dram_tensor("h_dram", [NT, 128, II], F32, kind="Internal")
    hs_fm_dram = nc.dram_tensor("hs_fm", [64, 128, TOK], BF16, kind="Internal")
    wd_bf = nc.dram_tensor("wd_bf", [128, 64 * DM], BF16, kind="Internal")
    if debug:
        dbg_xn = nc.dram_tensor("dbg_xn", [128, NF * TOK], BF16, kind="ExternalOutput").ap()
        dbg_x2 = nc.dram_tensor("dbg_x2", [128, NF * TOK], F32, kind="ExternalOutput").ap()
        dbg_ccin = nc.dram_tensor("dbg_ccin", [4096, 1024], BF16, kind="ExternalOutput").ap()
        dbg_q = nc.dram_tensor("dbg_q", [NH, 128, TOK], BF16, kind="ExternalOutput").ap()
        dbg_h = nc.dram_tensor("dbg_h", [NT, 128, II], F32, kind="ExternalOutput").ap()
        dbg_hs = nc.dram_tensor("dbg_hs", [64, 128, TOK], BF16, kind="ExternalOutput").ap()
        dbg_y = nc.dram_tensor("dbg_y", [128, NF * TOK], F32, kind="ExternalOutput").ap()

    es = contextlib.ExitStack()
    with es:
        stream = es.enter_context(nc.sbuf_tensor("stream", [128, NF * TOK], F32))
        cpool = es.enter_context(nc.sbuf_tensor("cpool", [128, 8], F32))
        bq_sb = es.enter_context(nc.sbuf_tensor("bq_sb", [128, 16], F32))
        bk_sb = es.enter_context(nc.sbuf_tensor("bk_sb", [128, 16], F32))
        bo_sb = es.enter_context(nc.sbuf_tensor("bo_sb", [128, 16], F32))
        lng_sb = es.enter_context(nc.sbuf_tensor("lng_sb", [128, 16], F32))
        lnb_sb = es.enter_context(nc.sbuf_tensor("lnb_sb", [128, 16], F32))
        ones128 = es.enter_context(nc.sbuf_tensor("ones128", [128, 128], BF16))
        ident = es.enter_context(nc.sbuf_tensor("ident", [128, 128], BF16))
        onescol = es.enter_context(nc.sbuf_tensor("onescol", [128, 1], BF16))
        onesrow = es.enter_context(nc.sbuf_tensor("onesrow", [1, 128], F32))

        sx = Sem(nc, es, "sx")
        s_w = [Sem(nc, es, f"s_w{i}") for i in range(2)]
        s_st = [Sem(nc, es, f"s_st{i}") for i in range(2)]
        s_vst = [Sem(nc, es, f"s_vst{i}") for i in range(2)]
        s_qst = [Sem(nc, es, f"s_qst{i}") for i in range(2)]
        s_kv = [Sem(nc, es, f"s_kv{i}") for i in range(2)]
        s_hst = [Sem(nc, es, f"s_hst{i}") for i in range(4)]
        s_tp = [Sem(nc, es, f"s_tp{i}") for i in range(4)]
        s_wd = [Sem(nc, es, f"s_wd{i}") for i in range(3)]
        s_wdc = Sem(nc, es, "s_wdc")
        scast = Sem(nc, es, "scast")
        skvst = Sem(nc, es, "skvst")
        scc = Sem(nc, es, "scc")
        skvf = Sem(nc, es, "skvf")
        sh = Sem(nc, es, "sh")
        shs = Sem(nc, es, "shs")
        sout = Sem(nc, es, "sout")
        spe = Sem(nc, es, "spe")
        sact = Sem(nc, es, "sact")
        sdve = Sem(nc, es, "sdve")

        C_SCALE, C_EBIAS, C_EPS, C_NHALF, C_NONE = 0, 1, 2, 3, 4
        CVALS = {C_SCALE: SCALE, C_EBIAS: EXP_BIAS, C_EPS: EPS,
                 C_NHALF: -0.5, C_NONE: -1.0}

        def cp(i):
            return cpool[:, i:i + 1]

        def emit_layernorm(name, src_is_xb, xb_ap, with_gb, out_fn, pre_wait):
            with contextlib.ExitStack() as ps:
                s1_ps = ps.enter_context(nc.psum_tensor(name + "_s1", [128, TOK], F32))
                s2_ps = ps.enter_context(nc.psum_tensor(name + "_s2", [128, TOK], F32))
                dln_ps = ps.enter_context(nc.psum_tensor(name + "_dum", [32, 1], F32))
                xsq = ps.enter_context(nc.sbuf_tensor(name + "_xsq", [128, 2 * TOK], BF16))
                xbf = None
                if not src_is_xb:
                    xbf = ps.enter_context(nc.sbuf_tensor(name + "_xbf", [128, 2 * TOK], BF16))
                mu = ps.enter_context(nc.sbuf_tensor(name + "_mu", [128, TOK], F32))
                rstd = ps.enter_context(nc.sbuf_tensor(name + "_rstd", [128, TOK], F32))
                m2 = ps.enter_context(nc.sbuf_tensor(name + "_m2", [128, TOK], F32))
                ta = ps.enter_context(nc.sbuf_tensor(name + "_ta", [128, TOK], F32))

                plan = Plan()
                cast_dve = {}
                sq_act = {}
                mm_pe = {}
                for f in range(NF):
                    if xbf is not None:
                        def cast_fn(eng, ff=f):
                            if ff >= 2:
                                eng.wait_ge(spe.h, mm_pe[ff - 2])
                            eng.tensor_scalar(
                                out=xbf[:, (ff % 2) * TOK:(ff % 2 + 1) * TOK],
                                in0=stream[:, ff * TOK:(ff + 1) * TOK],
                                scalar1=1.0, scalar2=None,
                                op0=ALU.mult).then_inc(sdve.h, 1)
                        cast_dve[f] = sdve.inc()
                        plan.add("dve", cast_fn)

                    def sq_fn(eng, ff=f):
                        if xbf is not None:
                            eng.wait_ge(sdve.h, cast_dve[ff])
                        elif ff >= 2:
                            eng.wait_ge(spe.h, mm_pe[ff - 2])
                        src = (xbf[:, (ff % 2) * TOK:(ff % 2 + 1) * TOK]
                               if xbf is not None
                               else xb_ap[:, ff * TOK:(ff + 1) * TOK])
                        eng.activation(out=xsq[:, (ff % 2) * TOK:(ff % 2 + 1) * TOK],
                                       in_=src, func=AF.Square).then_inc(sact.h, 1)
                    sq_act[f] = sact.inc()
                    plan.add("act", sq_fn)

                    def mm_fn(eng, ff=f):
                        eng.wait_ge(sact.h, sq_act[ff])
                        src1 = (xbf[:, (ff % 2) * TOK:(ff % 2 + 1) * TOK]
                                if xbf is not None
                                else xb_ap[:, ff * TOK:(ff + 1) * TOK])
                        src2 = xsq[:, (ff % 2) * TOK:(ff % 2 + 1) * TOK]
                        st = (ff == 0)
                        sp = (ff == NF - 1)
                        eng.matmul(s1_ps[:, 0:512], ones128[:], src1[:, 0:512], start=st, stop=sp)
                        eng.matmul(s1_ps[:, 512:1024], ones128[:], src1[:, 512:1024], start=st, stop=sp)
                        eng.matmul(s2_ps[:, 0:512], ones128[:], src2[:, 0:512], start=st, stop=sp)
                        eng.matmul(s2_ps[:, 512:1024], ones128[:], src2[:, 512:1024],
                                   start=st, stop=sp).then_inc(spe.h, 1)
                        if sp:
                            eng.matmul(dln_ps[:], ones128[:, 0:32], onescol[:],
                                       start=True, stop=True).then_inc(spe.h, 1)
                    mm_pe[f] = spe.inc()
                    if f == NF - 1:
                        spe.inc()
                    plan.add("pe", mm_fn)

                stats_done = mm_pe[NF - 1] + 1

                def mu_fn(eng):
                    eng.wait_ge(spe.h, stats_done)
                    eng.tensor_scalar(out=mu[:], in0=s1_ps[:], scalar1=1.0 / 2048.0,
                                      scalar2=None, op0=ALU.mult).then_inc(sdve.h, 1)
                mu_dve = sdve.inc()
                plan.add("dve", mu_fn)

                def musq_fn(eng):
                    eng.wait_ge(sdve.h, mu_dve)
                    eng.activation(out=ta[:], in_=mu[:], func=AF.Square).then_inc(sact.h, 1)
                musq_act = sact.inc()
                plan.add("act", musq_fn)

                def var_fn(eng):
                    eng.wait_ge(sact.h, musq_act)
                    eng.scalar_tensor_tensor(
                        out=rstd[:], in0=s2_ps[:], scalar=1.0 / 2048.0, in1=ta[:],
                        op0=ALU.mult, op1=ALU.subtract).then_inc(sdve.h, 1)
                var_dve = sdve.inc()
                plan.add("dve", var_fn)

                rstd_act = sact.inc(2)

                def rstd_fn(eng):
                    eng.wait_ge(sdve.h, var_dve)
                    eng.activation(out=rstd[:], in_=rstd[:], func=AF.Ln,
                                   bias=cp(C_EPS)).then_inc(sact.h, 1)
                    eng.activation(out=rstd[:], in_=rstd[:], func=AF.Exp,
                                   scale=cp(C_NHALF)).then_inc(sact.h, 1)
                plan.add("act", rstd_fn)

                def m2_fn(eng):
                    eng.wait_ge(sact.h, rstd_act)
                    eng.tensor_mul(out=m2[:], in0=mu[:], in1=rstd[:]).then_inc(sdve.h, 1)
                m2_dve = sdve.inc()
                plan.add("dve", m2_fn)

                apply_dve = {}
                for f in range(NF):
                    def ap_fn(eng, ff=f):
                        if ff == 0:
                            eng.wait_ge(sdve.h, m2_dve)
                        eng.tensor_mul(out=ta[:], in0=stream[:, ff * TOK:(ff + 1) * TOK],
                                       in1=rstd[:])
                        if with_gb:
                            eng.tensor_sub(out=ta[:], in0=ta[:], in1=m2[:])
                            eng.tensor_scalar(out=out_fn(ff), in0=ta[:],
                                              scalar1=lng_sb[:, ff:ff + 1],
                                              scalar2=lnb_sb[:, ff:ff + 1],
                                              op0=ALU.mult, op1=ALU.add).then_inc(sdve.h, 1)
                        else:
                            eng.tensor_sub(out=out_fn(ff), in0=ta[:],
                                           in1=m2[:]).then_inc(sdve.h, 1)
                    apply_dve[f] = sdve.inc()
                    plan.add("dve", ap_fn)

                with nc.Block(name) as blk:
                    @blk.tensor
                    def _(eng):
                        for sem, val in pre_wait.get("pe", []):
                            eng.wait_ge(sem.h, val)
                        plan.run("pe", eng)

                    @blk.scalar
                    def _(eng):
                        for sem, val in pre_wait.get("act", []):
                            eng.wait_ge(sem.h, val)
                        plan.run("act", eng)

                    @blk.vector
                    def _(eng):
                        for sem, val in pre_wait.get("dve", []):
                            eng.wait_ge(sem.h, val)
                        plan.run("dve", eng)
                return apply_dve

        with nc.sbuf_tensor("xb", [128, NF * TOK], BF16) as xb:
            with nc.Block("p0") as blk:
                @blk.sync
                def _(eng):
                    src = x_in.rearrange("(f p) t -> p f t", p=128)
                    dst = stream[:].rearrange("p (f t) -> p f t", f=NF)
                    eng.dma_start(out=dst[:, 0:8], in_=src[:, 0:8]).then_inc(sx.h, 16)
                    eng.dma_start(out=dst[:, 8:16], in_=src[:, 8:16]).then_inc(sx.h, 16)
                    for ap_in, sb in ((bq_in, bq_sb), (bk_in, bk_sb), (bo_in, bo_sb),
                                      (lng_in, lng_sb), (lnb_in, lnb_sb), (ident_in, ident)):
                        eng.dma_start(out=sb[:], in_=ap_in).then_inc(sx.h, 16)

                @blk.gpsimd
                def _(eng):
                    src = x_in.rearrange("(f p) t -> p f t", p=128)
                    dst = xb[:].rearrange("p (f t) -> p f t", f=NF)
                    eng.dma_start(out=dst[:, 0:8], in_=src[:, 0:8]).then_inc(scast.h, 16)
                    eng.dma_start(out=dst[:, 8:16], in_=src[:, 8:16]).then_inc(scast.h, 16)
                    eng.memset(ones128[:], 1.0)
                    eng.memset(onescol[:], 1.0)
                    eng.memset(onesrow[:], 1.0)
                    for i, v in CVALS.items():
                        eng.memset(cpool[:, i:i + 1], v)
            sx.n += 8 * 16
            scast.n += 32
            p0_sx = sx.n
            p0_scast = scast.n

            emit_layernorm("p1", True, xb,
                           True, lambda f: xb[:, f * TOK:(f + 1) * TOK],
                           {"pe": [(scast, p0_scast)],
                            "act": [(scast, p0_scast)],
                            "dve": [(sx, p0_sx)]})

            with contextlib.ExitStack() as ps:
                bv_bc = ps.enter_context(nc.sbuf_tensor("bv_bc", [128, DM], F32))
                bv_row = ps.enter_context(nc.sbuf_tensor("bv_row", [1, DM], F32))
                wbuf = [ps.enter_context(nc.sbuf_tensor(f"w2_{i}", [128, NF * 512], BF16))
                        for i in range(2)]
                kstage = [ps.enter_context(nc.sbuf_tensor(f"kst_{i}", [128, TOK], BF16))
                          for i in range(2)]
                vstage = [ps.enter_context(nc.sbuf_tensor(f"vst_{i}", [128, 512], BF16))
                          for i in range(2)]
                pp = [ps.enter_context(nc.psum_tensor(f"p2_{i}", [128, 512], F32))
                      for i in range(2)]
                bv_ps = ps.enter_context(nc.psum_tensor("p2_bv", [128, DM], F32))
                dum2_ps = ps.enter_context(nc.psum_tensor("p2_dum", [32, 1], F32))

                def p2dummy(eng):
                    eng.matmul(dum2_ps[:], ones128[:, 0:32], onescol[:],
                               start=True, stop=True).then_inc(spe.h, 1)

                plan = Plan()
                cc_in_ap = cc_in.ap()
                cc_in_V = cc_in_ap[2048:4096, :].rearrange("(t two) n -> t (two n)", two=2)

                def bvload_fn(eng):
                    eng.dma_start(out=bv_row[:], in_=bv_in).then_inc(sx.h, 16)
                bv_sx = sx.inc(16)
                plan.add("sync", bvload_fn)

                def bvbc_fn(eng):
                    eng.wait_ge(sx.h, bv_sx)
                    for j in range(3):
                        eng.matmul(bv_ps[:, j * 512:(j + 1) * 512], onesrow[:],
                                   bv_row[:, j * 512:(j + 1) * 512], start=True, stop=True)
                    eng.matmul(bv_ps[:, 1536:2048], onesrow[:],
                               bv_row[:, 1536:2048], start=True, stop=True).then_inc(spe.h, 1)
                    p2dummy(eng)
                spe.inc()
                bv_pe = spe.inc()
                plan.add("pe", bvbc_fn)

                def bvcp_fn(eng):
                    eng.wait_ge(spe.h, bv_pe)
                    eng.tensor_copy(out=bv_bc[:], in_=bv_ps[:]).then_inc(sdve.h, 1)
                bv_dve = sdve.inc()
                plan.add("dve", bvcp_fn)

                kmm_pe, kevac_act, kdma = {}, {}, {}
                vmm_pe, vevac_dve, vdma = {}, {}, {}
                qmm_pe, qevac_act, qdma = {}, {}, {}
                wk_fetch, wv_fetch, wq_fetch = {}, {}, {}

                for g in range(16):
                    def wkf(eng, g=g):
                        if g >= 2:
                            eng.wait_ge(spe.h, kmm_pe[(g - 2, 1)])
                        eng.dma_start(out=wbuf[g % 2][:, 0:2048],
                                      in_=wk_in[:, g * 2048:(g + 1) * 2048]).then_inc(s_w[g % 2].h, 16)
                    plan.add("gpsimd", wkf)
                    wk_fetch[g] = s_w[g % 2].inc(16)

                    for half in range(2):
                        def kmm(eng, g=g, half=half):
                            eng.wait_ge(s_w[g % 2].h, wk_fetch[g])
                            idx = g * 2 + half
                            if idx >= 2:
                                pg, ph = divmod(idx - 2, 2)
                                eng.wait_ge(sact.h, kevac_act[(pg, ph)])
                            ins = None
                            for f in range(NF):
                                ins = eng.matmul(
                                    pp[half][:],
                                    wbuf[g % 2][:, f * 128:(f + 1) * 128],
                                    xb[:, f * TOK + half * 512: f * TOK + half * 512 + 512],
                                    start=(f == 0), stop=(f == NF - 1))
                            ins.then_inc(spe.h, 1)
                            p2dummy(eng)
                        spe.inc()
                        kmm_pe[(g, half)] = spe.inc()
                        plan.add("pe", kmm)

                        def kevac(eng, g=g, half=half):
                            eng.wait_ge(spe.h, kmm_pe[(g, half)])
                            if g >= 2 and half == 0:
                                eng.wait_ge(s_st[g % 2].h, kdma[g - 2])
                            eng.activation(out=kstage[g % 2][:, half * 512:half * 512 + 512],
                                           in_=pp[half][:], func=AF.Identity,
                                           bias=bk_sb[:, g:g + 1]).then_inc(sact.h, 1)
                        kevac_act[(g, half)] = sact.inc()
                        plan.add("act", kevac)

                    def kdma_fn(eng, g=g):
                        eng.wait_ge(sact.h, kevac_act[(g, 1)])
                        eng.dma_start(out=cc_in_ap[g * 128:(g + 1) * 128, :],
                                      in_=kstage[g % 2][:]).then_inc(s_st[g % 2].h, 16)
                    plan.add("sync", kdma_fn)
                    kdma[g] = s_st[g % 2].inc(16)

                wv_r = wv_in.rearrange("p (f n) -> p f n", f=NF)
                for j in range(4):
                    def wvf(eng, j=j):
                        if j >= 2:
                            eng.wait_ge(spe.h, vmm_pe[(j - 2, NT - 1)])
                        src = wv_r[:, :, j * 512:(j + 1) * 512]
                        dst = wbuf[j % 2][:].rearrange("p (f n) -> p f n", f=NF)
                        eng.dma_start(out=dst[:, 0:8], in_=src[:, 0:8]).then_inc(s_w[j % 2].h, 16)
                        eng.dma_start(out=dst[:, 8:16], in_=src[:, 8:16]).then_inc(s_w[j % 2].h, 16)
                    plan.add("gpsimd", wvf)
                    wv_fetch[j] = s_w[j % 2].inc(32)

                    for t in range(NT):
                        def vmm(eng, j=j, t=t):
                            eng.wait_ge(s_w[j % 2].h, wv_fetch[j])
                            idx = j * NT + t
                            if idx < 2:
                                eng.wait_ge(sact.h, kevac_act[(15, 1)])
                            else:
                                pj, pt = divmod(idx - 2, NT)
                                eng.wait_ge(sdve.h, vevac_dve[(pj, pt)])
                            ins = None
                            for f in range(NF):
                                ins = eng.matmul(
                                    pp[t % 2][:],
                                    xb[:, f * TOK + t * 128: f * TOK + t * 128 + 128],
                                    wbuf[j % 2][:, f * 512:(f + 1) * 512],
                                    start=(f == 0), stop=(f == NF - 1))
                            ins.then_inc(spe.h, 1)
                            p2dummy(eng)
                        spe.inc()
                        vmm_pe[(j, t)] = spe.inc()
                        plan.add("pe", vmm)

                        def vevac(eng, j=j, t=t):
                            eng.wait_ge(spe.h, vmm_pe[(j, t)])
                            idx = j * NT + t
                            if idx == 0:
                                eng.wait_ge(sdve.h, bv_dve)
                            if idx >= 2:
                                pj, pt = divmod(idx - 2, NT)
                                eng.wait_ge(s_vst[t % 2].h, vdma[(pj, pt)])
                            eng.tensor_add(out=vstage[t % 2][:], in0=pp[t % 2][:],
                                           in1=bv_bc[:, j * 512:(j + 1) * 512]).then_inc(sdve.h, 1)
                        vevac_dve[(j, t)] = sdve.inc()
                        plan.add("dve", vevac)

                        def vdma_fn(eng, j=j, t=t):
                            eng.wait_ge(sdve.h, vevac_dve[(j, t)])
                            eng.dma_start(
                                out=cc_in_V[t * 128:(t + 1) * 128, j * 512:(j + 1) * 512],
                                in_=vstage[t % 2][:]).then_inc(s_vst[t % 2].h, 16)
                        plan.add("sync", vdma_fn)
                        vdma[(j, t)] = s_vst[t % 2].inc(16)

                staged_vals = [(s_st[0], s_st[0].n), (s_st[1], s_st[1].n),
                               (s_vst[0], s_vst[0].n), (s_vst[1], s_vst[1].n)]

                def coll_fn(eng):
                    for sem, val in staged_vals:
                        eng.wait_ge(sem.h, val)
                    eng.collective_compute(
                        "AllGather", ALU.bypass,
                        replica_groups=[list(range(NCORES))],
                        ins=[cc_in.ap().opt()],
                        outs=[cc_out.ap().opt()],
                    ).then_inc(scc.h, 1)
                plan.add("gpsimd", coll_fn)
                scc.inc()

                for g in range(16):
                    def wqf(eng, g=g):
                        if g >= 2:
                            eng.wait_ge(spe.h, qmm_pe[(g - 2, 1)])
                        else:
                            eng.wait_ge(spe.h, vmm_pe[(3, NT - 1)])
                        eng.dma_start(out=wbuf[g % 2][:, 0:2048],
                                      in_=wq_in[:, g * 2048:(g + 1) * 2048]).then_inc(s_w[g % 2].h, 16)
                    plan.add("gpsimd", wqf)
                    wq_fetch[g] = s_w[g % 2].inc(16)

                    for half in range(2):
                        def qmm(eng, g=g, half=half):
                            eng.wait_ge(s_w[g % 2].h, wq_fetch[g])
                            idx = g * 2 + half
                            if idx < 2:
                                eng.wait_ge(sdve.h, vevac_dve[(3, NT - 1)])
                            else:
                                pg, ph = divmod(idx - 2, 2)
                                eng.wait_ge(sact.h, qevac_act[(pg, ph)])
                            ins = None
                            for f in range(NF):
                                ins = eng.matmul(
                                    pp[half][:],
                                    wbuf[g % 2][:, f * 128:(f + 1) * 128],
                                    xb[:, f * TOK + half * 512: f * TOK + half * 512 + 512],
                                    start=(f == 0), stop=(f == NF - 1))
                            ins.then_inc(spe.h, 1)
                            p2dummy(eng)
                        spe.inc()
                        qmm_pe[(g, half)] = spe.inc()
                        plan.add("pe", qmm)

                        def qevac(eng, g=g, half=half):
                            eng.wait_ge(spe.h, qmm_pe[(g, half)])
                            if half == 0:
                                if g < 2:
                                    eng.wait_ge(s_st[g % 2].h, kdma[14 + g])
                                else:
                                    eng.wait_ge(s_qst[g % 2].h, qdma[g - 2])
                            eng.activation(
                                out=kstage[g % 2][:, half * 512:half * 512 + 512],
                                in_=pp[half][:], func=AF.Identity,
                                bias=bq_sb[:, g:g + 1]).then_inc(sact.h, 1)
                        qevac_act[(g, half)] = sact.inc()
                        plan.add("act", qevac)

                    def qdma_fn(eng, g=g):
                        eng.wait_ge(sact.h, qevac_act[(g, 1)])
                        eng.dma_start(out=q_dram.ap()[g],
                                      in_=kstage[g % 2][:]).then_inc(s_qst[g % 2].h, 16)
                    plan.add("sync", qdma_fn)
                    qdma[g] = s_qst[g % 2].inc(16)

                with nc.Block("p2") as blk:
                    @blk.tensor
                    def _(eng):
                        plan.run("pe", eng)

                    @blk.scalar
                    def _(eng):
                        plan.run("act", eng)

                    @blk.vector
                    def _(eng):
                        plan.run("dve", eng)

                    @blk.sync
                    def _(eng):
                        plan.run("sync", eng)

                    @blk.gpsimd
                    def _(eng):
                        plan.run("gpsimd", eng)
            p2_qst = (s_qst[0].n, s_qst[1].n)

            if debug:
                with nc.Block("dbgxn") as blk:
                    @blk.sync
                    def _(eng):
                        eng.dma_start(out=dbg_xn, in_=xb[:]).then_inc(sout.h, 16)
                        sout.n += 16
                        eng.wait_ge(sout.h, sout.n)

        with contextlib.ExitStack() as ps:
            kb_sb = [ps.enter_context(nc.sbuf_tensor(f"kb_{i}", [128, 8192], BF16))
                     for i in range(2)]
            vb_sb = [ps.enter_context(nc.sbuf_tensor(f"vb_{i}", [128, 8192], BF16))
                     for i in range(2)]
            qcur = [ps.enter_context(nc.sbuf_tensor(f"qc_{i}", [128, 512], BF16))
                    for i in range(2)]
            abuf = [ps.enter_context(nc.sbuf_tensor(f"ab_{i}", [128, 512], BF16))
                    for i in range(4)]
            av_sb = ps.enter_context(nc.sbuf_tensor("av_sb", [128, NH * 512], BF16))
            mask_sb = ps.enter_context(nc.sbuf_tensor("mask_sb", [128, 8 * 896], BF16))
            rdbc_sb = ps.enter_context(nc.sbuf_tensor("rdbc_sb", [128, 512], F32))
            wob = [ps.enter_context(nc.sbuf_tensor(f"wob_{i}", [128, 2048], BF16))
                   for i in range(2)]

            s_ps = [ps.enter_context(nc.psum_tensor(f"s_ps{i}", [128, 512], F32))
                    for i in range(3)]
            av_ps = [ps.enter_context(nc.psum_tensor(f"av_ps{i}", [128, 512], F32))
                     for i in range(2)]
            den_ps = [ps.enter_context(nc.psum_tensor(f"den_ps{i}", [128, 512], F32))
                      for i in range(2)]
            dum3_ps = ps.enter_context(nc.psum_tensor("dum3_ps", [32, 1], F32))

            plan = Plan()

            def mload(eng):
                src = masks_in.transpose([1, 0, 2])
                dst = mask_sb[:].rearrange("p (r n) -> p r n", r=8)
                eng.dma_start(out=dst[:, 0:4], in_=src[:, 0:4]).then_inc(sx.h, 16)
                eng.dma_start(out=dst[:, 4:8], in_=src[:, 4:8]).then_inc(sx.h, 16)
            plan.add("sync", mload)
            mask_sx = sx.inc(32)

            ccr = cc_out.ap().rearrange("(r x) n -> r x n", x=4096)
            vsrc_all = ccr[:, 2048:4096, :].rearrange("r (t two) n -> r t (two n)", two=2)

            exp_act, mask_dve = {}, {}
            headdone_pe, headfinal_pe, rden_act = {}, {}, {}
            avevac_dve = {}
            mask_info = {}

            def emit_mask(mgt):
                if mgt not in mask_info:
                    return
                mr, mkbr, masl = mask_info[mgt]

                def maskf(eng, mgt=mgt, mr=mr, mkbr=mkbr, masl=masl):
                    if mgt == 0:
                        eng.wait_ge(sx.h, mask_sx)
                    eng.wait_ge(sact.h, exp_act[mgt])
                    moff = mr * 896 + (384 - 128 * mkbr)
                    eng.tensor_mul(out=abuf[masl][:], in0=abuf[masl][:],
                                   in1=mask_sb[:, moff:moff + 512]).then_inc(sdve.h, 1)
                mask_dve[mgt] = sdve.inc()
                plan.add("dve", maskf)
            kvf_done, smm_pe = {}, {}
            wo_pe, woevac_dve, wof_cast, wo_cover = {}, {}, {}, {}

            def emit_kv_fetch(qb, h, hh):
                Tq = 512 * (qb + 1)
                ktn = 4 * (qb + 1)
                slot = hh % 2

                def qf(eng, qb=qb, h=h, slot=slot, hh=hh):
                    if hh >= 2:
                        eng.wait_ge(spe.h, headfinal_pe[hh - 2])
                    eng.dma_start(out=qcur[slot][:],
                                  in_=q_dram.ap()[h, :, qb * 512:(qb + 1) * 512]).then_inc(s_kv[slot].h, 16)
                plan.add("sync", qf)
                s_kv[slot].inc(16)

                def kf(eng, h=h, slot=slot, Tq=Tq):
                    ksrc = ccr[:, h * 128:(h + 1) * 128, 0:Tq].transpose([1, 0, 2])
                    kdst = kb_sb[slot][:, 0:8 * Tq].rearrange("p (r t) -> p r t", r=8)
                    eng.dma_start(out=kdst[:, 0:4], in_=ksrc[:, 0:4]).then_inc(s_kv[slot].h, 16)
                    eng.dma_start(out=kdst[:, 4:8], in_=ksrc[:, 4:8]).then_inc(s_kv[slot].h, 16)
                plan.add("sync", kf)
                s_kv[slot].inc(32)

                def vf(eng, h=h, slot=slot, Tq=Tq, ktn=ktn):
                    vsrc = vsrc_all[:, 0:Tq, h * 128:(h + 1) * 128]
                    vsrc2 = vsrc.rearrange("r (kt p) d -> p r kt d", p=128)
                    vdst = vb_sb[slot][:, 0:8 * Tq].rearrange("p (r kt d) -> p r kt d",
                                                              r=8, kt=ktn)
                    for r in range(8):
                        if ktn <= 4:
                            eng.dma_start(out=vdst[:, r:r + 1],
                                          in_=vsrc2[:, r:r + 1]).then_inc(s_kv[slot].h, 16)
                        else:
                            eng.dma_start(out=vdst[:, r:r + 1, 0:4],
                                          in_=vsrc2[:, r:r + 1, 0:4]).then_inc(s_kv[slot].h, 16)
                            eng.dma_start(out=vdst[:, r:r + 1, 4:ktn],
                                          in_=vsrc2[:, r:r + 1, 4:ktn]).then_inc(s_kv[slot].h, 16)
                plan.add("sync", vf)
                s_kv[slot].inc(16 * (8 if ktn <= 4 else 16))
                return s_kv[slot].n

            gt = 0
            avmm_den = {}

            for qb in range(2):
                ktn = 4 * (qb + 1)
                ntile = 8 * ktn
                for h in range(NH):
                    hh = qb * NH + h
                    kvf_done[hh] = emit_kv_fetch(qb, h, hh)

                    def mk_smm(hh, r, kt, ti, gt, sl, ktn):
                        def smm(eng):
                            if ti == 0:
                                eng.wait_ge(s_kv[hh % 2].h, kvf_done[hh])
                                if hh >= 2:
                                    eng.wait_ge(sdve.h, avevac_dve[hh - 2])
                                if hh >= 2:
                                    eng.wait_ge(sact.h, rden_act[hh - 2])
                            if gt >= 2:
                                eng.wait_ge(sact.h, exp_act[gt - 2])
                            off = (r * ktn + kt) * 128
                            eng.matmul(s_ps[sl][:], kb_sb[hh % 2][:, off:off + 128],
                                       qcur[hh % 2][:],
                                       start=True, stop=True).then_inc(spe.h, 1)
                            eng.matmul(dum3_ps[:], ones128[:, 0:32], onescol[:],
                                       start=True, stop=True).then_inc(spe.h, 1)
                        return smm

                    def mk_avmm(hh, ti, gt, asl, ntile, band, r, kt, ktn):
                        def avmm(eng):
                            if band:
                                eng.wait_ge(sdve.h, mask_dve[gt])
                            else:
                                eng.wait_ge(sact.h, exp_act[gt])
                            off = (r * ktn + kt) * 128
                            eng.matmul(av_ps[hh % 2][:], vb_sb[hh % 2][:, off:off + 128],
                                       abuf[asl][:], start=(ti == 0),
                                       stop=(ti == ntile - 1)).then_inc(spe.h, 1)
                            eng.matmul(den_ps[hh % 2][:], ones128[:], abuf[asl][:],
                                       start=(ti == 0),
                                       stop=(ti == ntile - 1)).then_inc(spe.h, 1)
                        return avmm

                    def add_avmm(args):
                        hh_, ti_, gt_, asl_, ntile_, band_, r_, kt_, ktn_ = args
                        spe.inc()
                        avmm_den[gt_] = spe.inc()
                        plan.add("pe", mk_avmm(hh_, ti_, gt_, asl_, ntile_, band_,
                                               r_, kt_, ktn_))

                    def mk_exp(pgt, pasl, ppsl, cover):
                        def expf(eng):
                            eng.wait_ge(spe.h, cover)
                            eng.activation(out=abuf[pasl][:], in_=s_ps[ppsl][:],
                                           func=AF.Exp, bias=cp(C_EBIAS),
                                           scale=cp(C_SCALE)).then_inc(sact.h, 1)
                        return expf

                    pend_av = []
                    for r in range(8):
                        for kt in range(ktn):
                            ti = r * ktn + kt
                            band = (kt >= 4 * qb)
                            kbr = kt - 4 * qb
                            sl = gt % 3
                            asl = gt % 4
                            if band:
                                mask_info[gt] = (r, kbr, asl)

                            spe.inc()
                            smm_pe[gt] = spe.inc()
                            plan.add("pe", mk_smm(hh, r, kt, ti, gt, sl, ktn))

                            exp_act[gt] = sact.inc()
                            plan.add("act", mk_exp(gt, asl, sl, smm_pe[gt]))
                            emit_mask(gt)

                            pend_av.append((hh, ti, gt, asl, ntile, band, r, kt, ktn))
                            if len(pend_av) > 2:
                                add_avmm(pend_av.pop(0))
                            gt += 1

                    for a in pend_av:
                        add_avmm(a)

                    def headdummy(eng):
                        eng.matmul(dum3_ps[:], ones128[:, 0:32], onescol[:],
                                   start=True, stop=True).then_inc(spe.h, 1)
                    headfinal_pe[hh] = spe.inc()
                    plan.add("pe", headdummy)

                    rden_act[hh] = sact.inc(2)

                    def rdenf(eng, hh=hh):
                        eng.wait_ge(spe.h, headfinal_pe[hh])
                        if hh >= 1:
                            eng.wait_ge(sdve.h, avevac_dve[hh - 1])
                        eng.activation(out=rdbc_sb[:], in_=den_ps[hh % 2][:],
                                       func=AF.Ln).then_inc(sact.h, 1)
                        eng.activation(out=rdbc_sb[:], in_=rdbc_sb[:], func=AF.Exp,
                                       scale=cp(C_NONE)).then_inc(sact.h, 1)
                    plan.add("act", rdenf)

                    def avevac(eng, hh=hh, h=h, qb=qb):
                        eng.wait_ge(sact.h, rden_act[hh])
                        if qb == 1 and h == 0:
                            eng.wait_ge(spe.h, wo_pe[(0, 15)])
                        eng.tensor_mul(out=av_sb[:, h * 512:(h + 1) * 512],
                                       in0=av_ps[hh % 2][:],
                                       in1=rdbc_sb[:]).then_inc(sdve.h, 1)
                    avevac_dve[hh] = sdve.inc()
                    plan.add("dve", avevac)

                for g in range(16):
                    def wof(eng, qb=qb, g=g):
                        idx = qb * 16 + g
                        if idx >= 2:
                            pqb, pg = divmod(idx - 2, 16)
                            eng.wait_ge(spe.h, wo_pe[(pqb, pg)])
                        eng.dma_start(out=wob[g % 2][:],
                                      in_=wo_in[:, g * 2048:(g + 1) * 2048]).then_inc(s_w[g % 2].h, 16)
                    plan.add("gpsimd", wof)
                    wof_cast[(qb, g)] = s_w[g % 2].inc(16)

                    def womm(eng, qb=qb, g=g):
                        eng.wait_ge(s_w[g % 2].h, wof_cast[(qb, g)])
                        if g == 0:
                            eng.wait_ge(sdve.h, avevac_dve[qb * NH + NH - 1])
                        if g >= 2:
                            eng.wait_ge(sdve.h, woevac_dve[(qb, g - 2)])
                        ins = None
                        for f in range(NF):
                            ins = eng.matmul(s_ps[g % 2][:],
                                             wob[g % 2][:, f * 128:(f + 1) * 128],
                                             av_sb[:, f * 512:(f + 1) * 512],
                                             start=(f == 0), stop=(f == NF - 1))
                        ins.then_inc(spe.h, 1)
                        eng.matmul(dum3_ps[:], ones128[:, 0:32], onescol[:],
                                   start=True, stop=True).then_inc(spe.h, 1)
                    wo_pe[(qb, g)] = spe.inc()
                    wo_cover[(qb, g)] = spe.inc()
                    plan.add("pe", womm)

                    def woevac(eng, qb=qb, g=g):
                        eng.wait_ge(spe.h, wo_cover[(qb, g)])
                        sl_ = stream[:, g * TOK + qb * 512: g * TOK + qb * 512 + 512]
                        eng.scalar_tensor_tensor(out=sl_, in0=s_ps[g % 2][:],
                                                 scalar=bo_sb[:, g:g + 1], in1=sl_,
                                                 op0=ALU.add, op1=ALU.add).then_inc(sdve.h, 1)
                    woevac_dve[(qb, g)] = sdve.inc()
                    plan.add("dve", woevac)

            with nc.Block("p3") as blk:
                @blk.tensor
                def _(eng):
                    eng.wait_ge(scc.h, 1)
                    plan.run("pe", eng)

                @blk.scalar
                def _(eng):
                    plan.run("act", eng)

                @blk.vector
                def _(eng):
                    plan.run("dve", eng)

                @blk.sync
                def _(eng):
                    eng.wait_ge(scc.h, 1)
                    eng.wait_ge(s_qst[0].h, p2_qst[0])
                    eng.wait_ge(s_qst[1].h, p2_qst[1])
                    plan.run("sync", eng)

                @blk.gpsimd
                def _(eng):
                    plan.run("gpsimd", eng)

        if debug:
            with nc.Block("dbgx2") as blk:
                @blk.sync
                def _(eng):
                    eng.dma_start(out=dbg_x2, in_=stream[:]).then_inc(sout.h, 16)
                    sout.n += 16
                    eng.wait_ge(sout.h, sout.n)

        with nc.sbuf_tensor("xn2", [128, NF * TOK], BF16) as xn2:
            emit_layernorm("p4", False, None, False,
                           lambda f: xn2[:, f * TOK:(f + 1) * TOK], {})

            with contextlib.ExitStack() as ps:
                wgb = [ps.enter_context(nc.sbuf_tensor(f"wgb_{i}", [128, NF * 512], BF16))
                       for i in range(2)]
                wub = [ps.enter_context(nc.sbuf_tensor(f"wub_{i}", [128, NF * 512], BF16))
                       for i in range(2)]
                sgb = [ps.enter_context(nc.sbuf_tensor(f"sgb_{i}", [128, 512], F32))
                       for i in range(2)]
                hst = [ps.enter_context(nc.sbuf_tensor(f"hst_{i}", [128, 512], F32))
                       for i in range(4)]
                g_ps = [ps.enter_context(nc.psum_tensor(f"g_ps{i}", [128, 512], F32))
                        for i in range(2)]
                u_ps = [ps.enter_context(nc.psum_tensor(f"u_ps{i}", [128, 512], F32))
                        for i in range(2)]
                dum5_ps = ps.enter_context(nc.psum_tensor("p5_dum", [32, 1], F32))

                def p5dummy(eng):
                    eng.matmul(dum5_ps[:], ones128[:, 0:32], onescol[:],
                               start=True, stop=True).then_inc(spe.h, 1)

                plan = Plan()
                wgu_fetch, gmm_pe, umm_pe = {}, {}, {}
                silu_act, ht_dve, hw_sync = {}, {}, {}

                wg_r = wg_in.rearrange("p (f n) -> p f n", f=NF)
                wu_r = wu_in.rearrange("p (f n) -> p f n", f=NF)

                for js in range(16):
                    def wguf(eng, js=js):
                        if js >= 2:
                            eng.wait_ge(spe.h, umm_pe[(js - 2, NT - 1)])
                        srcg = wg_r[:, :, js * 512:(js + 1) * 512]
                        srcu = wu_r[:, :, js * 512:(js + 1) * 512]
                        dstg = wgb[js % 2][:].rearrange("p (f n) -> p f n", f=NF)
                        dstu = wub[js % 2][:].rearrange("p (f n) -> p f n", f=NF)
                        eng.dma_start(out=dstg[:, 0:8], in_=srcg[:, 0:8]).then_inc(s_w[js % 2].h, 16)
                        eng.dma_start(out=dstg[:, 8:16], in_=srcg[:, 8:16]).then_inc(s_w[js % 2].h, 16)
                        eng.dma_start(out=dstu[:, 0:8], in_=srcu[:, 0:8]).then_inc(s_w[js % 2].h, 16)
                        eng.dma_start(out=dstu[:, 8:16], in_=srcu[:, 8:16]).then_inc(s_w[js % 2].h, 16)
                    plan.add("gpsimd", wguf)
                    wgu_fetch[js] = s_w[js % 2].inc(64)

                    for t in range(NT):
                        idx = js * NT + t

                        def gmm(eng, js=js, t=t, idx=idx):
                            eng.wait_ge(s_w[js % 2].h, wgu_fetch[js])
                            if idx >= 2:
                                pj, pt = divmod(idx - 2, NT)
                                eng.wait_ge(sact.h, silu_act[(pj, pt)])
                            ins = None
                            for f in range(NF):
                                ins = eng.matmul(
                                    g_ps[t % 2][:],
                                    xn2[:, f * TOK + t * 128: f * TOK + t * 128 + 128],
                                    wgb[js % 2][:, f * 512:(f + 1) * 512],
                                    start=(f == 0), stop=(f == NF - 1))
                            ins.then_inc(spe.h, 1)
                            p5dummy(eng)
                        spe.inc()
                        gmm_pe[(js, t)] = spe.inc()
                        plan.add("pe", gmm)

                        def umm(eng, js=js, t=t, idx=idx):
                            if idx >= 2:
                                pj, pt = divmod(idx - 2, NT)
                                eng.wait_ge(sdve.h, ht_dve[(pj, pt)])
                            ins = None
                            for f in range(NF):
                                ins = eng.matmul(
                                    u_ps[t % 2][:],
                                    xn2[:, f * TOK + t * 128: f * TOK + t * 128 + 128],
                                    wub[js % 2][:, f * 512:(f + 1) * 512],
                                    start=(f == 0), stop=(f == NF - 1))
                            ins.then_inc(spe.h, 1)
                            p5dummy(eng)
                        spe.inc()
                        umm_pe[(js, t)] = spe.inc()
                        plan.add("pe", umm)

                        def siluf(eng, js=js, t=t, idx=idx):
                            eng.wait_ge(spe.h, gmm_pe[(js, t)])
                            if idx >= 2:
                                pj, pt = divmod(idx - 2, NT)
                                eng.wait_ge(sdve.h, ht_dve[(pj, pt)])
                            eng.activation(out=sgb[t % 2][:], in_=g_ps[t % 2][:],
                                           func=AF.Silu).then_inc(sact.h, 1)
                        silu_act[(js, t)] = sact.inc()
                        plan.add("act", siluf)

                        def htf(eng, js=js, t=t, idx=idx):
                            eng.wait_ge(spe.h, umm_pe[(js, t)])
                            eng.wait_ge(sact.h, silu_act[(js, t)])
                            if idx >= 4:
                                pj, pt = divmod(idx - 4, NT)
                                eng.wait_ge(s_hst[idx % 4].h, hw_sync[(pj, pt)])
                            eng.tensor_mul(out=hst[idx % 4][:], in0=sgb[t % 2][:],
                                           in1=u_ps[t % 2][:]).then_inc(sdve.h, 1)
                        ht_dve[(js, t)] = sdve.inc()
                        plan.add("dve", htf)

                        def hwf(eng, js=js, t=t, idx=idx):
                            eng.wait_ge(sdve.h, ht_dve[(js, t)])
                            eng.dma_start(out=h_dram.ap()[t, :, js * 512:(js + 1) * 512],
                                          in_=hst[idx % 4][:]).then_inc(s_hst[idx % 4].h, 16)
                        plan.add("sync", hwf)
                        hw_sync[(js, t)] = s_hst[idx % 4].inc(16)

                with nc.Block("p5a") as blk:
                    @blk.tensor
                    def _(eng):
                        plan.run("pe", eng)

                    @blk.scalar
                    def _(eng):
                        plan.run("act", eng)

                    @blk.vector
                    def _(eng):
                        plan.run("dve", eng)

                    @blk.sync
                    def _(eng):
                        plan.run("sync", eng)

                    @blk.gpsimd
                    def _(eng):
                        plan.run("gpsimd", eng)
                p5a_hst = [s.n for s in s_hst]

        with contextlib.ExitStack() as ps:
            htile2 = [ps.enter_context(nc.sbuf_tensor(f"htile{i}", [128, II], F32))
                      for i in range(2)]
            hbf2 = [ps.enter_context(nc.sbuf_tensor(f"hbf{i}", [128, II], BF16))
                    for i in range(2)]
            t1 = ps.enter_context(nc.sbuf_tensor("t1", [128, 2048], F32))
            t2 = ps.enter_context(nc.sbuf_tensor("t2", [128, 2048], F32))
            t3 = ps.enter_context(nc.sbuf_tensor("t3", [128, 2048], F32))
            tstage = ps.enter_context(nc.sbuf_tensor("tstage", [128, 2048], BF16))
            tp_ps = [ps.enter_context(nc.psum_tensor(f"tp_ps{i}", [128, 128], BF16))
                     for i in range(2)]
            dum5b_ps = ps.enter_context(nc.psum_tensor("p5b_dum", [32, 1], F32))

            plan = Plan()
            hr_sync, msk_dve, sq_act5, hsw_sync = {}, {}, {}, {}
            tp_pe, tev_n = {}, {}

            def q_(t, k):
                return htile2[t % 2][:, k * 2048:(k + 1) * 2048]

            def qb_(t, k):
                return hbf2[t % 2][:, k * 2048:(k + 1) * 2048]

            for t in range(NT):
                def hrf(eng, t=t):
                    if t == 0:
                        for i in range(4):
                            eng.wait_ge(s_hst[i].h, p5a_hst[i])
                    if t >= 2:
                        eng.wait_ge(spe.h, tp_pe[(t - 2, 63)])
                    eng.dma_start(out=htile2[t % 2][:], in_=h_dram.ap()[t]).then_inc(sh.h, 16)
                plan.add("sync", hrf)
                hr_sync[t] = sh.inc(16)

                def castf(eng, t=t):
                    eng.wait_ge(sh.h, hr_sync[t])
                    eng.tensor_copy(out=hbf2[t % 2][:], in_=htile2[t % 2][:]).then_inc(sdve.h, 1)
                cast_v = sdve.inc()
                plan.add("dve", castf)

                def sqf(eng, t=t, cast_v=cast_v):
                    eng.wait_ge(sdve.h, cast_v)
                    eng.activation(out=htile2[t % 2][:], in_=htile2[t % 2][:],
                                   func=AF.Square).then_inc(sact.h, 1)
                sq_act5[t] = sact.inc()
                plan.add("act", sqf)

                def mskf(eng, t=t):
                    eng.wait_ge(sact.h, sq_act5[t])
                    eng.tensor_tensor(out=t1[:], in0=q_(t, 0), in1=q_(t, 1), op=ALU.min)
                    eng.tensor_tensor(out=t2[:], in0=q_(t, 2), in1=q_(t, 3), op=ALU.min)
                    eng.tensor_max(out=t1[:], in0=t1[:], in1=t2[:])
                    eng.tensor_max(out=t2[:], in0=q_(t, 0), in1=q_(t, 1))
                    eng.tensor_max(out=t3[:], in0=q_(t, 2), in1=q_(t, 3))
                    eng.tensor_tensor(out=t2[:], in0=t2[:], in1=t3[:], op=ALU.min)
                    eng.tensor_max(out=t2[:], in0=t1[:], in1=t2[:])
                    for k in range(4):
                        eng.tensor_tensor(out=q_(t, k), in0=q_(t, k), in1=t2[:], op=ALU.is_ge)
                        ins = eng.tensor_mul(out=qb_(t, k), in0=qb_(t, k), in1=q_(t, k))
                    ins.then_inc(sdve.h, 1)
                msk_dve[t] = sdve.inc()
                plan.add("dve", mskf)

                for ic in range(64):
                    def tpf(eng, t=t, ic=ic):
                        if ic == 0:
                            eng.wait_ge(sdve.h, msk_dve[t])
                            if t >= 1:
                                eng.wait_ge(shs.h, hsw_sync[(t - 1, 7)])
                        if ic >= 2:
                            kind, val = tev_n[(t, ic - 2)]
                            eng.wait_ge(sdve.h if kind == "dve" else sact.h, val)
                        eng.transpose(tp_ps[ic % 2][:],
                                      hbf2[t % 2][:, ic * 128:(ic + 1) * 128],
                                      ident[:]).then_inc(spe.h, 1)
                    tp_pe[(t, ic)] = spe.inc()
                    plan.add("pe", tpf)

                    eng_name = "dve" if ic % 2 == 0 else "act"

                    def tevac(eng, t=t, ic=ic, eng_name=eng_name):
                        eng.wait_ge(spe.h, tp_pe[(t, ic)] + 1)
                        if ic >= 16:
                            eng.wait_ge(shs.h, hsw_sync[(t, ic // 8 - 2)])
                        dst = tstage[:, (ic % 16) * 128:(ic % 16) * 128 + 128]
                        if eng_name == "dve":
                            eng.tensor_copy(out=dst, in_=tp_ps[ic % 2][:]).then_inc(sdve.h, 1)
                        else:
                            eng.activation(out=dst, in_=tp_ps[ic % 2][:],
                                           func=AF.Identity).then_inc(sact.h, 1)
                    if eng_name == "dve":
                        tev_n[(t, ic)] = ("dve", sdve.inc())
                    else:
                        tev_n[(t, ic)] = ("act", sact.inc())
                    plan.add(eng_name, tevac)

                    if ic % 8 == 7:
                        def tstore(eng, t=t, ic8=ic // 8):
                            for icc in range(ic8 * 8, ic8 * 8 + 8):
                                kind, val = tev_n[(t, icc)]
                                eng.wait_ge(sdve.h if kind == "dve" else sact.h, val)
                            src_ = tstage[:, (ic8 % 2) * 1024:(ic8 % 2) * 1024 + 1024]
                            eng.dma_start(
                                out=hs_fm_dram.ap()[ic8 * 8:(ic8 + 1) * 8, :,
                                                    t * 128:(t + 1) * 128].transpose([1, 0, 2]),
                                in_=src_.rearrange("p (i n) -> p i n", i=8)).then_inc(shs.h, 16)
                        plan.add("sync", tstore)
                        hsw_sync[(t, ic // 8)] = shs.inc(16)

                def tpdum(eng, t=t):
                    eng.matmul(dum5b_ps[:], ones128[:, 0:32], onescol[:],
                               start=True, stop=True).then_inc(spe.h, 1)
                spe.inc()
                plan.add("pe", tpdum)
            p5b_shs = shs.n

            with nc.Block("p5b") as blk:
                @blk.tensor
                def _(eng):
                    plan.run("pe", eng)

                @blk.scalar
                def _(eng):
                    plan.run("act", eng)

                @blk.vector
                def _(eng):
                    plan.run("dve", eng)

                @blk.sync
                def _(eng):
                    plan.run("sync", eng)

        with contextlib.ExitStack() as ps:
            hfm = [ps.enter_context(nc.sbuf_tensor(f"hfm_{i}", [128, 512], BF16))
                   for i in range(4)]
            wdb = [ps.enter_context(nc.sbuf_tensor(f"wdb_{i}", [128, 2048], BF16))
                   for i in range(3)]
            y_ps = [ps.enter_context(nc.psum_tensor(f"y_ps{i}", [128, 512], F32))
                    for i in range(8)]

            plan = Plan()
            hs_ts, wd_fetch, dmm_pe, yev_dve = {}, {}, {}, {}
            wd_slot_user = {}

            for half in range(2):
                for gh in range(2):
                    pi = half * 2 + gh
                    for ic in range(64):
                        gi = pi * 64 + ic

                        def hstr(eng, half=half, ic=ic, gi=gi):
                            if gi == 0:
                                eng.wait_ge(shs.h, p5b_shs)
                            if gi >= 4:
                                ppi, pic = divmod(gi - 4, 64)
                                eng.wait_ge(spe.h, dmm_pe[(ppi, pic)])
                            eng.dma_start(
                                out=hfm[ic % 4][:],
                                in_=hs_fm_dram.ap()[ic, :, half * 512:(half + 1) * 512]
                            ).then_inc(s_tp[ic % 4].h, 16)
                        plan.add("sync", hstr)
                        hs_ts[(pi, ic)] = s_tp[ic % 4].inc(16)

                        prev_user = wd_slot_user.get(ic % 3)

                        def wdf(eng, ic=ic, gi=gi, prev_user=prev_user):
                            if gi == 0:
                                eng.wait_ge(s_wdc.h, s_wdc.n)
                            if prev_user is not None:
                                eng.wait_ge(spe.h, dmm_pe[prev_user])
                            eng.dma_start(out=wdb[ic % 3][:],
                                          in_=wd_in[:, ic * 2048:(ic + 1) * 2048]).then_inc(s_wd[ic % 3].h, 16)
                        plan.add("gpsimd", wdf)
                        wd_fetch[(pi, ic)] = s_wd[ic % 3].inc(16)
                        wd_slot_user[ic % 3] = (pi, ic)

                        def dmm(eng, gh=gh, ic=ic, pi=pi):
                            eng.wait_ge(s_tp[ic % 4].h, hs_ts[(pi, ic)])
                            eng.wait_ge(s_wd[ic % 3].h, wd_fetch[(pi, ic)])
                            if ic == 0 and pi >= 1:
                                eng.wait_ge(sdve.h, yev_dve[pi - 1])
                            ins = None
                            for gg in range(8):
                                g = gh * 8 + gg
                                ins = eng.matmul(y_ps[gg][:],
                                                 wdb[ic % 3][:, g * 128:(g + 1) * 128],
                                                 hfm[ic % 4][:],
                                                 start=(ic == 0), stop=(ic == 63))
                            ins.then_inc(spe.h, 1)
                        dmm_pe[(pi, ic)] = spe.inc()
                        plan.add("pe", dmm)

                    def yev(eng, half=half, gh=gh, pi=pi):
                        eng.wait_ge(spe.h, dmm_pe[(pi, 63)])
                        ins = None
                        for gg in range(8):
                            g = gh * 8 + gg
                            sl_ = stream[:, g * TOK + half * 512: g * TOK + half * 512 + 512]
                            ins = eng.tensor_add(out=sl_, in0=y_ps[gg][:], in1=sl_)
                        ins.then_inc(sdve.h, 1)
                    yev_dve[pi] = sdve.inc()
                    plan.add("dve", yev)

            with nc.Block("p5c") as blk:
                @blk.tensor
                def _(eng):
                    plan.run("pe", eng)

                @blk.vector
                def _(eng):
                    plan.run("dve", eng)

                @blk.sync
                def _(eng):
                    plan.run("sync", eng)

                @blk.gpsimd
                def _(eng):
                    plan.run("gpsimd", eng)

        if debug:
            with nc.Block("dbgy") as blk:
                @blk.sync
                def _(eng):
                    eng.dma_start(out=dbg_y, in_=stream[:]).then_inc(sout.h, 16)
                    sout.n += 16
                    eng.wait_ge(sout.h, sout.n)

        apply6 = emit_layernorm("p6", False, None, False,
                                lambda f: stream[:, f * TOK:(f + 1) * TOK], {})

        with nc.Block("p6s") as blk:
            @blk.sync
            def _(eng):
                for f in range(NF):
                    eng.wait_ge(sdve.h, apply6[f])
                    eng.dma_start(out=out_t[f * 128:(f + 1) * 128, :],
                                  in_=stream[:, f * TOK:(f + 1) * TOK]).then_inc(sout.h, 16)
                    sout.n += 16
                if debug:
                    eng.dma_start(out=dbg_ccin, in_=cc_in.ap()).then_inc(sout.h, 16)
                    eng.dma_start(out=dbg_q.rearrange("h p t -> (h p) t"),
                                  in_=q_dram.ap().rearrange("h p t -> (h p) t")).then_inc(sout.h, 16)
                    for t in range(NT):
                        eng.dma_start(out=dbg_h[t], in_=h_dram.ap()[t]).then_inc(sout.h, 16)
                    eng.dma_start(out=dbg_hs.rearrange("i p t -> (i p) t"),
                                  in_=hs_fm_dram.ap().rearrange("i p t -> (i p) t")).then_inc(sout.h, 16)
                    sout.n += 16 * (3 + NT)
                eng.wait_ge(sout.h, sout.n)

    return nc


_NC_CACHE = None


def _get_nc():
    global _NC_CACHE
    if _NC_CACHE is None:
        _NC_CACHE = build_program()
    return _NC_CACHE


def kernel(L0_attn_X, L0_attn_ln_g, L0_attn_ln_b, L0_attn_Wq, L0_attn_bq,
           L0_attn_Wk, L0_attn_bk, L0_attn_Wv, L0_attn_bv, L0_attn_Wo,
           L0_attn_bo, L0_ffn_w_gate, L0_ffn_w_up, L0_ffn_w_down):
    X = np.asarray(L0_attn_X)[0]
    Wq = np.asarray(L0_attn_Wq)
    Wk = np.asarray(L0_attn_Wk)
    Wv = np.asarray(L0_attn_Wv)
    Wo = np.asarray(L0_attn_Wo)
    Wg = np.asarray(L0_ffn_w_gate)
    Wu = np.asarray(L0_ffn_w_up)
    Wd = np.asarray(L0_ffn_w_down)

    def lhsT_layout(W):
        n_in = W.shape[0] // 128
        return np.ascontiguousarray(
            W.reshape(n_in, 128, 16, 128).transpose(1, 2, 0, 3).reshape(128, -1))

    def rhs_layout(W):
        return np.ascontiguousarray(
            W.reshape(16, 128, -1).transpose(1, 0, 2).reshape(128, -1))

    perm = 4 * (np.arange(II) % 2048) + np.arange(II) // 2048

    wq_dev = lhsT_layout(Wq)
    wk_dev = lhsT_layout(Wk)
    wo_dev = lhsT_layout(Wo)
    wv_dev = rhs_layout(Wv)
    wg_dev = rhs_layout(Wg[:, perm])
    wu_dev = rhs_layout(Wu[:, perm])
    wd_dev = np.ascontiguousarray(
        Wd[perm, :].reshape(64, 128, 16, 128).transpose(1, 0, 2, 3).reshape(128, -1))

    def colvec(b):
        return np.ascontiguousarray(np.asarray(b).reshape(16, 128).T)

    bq_dev = colvec(L0_attn_bq)
    bk_dev = colvec(L0_attn_bk)
    bo_dev = colvec(L0_attn_bo)
    lng_dev = colvec(L0_attn_ln_g)
    lnb_dev = colvec(L0_attn_ln_b)
    bv_dev = np.ascontiguousarray(np.asarray(L0_attn_bv)[None, :])

    p = np.arange(128)[:, None]
    v = np.arange(896)[None, :]
    in_maps = []
    for c in range(NCORES):
        Xc = np.ascontiguousarray(X[c::8, :].T)
        masks = np.zeros((8, 128, 896), dtype=ml_dtypes.bfloat16)
        for r in range(8):
            delta = 1 if r > c else 0
            masks[r] = ((v - 384 - p) >= delta).astype(ml_dtypes.bfloat16)
        in_maps.append({
            "x": Xc, "wq": wq_dev, "wk": wk_dev, "wv": wv_dev, "wo": wo_dev,
            "wg": wg_dev, "wu": wu_dev, "wd": wd_dev,
            "bq": bq_dev, "bk": bk_dev, "bo": bo_dev, "bvr": bv_dev,
            "lng": lng_dev, "lnb": lnb_dev, "masks": masks,
            "identw": np.eye(128, dtype=ml_dtypes.bfloat16),
        })

    nc = _get_nc()
    res = run_bass_kernel_spmd(nc, in_maps, core_ids=list(range(NCORES)))
    out = np.empty((S, DM), np.float32)
    for c in range(NCORES):
        out[c::8, :] = res.results[c]["out"].T
    return out[None]


# version 19
# speedup vs baseline: 1.1921x; 1.0383x over previous
import contextlib
import numpy as np
import ml_dtypes
import concourse.bass as bass
import concourse.mybir as mybir
from concourse.bass_utils import run_bass_kernel_spmd

F32 = mybir.dt.float32
BF16 = mybir.dt.bfloat16
AF = mybir.ActivationFunctionType
ALU = mybir.AluOpType

NCORES = 8
DM = 2048
NF = 16
NH = 16
S = 8192
TOK = 1024
NT = 8
II = 8192
EPS = 1e-5
SCALE = 1.0 / float(np.sqrt(128.0))
EXP_BIAS = -2.0


class Sem:
    def __init__(self, nc, es, name):
        self.h = es.enter_context(nc.semaphore(name))
        self.n = 0

    def inc(self, k=1):
        self.n += k
        return self.n


class Plan:
    def __init__(self):
        self.items = []

    def add(self, eng, fn):
        self.items.append((eng, fn))

    def run(self, name, engobj):
        for e, fn in self.items:
            if e == name:
                fn(engobj)


def build_program(debug=False):
    nc = bass.Bass("TRN2", target_bir_lowering=False)

    x_in = nc.dram_tensor("x", [DM, TOK], F32, kind="ExternalInput").ap()
    wq_in = nc.dram_tensor("wq", [128, NF * DM], F32, kind="ExternalInput").ap()
    wk_in = nc.dram_tensor("wk", [128, NF * DM], F32, kind="ExternalInput").ap()
    wv_in = nc.dram_tensor("wv", [128, NF * DM], F32, kind="ExternalInput").ap()
    wo_in = nc.dram_tensor("wo", [128, NF * DM], F32, kind="ExternalInput").ap()
    wg_in = nc.dram_tensor("wg", [128, NF * II], F32, kind="ExternalInput").ap()
    wu_in = nc.dram_tensor("wu", [128, NF * II], F32, kind="ExternalInput").ap()
    wd_in = nc.dram_tensor("wd", [128, 64 * DM], F32, kind="ExternalInput").ap()
    bq_in = nc.dram_tensor("bq", [128, 16], F32, kind="ExternalInput").ap()
    bk_in = nc.dram_tensor("bk", [128, 16], F32, kind="ExternalInput").ap()
    bo_in = nc.dram_tensor("bo", [128, 16], F32, kind="ExternalInput").ap()
    bv_in = nc.dram_tensor("bvr", [1, DM], F32, kind="ExternalInput").ap()
    lng_in = nc.dram_tensor("lng", [128, 16], F32, kind="ExternalInput").ap()
    lnb_in = nc.dram_tensor("lnb", [128, 16], F32, kind="ExternalInput").ap()
    masks_in = nc.dram_tensor("masks", [8, 128, 896], BF16, kind="ExternalInput").ap()
    ident_in = nc.dram_tensor("identw", [128, 128], BF16, kind="ExternalInput").ap()
    out_t = nc.dram_tensor("out", [DM, TOK], F32, kind="ExternalOutput").ap()

    cc_in = nc.dram_tensor("cc_in", [4096, 1024], BF16, kind="Internal")
    cc_out = nc.dram_tensor("cc_out", [NCORES * 4096, 1024], BF16, kind="Internal")
    q_dram = nc.dram_tensor("q_dram", [NH, 128, TOK], BF16, kind="Internal")
    h_dram = nc.dram_tensor("h_dram", [NT, 128, II], F32, kind="Internal")
    hs_fm_dram = nc.dram_tensor("hs_fm", [64, 128, TOK], BF16, kind="Internal")
    wd_bf = nc.dram_tensor("wd_bf", [128, 64 * DM], BF16, kind="Internal")
    if debug:
        dbg_xn = nc.dram_tensor("dbg_xn", [128, NF * TOK], BF16, kind="ExternalOutput").ap()
        dbg_x2 = nc.dram_tensor("dbg_x2", [128, NF * TOK], F32, kind="ExternalOutput").ap()
        dbg_ccin = nc.dram_tensor("dbg_ccin", [4096, 1024], BF16, kind="ExternalOutput").ap()
        dbg_q = nc.dram_tensor("dbg_q", [NH, 128, TOK], BF16, kind="ExternalOutput").ap()
        dbg_h = nc.dram_tensor("dbg_h", [NT, 128, II], F32, kind="ExternalOutput").ap()
        dbg_hs = nc.dram_tensor("dbg_hs", [64, 128, TOK], BF16, kind="ExternalOutput").ap()
        dbg_y = nc.dram_tensor("dbg_y", [128, NF * TOK], F32, kind="ExternalOutput").ap()

    es = contextlib.ExitStack()
    with es:
        stream = es.enter_context(nc.sbuf_tensor("stream", [128, NF * TOK], F32))
        cpool = es.enter_context(nc.sbuf_tensor("cpool", [128, 8], F32))
        bq_sb = es.enter_context(nc.sbuf_tensor("bq_sb", [128, 16], F32))
        bk_sb = es.enter_context(nc.sbuf_tensor("bk_sb", [128, 16], F32))
        bo_sb = es.enter_context(nc.sbuf_tensor("bo_sb", [128, 16], F32))
        lng_sb = es.enter_context(nc.sbuf_tensor("lng_sb", [128, 16], F32))
        lnb_sb = es.enter_context(nc.sbuf_tensor("lnb_sb", [128, 16], F32))
        ones128 = es.enter_context(nc.sbuf_tensor("ones128", [128, 128], BF16))
        ident = es.enter_context(nc.sbuf_tensor("ident", [128, 128], BF16))
        onescol = es.enter_context(nc.sbuf_tensor("onescol", [128, 1], BF16))
        onesrow = es.enter_context(nc.sbuf_tensor("onesrow", [1, 128], F32))

        sx = Sem(nc, es, "sx")
        s_w = [Sem(nc, es, f"s_w{i}") for i in range(2)]
        s_st = [Sem(nc, es, f"s_st{i}") for i in range(2)]
        s_vst = [Sem(nc, es, f"s_vst{i}") for i in range(2)]
        s_qst = [Sem(nc, es, f"s_qst{i}") for i in range(2)]
        s_kv = [Sem(nc, es, f"s_kv{i}") for i in range(2)]
        s_hst = [Sem(nc, es, f"s_hst{i}") for i in range(4)]
        s_tp = [Sem(nc, es, f"s_tp{i}") for i in range(4)]
        s_wd = [Sem(nc, es, f"s_wd{i}") for i in range(3)]
        s_wdc = Sem(nc, es, "s_wdc")
        scast = Sem(nc, es, "scast")
        skvst = Sem(nc, es, "skvst")
        scc = Sem(nc, es, "scc")
        skvf = Sem(nc, es, "skvf")
        sh = Sem(nc, es, "sh")
        shs = Sem(nc, es, "shs")
        sout = Sem(nc, es, "sout")
        spe = Sem(nc, es, "spe")
        sact = Sem(nc, es, "sact")
        sdve = Sem(nc, es, "sdve")

        C_SCALE, C_EBIAS, C_EPS, C_NHALF, C_NONE = 0, 1, 2, 3, 4
        CVALS = {C_SCALE: SCALE, C_EBIAS: EXP_BIAS, C_EPS: EPS,
                 C_NHALF: -0.5, C_NONE: -1.0}

        def cp(i):
            return cpool[:, i:i + 1]

        def emit_layernorm(name, src_is_xb, xb_ap, with_gb, out_fn, pre_wait):
            with contextlib.ExitStack() as ps:
                s1_ps = ps.enter_context(nc.psum_tensor(name + "_s1", [128, TOK], F32))
                s2_ps = ps.enter_context(nc.psum_tensor(name + "_s2", [128, TOK], F32))
                dln_ps = ps.enter_context(nc.psum_tensor(name + "_dum", [32, 1], F32))
                xsq = ps.enter_context(nc.sbuf_tensor(name + "_xsq", [128, 2 * TOK], BF16))
                xbf = None
                if not src_is_xb:
                    xbf = ps.enter_context(nc.sbuf_tensor(name + "_xbf", [128, 2 * TOK], BF16))
                mu = ps.enter_context(nc.sbuf_tensor(name + "_mu", [128, TOK], F32))
                rstd = ps.enter_context(nc.sbuf_tensor(name + "_rstd", [128, TOK], F32))
                m2 = ps.enter_context(nc.sbuf_tensor(name + "_m2", [128, TOK], F32))
                ta = ps.enter_context(nc.sbuf_tensor(name + "_ta", [128, TOK], F32))

                plan = Plan()
                cast_dve = {}
                sq_act = {}
                mm_pe = {}
                for f in range(NF):
                    if xbf is not None:
                        def cast_fn(eng, ff=f):
                            if ff >= 2:
                                eng.wait_ge(spe.h, mm_pe[ff - 2])
                            eng.tensor_scalar(
                                out=xbf[:, (ff % 2) * TOK:(ff % 2 + 1) * TOK],
                                in0=stream[:, ff * TOK:(ff + 1) * TOK],
                                scalar1=1.0, scalar2=None,
                                op0=ALU.mult).then_inc(sdve.h, 1)
                        cast_dve[f] = sdve.inc()
                        plan.add("dve", cast_fn)

                    def sq_fn(eng, ff=f):
                        if xbf is not None:
                            eng.wait_ge(sdve.h, cast_dve[ff])
                        elif ff >= 2:
                            eng.wait_ge(spe.h, mm_pe[ff - 2])
                        src = (xbf[:, (ff % 2) * TOK:(ff % 2 + 1) * TOK]
                               if xbf is not None
                               else xb_ap[:, ff * TOK:(ff + 1) * TOK])
                        eng.activation(out=xsq[:, (ff % 2) * TOK:(ff % 2 + 1) * TOK],
                                       in_=src, func=AF.Square).then_inc(sact.h, 1)
                    sq_act[f] = sact.inc()
                    plan.add("act", sq_fn)

                    def mm_fn(eng, ff=f):
                        eng.wait_ge(sact.h, sq_act[ff])
                        src1 = (xbf[:, (ff % 2) * TOK:(ff % 2 + 1) * TOK]
                                if xbf is not None
                                else xb_ap[:, ff * TOK:(ff + 1) * TOK])
                        src2 = xsq[:, (ff % 2) * TOK:(ff % 2 + 1) * TOK]
                        st = (ff == 0)
                        sp = (ff == NF - 1)
                        eng.matmul(s1_ps[:, 0:512], ones128[:], src1[:, 0:512], start=st, stop=sp)
                        eng.matmul(s1_ps[:, 512:1024], ones128[:], src1[:, 512:1024], start=st, stop=sp)
                        eng.matmul(s2_ps[:, 0:512], ones128[:], src2[:, 0:512], start=st, stop=sp)
                        eng.matmul(s2_ps[:, 512:1024], ones128[:], src2[:, 512:1024],
                                   start=st, stop=sp).then_inc(spe.h, 1)
                        if sp:
                            eng.matmul(dln_ps[:], ones128[:, 0:32], onescol[:],
                                       start=True, stop=True).then_inc(spe.h, 1)
                    mm_pe[f] = spe.inc()
                    if f == NF - 1:
                        spe.inc()
                    plan.add("pe", mm_fn)

                stats_done = mm_pe[NF - 1] + 1

                def mu_fn(eng):
                    eng.wait_ge(spe.h, stats_done)
                    eng.tensor_scalar(out=mu[:], in0=s1_ps[:], scalar1=1.0 / 2048.0,
                                      scalar2=None, op0=ALU.mult).then_inc(sdve.h, 1)
                mu_dve = sdve.inc()
                plan.add("dve", mu_fn)

                def musq_fn(eng):
                    eng.wait_ge(sdve.h, mu_dve)
                    eng.activation(out=ta[:], in_=mu[:], func=AF.Square).then_inc(sact.h, 1)
                musq_act = sact.inc()
                plan.add("act", musq_fn)

                def var_fn(eng):
                    eng.wait_ge(sact.h, musq_act)
                    eng.scalar_tensor_tensor(
                        out=rstd[:], in0=s2_ps[:], scalar=1.0 / 2048.0, in1=ta[:],
                        op0=ALU.mult, op1=ALU.subtract).then_inc(sdve.h, 1)
                var_dve = sdve.inc()
                plan.add("dve", var_fn)

                rstd_act = sact.inc(2)

                def rstd_fn(eng):
                    eng.wait_ge(sdve.h, var_dve)
                    eng.activation(out=rstd[:], in_=rstd[:], func=AF.Ln,
                                   bias=cp(C_EPS)).then_inc(sact.h, 1)
                    eng.activation(out=rstd[:], in_=rstd[:], func=AF.Exp,
                                   scale=cp(C_NHALF)).then_inc(sact.h, 1)
                plan.add("act", rstd_fn)

                def m2_fn(eng):
                    eng.wait_ge(sact.h, rstd_act)
                    eng.tensor_mul(out=m2[:], in0=mu[:], in1=rstd[:]).then_inc(sdve.h, 1)
                m2_dve = sdve.inc()
                plan.add("dve", m2_fn)

                apply_dve = {}
                for f in range(NF):
                    def ap_fn(eng, ff=f):
                        if ff == 0:
                            eng.wait_ge(sdve.h, m2_dve)
                        eng.tensor_mul(out=ta[:], in0=stream[:, ff * TOK:(ff + 1) * TOK],
                                       in1=rstd[:])
                        if with_gb:
                            eng.tensor_sub(out=ta[:], in0=ta[:], in1=m2[:])
                            eng.tensor_scalar(out=out_fn(ff), in0=ta[:],
                                              scalar1=lng_sb[:, ff:ff + 1],
                                              scalar2=lnb_sb[:, ff:ff + 1],
                                              op0=ALU.mult, op1=ALU.add).then_inc(sdve.h, 1)
                        else:
                            eng.tensor_sub(out=out_fn(ff), in0=ta[:],
                                           in1=m2[:]).then_inc(sdve.h, 1)
                    apply_dve[f] = sdve.inc()
                    plan.add("dve", ap_fn)

                with nc.Block(name) as blk:
                    @blk.tensor
                    def _(eng):
                        for sem, val in pre_wait.get("pe", []):
                            eng.wait_ge(sem.h, val)
                        plan.run("pe", eng)

                    @blk.scalar
                    def _(eng):
                        for sem, val in pre_wait.get("act", []):
                            eng.wait_ge(sem.h, val)
                        plan.run("act", eng)

                    @blk.vector
                    def _(eng):
                        for sem, val in pre_wait.get("dve", []):
                            eng.wait_ge(sem.h, val)
                        plan.run("dve", eng)
                return apply_dve

        with nc.sbuf_tensor("xb", [128, NF * TOK], BF16) as xb:
            with nc.Block("p0") as blk:
                @blk.sync
                def _(eng):
                    src = x_in.rearrange("(f p) t -> p f t", p=128)
                    dst = stream[:].rearrange("p (f t) -> p f t", f=NF)
                    eng.dma_start(out=dst[:, 0:8], in_=src[:, 0:8]).then_inc(sx.h, 16)
                    eng.dma_start(out=dst[:, 8:16], in_=src[:, 8:16]).then_inc(sx.h, 16)
                    for ap_in, sb in ((bq_in, bq_sb), (bk_in, bk_sb), (bo_in, bo_sb),
                                      (lng_in, lng_sb), (lnb_in, lnb_sb), (ident_in, ident)):
                        eng.dma_start(out=sb[:], in_=ap_in).then_inc(sx.h, 16)

                @blk.gpsimd
                def _(eng):
                    src = x_in.rearrange("(f p) t -> p f t", p=128)
                    dst = xb[:].rearrange("p (f t) -> p f t", f=NF)
                    eng.dma_start(out=dst[:, 0:8], in_=src[:, 0:8]).then_inc(scast.h, 16)
                    eng.dma_start(out=dst[:, 8:16], in_=src[:, 8:16]).then_inc(scast.h, 16)
                    eng.memset(ones128[:], 1.0)
                    eng.memset(onescol[:], 1.0)
                    eng.memset(onesrow[:], 1.0)
                    for i, v in CVALS.items():
                        eng.memset(cpool[:, i:i + 1], v)
            sx.n += 8 * 16
            scast.n += 32
            p0_sx = sx.n
            p0_scast = scast.n

            emit_layernorm("p1", True, xb,
                           True, lambda f: xb[:, f * TOK:(f + 1) * TOK],
                           {"pe": [(scast, p0_scast)],
                            "act": [(scast, p0_scast)],
                            "dve": [(sx, p0_sx)]})

            with contextlib.ExitStack() as ps:
                bv_bc = ps.enter_context(nc.sbuf_tensor("bv_bc", [128, DM], F32))
                bv_row = ps.enter_context(nc.sbuf_tensor("bv_row", [1, DM], F32))
                wbuf = [ps.enter_context(nc.sbuf_tensor(f"w2_{i}", [128, NF * 512], BF16))
                        for i in range(2)]
                kstage = [ps.enter_context(nc.sbuf_tensor(f"kst_{i}", [128, TOK], BF16))
                          for i in range(2)]
                vstage = [ps.enter_context(nc.sbuf_tensor(f"vst_{i}", [128, 512], BF16))
                          for i in range(2)]
                pp = [ps.enter_context(nc.psum_tensor(f"p2_{i}", [128, 512], F32))
                      for i in range(2)]
                bv_ps = ps.enter_context(nc.psum_tensor("p2_bv", [128, DM], F32))
                dum2_ps = ps.enter_context(nc.psum_tensor("p2_dum", [32, 1], F32))

                def p2dummy(eng):
                    eng.matmul(dum2_ps[:], ones128[:, 0:32], onescol[:],
                               start=True, stop=True).then_inc(spe.h, 1)

                plan = Plan()
                cc_in_ap = cc_in.ap()
                cc_in_V = cc_in_ap[2048:4096, :].rearrange("(t two) n -> t (two n)", two=2)

                def bvload_fn(eng):
                    eng.dma_start(out=bv_row[:], in_=bv_in).then_inc(sx.h, 16)
                bv_sx = sx.inc(16)
                plan.add("sync", bvload_fn)

                def bvbc_fn(eng):
                    eng.wait_ge(sx.h, bv_sx)
                    for j in range(3):
                        eng.matmul(bv_ps[:, j * 512:(j + 1) * 512], onesrow[:],
                                   bv_row[:, j * 512:(j + 1) * 512], start=True, stop=True)
                    eng.matmul(bv_ps[:, 1536:2048], onesrow[:],
                               bv_row[:, 1536:2048], start=True, stop=True).then_inc(spe.h, 1)
                    p2dummy(eng)
                spe.inc()
                bv_pe = spe.inc()
                plan.add("pe", bvbc_fn)

                def bvcp_fn(eng):
                    eng.wait_ge(spe.h, bv_pe)
                    eng.tensor_copy(out=bv_bc[:], in_=bv_ps[:]).then_inc(sdve.h, 1)
                bv_dve = sdve.inc()
                plan.add("dve", bvcp_fn)

                kmm_pe, kevac_act, kdma = {}, {}, {}
                vmm_pe, vevac_dve, vdma = {}, {}, {}
                qmm_pe, qevac_act, qdma = {}, {}, {}
                wk_fetch, wv_fetch, wq_fetch = {}, {}, {}

                for g in range(16):
                    def wkf(eng, g=g):
                        if g >= 2:
                            eng.wait_ge(spe.h, kmm_pe[(g - 2, 1)])
                        eng.dma_start(out=wbuf[g % 2][:, 0:2048],
                                      in_=wk_in[:, g * 2048:(g + 1) * 2048]).then_inc(s_w[g % 2].h, 16)
                    plan.add("gpsimd", wkf)
                    wk_fetch[g] = s_w[g % 2].inc(16)

                    for half in range(2):
                        def kmm(eng, g=g, half=half):
                            eng.wait_ge(s_w[g % 2].h, wk_fetch[g])
                            idx = g * 2 + half
                            if idx >= 2:
                                pg, ph = divmod(idx - 2, 2)
                                eng.wait_ge(sact.h, kevac_act[(pg, ph)])
                            ins = None
                            for f in range(NF):
                                ins = eng.matmul(
                                    pp[half][:],
                                    wbuf[g % 2][:, f * 128:(f + 1) * 128],
                                    xb[:, f * TOK + half * 512: f * TOK + half * 512 + 512],
                                    start=(f == 0), stop=(f == NF - 1))
                            ins.then_inc(spe.h, 1)
                            p2dummy(eng)
                        spe.inc()
                        kmm_pe[(g, half)] = spe.inc()
                        plan.add("pe", kmm)

                        def kevac(eng, g=g, half=half):
                            eng.wait_ge(spe.h, kmm_pe[(g, half)])
                            if g >= 2 and half == 0:
                                eng.wait_ge(s_st[g % 2].h, kdma[g - 2])
                            eng.activation(out=kstage[g % 2][:, half * 512:half * 512 + 512],
                                           in_=pp[half][:], func=AF.Identity,
                                           bias=bk_sb[:, g:g + 1]).then_inc(sact.h, 1)
                        kevac_act[(g, half)] = sact.inc()
                        plan.add("act", kevac)

                    def kdma_fn(eng, g=g):
                        eng.wait_ge(sact.h, kevac_act[(g, 1)])
                        eng.dma_start(out=cc_in_ap[g * 128:(g + 1) * 128, :],
                                      in_=kstage[g % 2][:]).then_inc(s_st[g % 2].h, 16)
                    plan.add("sync", kdma_fn)
                    kdma[g] = s_st[g % 2].inc(16)

                wv_r = wv_in.rearrange("p (f n) -> p f n", f=NF)
                for j in range(4):
                    def wvf(eng, j=j):
                        if j >= 2:
                            eng.wait_ge(spe.h, vmm_pe[(j - 2, NT - 1)])
                        src = wv_r[:, :, j * 512:(j + 1) * 512]
                        dst = wbuf[j % 2][:].rearrange("p (f n) -> p f n", f=NF)
                        eng.dma_start(out=dst[:, 0:8], in_=src[:, 0:8]).then_inc(s_w[j % 2].h, 16)
                        eng.dma_start(out=dst[:, 8:16], in_=src[:, 8:16]).then_inc(s_w[j % 2].h, 16)
                    plan.add("gpsimd", wvf)
                    wv_fetch[j] = s_w[j % 2].inc(32)

                    for t in range(NT):
                        def vmm(eng, j=j, t=t):
                            eng.wait_ge(s_w[j % 2].h, wv_fetch[j])
                            idx = j * NT + t
                            if idx < 2:
                                eng.wait_ge(sact.h, kevac_act[(15, 1)])
                            else:
                                pj, pt = divmod(idx - 2, NT)
                                eng.wait_ge(sdve.h, vevac_dve[(pj, pt)])
                            ins = None
                            for f in range(NF):
                                ins = eng.matmul(
                                    pp[t % 2][:],
                                    xb[:, f * TOK + t * 128: f * TOK + t * 128 + 128],
                                    wbuf[j % 2][:, f * 512:(f + 1) * 512],
                                    start=(f == 0), stop=(f == NF - 1))
                            ins.then_inc(spe.h, 1)
                            p2dummy(eng)
                        spe.inc()
                        vmm_pe[(j, t)] = spe.inc()
                        plan.add("pe", vmm)

                        def vevac(eng, j=j, t=t):
                            eng.wait_ge(spe.h, vmm_pe[(j, t)])
                            idx = j * NT + t
                            if idx == 0:
                                eng.wait_ge(sdve.h, bv_dve)
                            if idx >= 2:
                                pj, pt = divmod(idx - 2, NT)
                                eng.wait_ge(s_vst[t % 2].h, vdma[(pj, pt)])
                            eng.tensor_add(out=vstage[t % 2][:], in0=pp[t % 2][:],
                                           in1=bv_bc[:, j * 512:(j + 1) * 512]).then_inc(sdve.h, 1)
                        vevac_dve[(j, t)] = sdve.inc()
                        plan.add("dve", vevac)

                        def vdma_fn(eng, j=j, t=t):
                            eng.wait_ge(sdve.h, vevac_dve[(j, t)])
                            eng.dma_start(
                                out=cc_in_V[t * 128:(t + 1) * 128, j * 512:(j + 1) * 512],
                                in_=vstage[t % 2][:]).then_inc(s_vst[t % 2].h, 16)
                        plan.add("sync", vdma_fn)
                        vdma[(j, t)] = s_vst[t % 2].inc(16)

                staged_vals = [(s_st[0], s_st[0].n), (s_st[1], s_st[1].n),
                               (s_vst[0], s_vst[0].n), (s_vst[1], s_vst[1].n)]

                def coll_fn(eng):
                    for sem, val in staged_vals:
                        eng.wait_ge(sem.h, val)
                    eng.collective_compute(
                        "AllGather", ALU.bypass,
                        replica_groups=[list(range(NCORES))],
                        ins=[cc_in.ap().opt()],
                        outs=[cc_out.ap().opt()],
                    ).then_inc(scc.h, 1)
                plan.add("gpsimd", coll_fn)
                scc.inc()

                for g in range(16):
                    def wqf(eng, g=g):
                        if g >= 2:
                            eng.wait_ge(spe.h, qmm_pe[(g - 2, 1)])
                        else:
                            eng.wait_ge(spe.h, vmm_pe[(3, NT - 1)])
                        eng.dma_start(out=wbuf[g % 2][:, 0:2048],
                                      in_=wq_in[:, g * 2048:(g + 1) * 2048]).then_inc(s_w[g % 2].h, 16)
                    plan.add("gpsimd", wqf)
                    wq_fetch[g] = s_w[g % 2].inc(16)

                    for half in range(2):
                        def qmm(eng, g=g, half=half):
                            eng.wait_ge(s_w[g % 2].h, wq_fetch[g])
                            idx = g * 2 + half
                            if idx < 2:
                                eng.wait_ge(sdve.h, vevac_dve[(3, NT - 1)])
                            else:
                                pg, ph = divmod(idx - 2, 2)
                                eng.wait_ge(sact.h, qevac_act[(pg, ph)])
                            ins = None
                            for f in range(NF):
                                ins = eng.matmul(
                                    pp[half][:],
                                    wbuf[g % 2][:, f * 128:(f + 1) * 128],
                                    xb[:, f * TOK + half * 512: f * TOK + half * 512 + 512],
                                    start=(f == 0), stop=(f == NF - 1))
                            ins.then_inc(spe.h, 1)
                            p2dummy(eng)
                        spe.inc()
                        qmm_pe[(g, half)] = spe.inc()
                        plan.add("pe", qmm)

                        def qevac(eng, g=g, half=half):
                            eng.wait_ge(spe.h, qmm_pe[(g, half)])
                            if half == 0:
                                if g < 2:
                                    eng.wait_ge(s_st[g % 2].h, kdma[14 + g])
                                else:
                                    eng.wait_ge(s_qst[g % 2].h, qdma[g - 2])
                            eng.activation(
                                out=kstage[g % 2][:, half * 512:half * 512 + 512],
                                in_=pp[half][:], func=AF.Identity,
                                bias=bq_sb[:, g:g + 1]).then_inc(sact.h, 1)
                        qevac_act[(g, half)] = sact.inc()
                        plan.add("act", qevac)

                    def qdma_fn(eng, g=g):
                        eng.wait_ge(sact.h, qevac_act[(g, 1)])
                        eng.dma_start(out=q_dram.ap()[g],
                                      in_=kstage[g % 2][:]).then_inc(s_qst[g % 2].h, 16)
                    plan.add("sync", qdma_fn)
                    qdma[g] = s_qst[g % 2].inc(16)

                with nc.Block("p2") as blk:
                    @blk.tensor
                    def _(eng):
                        plan.run("pe", eng)

                    @blk.scalar
                    def _(eng):
                        plan.run("act", eng)

                    @blk.vector
                    def _(eng):
                        plan.run("dve", eng)

                    @blk.sync
                    def _(eng):
                        plan.run("sync", eng)

                    @blk.gpsimd
                    def _(eng):
                        plan.run("gpsimd", eng)
            p2_qst = (s_qst[0].n, s_qst[1].n)

            if debug:
                with nc.Block("dbgxn") as blk:
                    @blk.sync
                    def _(eng):
                        eng.dma_start(out=dbg_xn, in_=xb[:]).then_inc(sout.h, 16)
                        sout.n += 16
                        eng.wait_ge(sout.h, sout.n)

        with contextlib.ExitStack() as ps:
            kb_sb = [ps.enter_context(nc.sbuf_tensor(f"kb_{i}", [128, 8192], BF16))
                     for i in range(2)]
            vb_sb = [ps.enter_context(nc.sbuf_tensor(f"vb_{i}", [128, 8192], BF16))
                     for i in range(2)]
            qcur = [ps.enter_context(nc.sbuf_tensor(f"qc_{i}", [128, 512], BF16))
                    for i in range(2)]
            abuf = [ps.enter_context(nc.sbuf_tensor(f"ab_{i}", [128, 512], BF16))
                    for i in range(4)]
            av_sb = ps.enter_context(nc.sbuf_tensor("av_sb", [128, NH * 512], BF16))
            mask_sb = ps.enter_context(nc.sbuf_tensor("mask_sb", [128, 8 * 896], BF16))
            rdbc_sb = ps.enter_context(nc.sbuf_tensor("rdbc_sb", [128, 512], F32))
            wob = [ps.enter_context(nc.sbuf_tensor(f"wob_{i}", [128, 2048], BF16))
                   for i in range(2)]

            s_ps = [ps.enter_context(nc.psum_tensor(f"s_ps{i}", [128, 512], F32))
                    for i in range(3)]
            av_ps = [ps.enter_context(nc.psum_tensor(f"av_ps{i}", [128, 512], F32))
                     for i in range(2)]
            den_ps = [ps.enter_context(nc.psum_tensor(f"den_ps{i}", [128, 512], F32))
                      for i in range(2)]
            dum3_ps = ps.enter_context(nc.psum_tensor("dum3_ps", [32, 1], F32))

            plan = Plan()

            def mload(eng):
                src = masks_in.transpose([1, 0, 2])
                dst = mask_sb[:].rearrange("p (r n) -> p r n", r=8)
                eng.dma_start(out=dst[:, 0:4], in_=src[:, 0:4]).then_inc(sx.h, 16)
                eng.dma_start(out=dst[:, 4:8], in_=src[:, 4:8]).then_inc(sx.h, 16)
            plan.add("sync", mload)
            mask_sx = sx.inc(32)

            ccr = cc_out.ap().rearrange("(r x) n -> r x n", x=4096)
            vsrc_all = ccr[:, 2048:4096, :].rearrange("r (t two) n -> r t (two n)", two=2)

            exp_act, mask_dve = {}, {}
            headdone_pe, headfinal_pe, rden_act = {}, {}, {}
            avevac_dve = {}
            mask_info = {}

            def emit_mask(mgt):
                if mgt not in mask_info:
                    return
                mr, mkbr, masl = mask_info[mgt]

                def maskf(eng, mgt=mgt, mr=mr, mkbr=mkbr, masl=masl):
                    if mgt == 0:
                        eng.wait_ge(sx.h, mask_sx)
                    eng.wait_ge(sact.h, exp_act[mgt])
                    moff = mr * 896 + (384 - 128 * mkbr)
                    eng.tensor_mul(out=abuf[masl][:], in0=abuf[masl][:],
                                   in1=mask_sb[:, moff:moff + 512]).then_inc(sdve.h, 1)
                mask_dve[mgt] = sdve.inc()
                plan.add("dve", maskf)
            kvf_done, smm_pe = {}, {}
            wo_pe, woevac_dve, wof_cast, wo_cover = {}, {}, {}, {}

            def emit_kv_fetch(qb, h, hh):
                Tq = 512 * (qb + 1)
                ktn = 4 * (qb + 1)
                slot = hh % 2

                def qf(eng, qb=qb, h=h, slot=slot, hh=hh):
                    if hh >= 2:
                        eng.wait_ge(spe.h, headfinal_pe[hh - 2])
                    eng.dma_start(out=qcur[slot][:],
                                  in_=q_dram.ap()[h, :, qb * 512:(qb + 1) * 512]).then_inc(s_kv[slot].h, 16)
                plan.add("sync", qf)
                s_kv[slot].inc(16)

                def kf(eng, h=h, slot=slot, Tq=Tq):
                    ksrc = ccr[:, h * 128:(h + 1) * 128, 0:Tq].transpose([1, 0, 2])
                    kdst = kb_sb[slot][:, 0:8 * Tq].rearrange("p (r t) -> p r t", r=8)
                    eng.dma_start(out=kdst[:, 0:4], in_=ksrc[:, 0:4]).then_inc(s_kv[slot].h, 16)
                    eng.dma_start(out=kdst[:, 4:8], in_=ksrc[:, 4:8]).then_inc(s_kv[slot].h, 16)
                plan.add("sync", kf)
                s_kv[slot].inc(32)

                def vf(eng, h=h, slot=slot, Tq=Tq, ktn=ktn):
                    vsrc = vsrc_all[:, 0:Tq, h * 128:(h + 1) * 128]
                    vsrc2 = vsrc.rearrange("r (kt p) d -> p r kt d", p=128)
                    vdst = vb_sb[slot][:, 0:8 * Tq].rearrange("p (r kt d) -> p r kt d",
                                                              r=8, kt=ktn)
                    for r in range(8):
                        if ktn <= 4:
                            eng.dma_start(out=vdst[:, r:r + 1],
                                          in_=vsrc2[:, r:r + 1]).then_inc(s_kv[slot].h, 16)
                        else:
                            eng.dma_start(out=vdst[:, r:r + 1, 0:4],
                                          in_=vsrc2[:, r:r + 1, 0:4]).then_inc(s_kv[slot].h, 16)
                            eng.dma_start(out=vdst[:, r:r + 1, 4:ktn],
                                          in_=vsrc2[:, r:r + 1, 4:ktn]).then_inc(s_kv[slot].h, 16)
                plan.add("sync", vf)
                s_kv[slot].inc(16 * (8 if ktn <= 4 else 16))
                return s_kv[slot].n

            gt = 0
            avmm_den = {}

            for qb in range(2):
                ktn = 4 * (qb + 1)
                ntile = 8 * ktn
                for h in range(NH):
                    hh = qb * NH + h
                    kvf_done[hh] = emit_kv_fetch(qb, h, hh)

                    def mk_smm(hh, r, kt, ti, gt, sl, ktn):
                        def smm(eng):
                            if ti == 0:
                                eng.wait_ge(s_kv[hh % 2].h, kvf_done[hh])
                                if hh >= 2:
                                    eng.wait_ge(sdve.h, avevac_dve[hh - 2])
                                if hh >= 2:
                                    eng.wait_ge(sact.h, rden_act[hh - 2])
                            if gt >= 2:
                                eng.wait_ge(sact.h, exp_act[gt - 2])
                            off = (r * ktn + kt) * 128
                            eng.matmul(s_ps[sl][:], kb_sb[hh % 2][:, off:off + 128],
                                       qcur[hh % 2][:],
                                       start=True, stop=True).then_inc(spe.h, 1)
                            eng.matmul(dum3_ps[:], ones128[:, 0:32], onescol[:],
                                       start=True, stop=True).then_inc(spe.h, 1)
                        return smm

                    def mk_avmm(hh, ti, gt, asl, ntile, band, r, kt, ktn):
                        def avmm(eng):
                            if band:
                                eng.wait_ge(sdve.h, mask_dve[gt])
                            else:
                                eng.wait_ge(sact.h, exp_act[gt])
                            off = (r * ktn + kt) * 128
                            eng.matmul(av_ps[hh % 2][:], vb_sb[hh % 2][:, off:off + 128],
                                       abuf[asl][:], start=(ti == 0),
                                       stop=(ti == ntile - 1)).then_inc(spe.h, 1)
                            eng.matmul(den_ps[hh % 2][:], ones128[:], abuf[asl][:],
                                       start=(ti == 0),
                                       stop=(ti == ntile - 1)).then_inc(spe.h, 1)
                        return avmm

                    def add_avmm(args):
                        hh_, ti_, gt_, asl_, ntile_, band_, r_, kt_, ktn_ = args
                        spe.inc()
                        avmm_den[gt_] = spe.inc()
                        plan.add("pe", mk_avmm(hh_, ti_, gt_, asl_, ntile_, band_,
                                               r_, kt_, ktn_))

                    def mk_exp(pgt, pasl, ppsl, cover):
                        def expf(eng):
                            eng.wait_ge(spe.h, cover)
                            eng.activation(out=abuf[pasl][:], in_=s_ps[ppsl][:],
                                           func=AF.Exp, bias=cp(C_EBIAS),
                                           scale=cp(C_SCALE)).then_inc(sact.h, 1)
                        return expf

                    pend_av = []
                    for r in range(8):
                        for kt in range(ktn):
                            ti = r * ktn + kt
                            band = (kt >= 4 * qb)
                            kbr = kt - 4 * qb
                            sl = gt % 3
                            asl = gt % 4
                            if band:
                                mask_info[gt] = (r, kbr, asl)

                            spe.inc()
                            smm_pe[gt] = spe.inc()
                            plan.add("pe", mk_smm(hh, r, kt, ti, gt, sl, ktn))

                            exp_act[gt] = sact.inc()
                            plan.add("act", mk_exp(gt, asl, sl, smm_pe[gt]))
                            emit_mask(gt)

                            pend_av.append((hh, ti, gt, asl, ntile, band, r, kt, ktn))
                            if len(pend_av) > 2:
                                add_avmm(pend_av.pop(0))
                            gt += 1

                    for a in pend_av:
                        add_avmm(a)

                    def headdummy(eng):
                        eng.matmul(dum3_ps[:], ones128[:, 0:32], onescol[:],
                                   start=True, stop=True).then_inc(spe.h, 1)
                    headfinal_pe[hh] = spe.inc()
                    plan.add("pe", headdummy)

                    rden_act[hh] = sact.inc(2)

                    def rdenf(eng, hh=hh):
                        eng.wait_ge(spe.h, headfinal_pe[hh])
                        if hh >= 1:
                            eng.wait_ge(sdve.h, avevac_dve[hh - 1])
                        eng.activation(out=rdbc_sb[:], in_=den_ps[hh % 2][:],
                                       func=AF.Ln).then_inc(sact.h, 1)
                        eng.activation(out=rdbc_sb[:], in_=rdbc_sb[:], func=AF.Exp,
                                       scale=cp(C_NONE)).then_inc(sact.h, 1)
                    plan.add("act", rdenf)

                    def avevac(eng, hh=hh, h=h, qb=qb):
                        eng.wait_ge(sact.h, rden_act[hh])
                        if qb == 1 and h == 0:
                            eng.wait_ge(spe.h, wo_pe[(0, 15)])
                        eng.tensor_mul(out=av_sb[:, h * 512:(h + 1) * 512],
                                       in0=av_ps[hh % 2][:],
                                       in1=rdbc_sb[:]).then_inc(sdve.h, 1)
                    avevac_dve[hh] = sdve.inc()
                    plan.add("dve", avevac)

                for g in range(16):
                    def wof(eng, qb=qb, g=g):
                        idx = qb * 16 + g
                        if idx >= 2:
                            pqb, pg = divmod(idx - 2, 16)
                            eng.wait_ge(spe.h, wo_pe[(pqb, pg)])
                        eng.dma_start(out=wob[g % 2][:],
                                      in_=wo_in[:, g * 2048:(g + 1) * 2048]).then_inc(s_w[g % 2].h, 16)
                    plan.add("gpsimd", wof)
                    wof_cast[(qb, g)] = s_w[g % 2].inc(16)

                    def womm(eng, qb=qb, g=g):
                        eng.wait_ge(s_w[g % 2].h, wof_cast[(qb, g)])
                        if g == 0:
                            eng.wait_ge(sdve.h, avevac_dve[qb * NH + NH - 1])
                        if g >= 2:
                            eng.wait_ge(sdve.h, woevac_dve[(qb, g - 2)])
                        ins = None
                        for f in range(NF):
                            ins = eng.matmul(s_ps[g % 2][:],
                                             wob[g % 2][:, f * 128:(f + 1) * 128],
                                             av_sb[:, f * 512:(f + 1) * 512],
                                             start=(f == 0), stop=(f == NF - 1))
                        ins.then_inc(spe.h, 1)
                        eng.matmul(dum3_ps[:], ones128[:, 0:32], onescol[:],
                                   start=True, stop=True).then_inc(spe.h, 1)
                    wo_pe[(qb, g)] = spe.inc()
                    wo_cover[(qb, g)] = spe.inc()
                    plan.add("pe", womm)

                    def woevac(eng, qb=qb, g=g):
                        eng.wait_ge(spe.h, wo_cover[(qb, g)])
                        sl_ = stream[:, g * TOK + qb * 512: g * TOK + qb * 512 + 512]
                        eng.scalar_tensor_tensor(out=sl_, in0=s_ps[g % 2][:],
                                                 scalar=bo_sb[:, g:g + 1], in1=sl_,
                                                 op0=ALU.add, op1=ALU.add).then_inc(sdve.h, 1)
                    woevac_dve[(qb, g)] = sdve.inc()
                    plan.add("dve", woevac)

            with nc.Block("p3") as blk:
                @blk.tensor
                def _(eng):
                    eng.wait_ge(scc.h, 1)
                    plan.run("pe", eng)

                @blk.scalar
                def _(eng):
                    plan.run("act", eng)

                @blk.vector
                def _(eng):
                    plan.run("dve", eng)

                @blk.sync
                def _(eng):
                    eng.wait_ge(scc.h, 1)
                    eng.wait_ge(s_qst[0].h, p2_qst[0])
                    eng.wait_ge(s_qst[1].h, p2_qst[1])
                    plan.run("sync", eng)

                @blk.gpsimd
                def _(eng):
                    plan.run("gpsimd", eng)

        if debug:
            with nc.Block("dbgx2") as blk:
                @blk.sync
                def _(eng):
                    eng.dma_start(out=dbg_x2, in_=stream[:]).then_inc(sout.h, 16)
                    sout.n += 16
                    eng.wait_ge(sout.h, sout.n)

        with nc.sbuf_tensor("xn2", [128, NF * TOK], BF16) as xn2:
            emit_layernorm("p4", False, None, False,
                           lambda f: xn2[:, f * TOK:(f + 1) * TOK], {})

            with contextlib.ExitStack() as ps:
                wgb = [ps.enter_context(nc.sbuf_tensor(f"wgb_{i}", [128, NF * 512], BF16))
                       for i in range(2)]
                wub = [ps.enter_context(nc.sbuf_tensor(f"wub_{i}", [128, NF * 512], BF16))
                       for i in range(2)]
                sgb = [ps.enter_context(nc.sbuf_tensor(f"sgb_{i}", [128, 512], F32))
                       for i in range(2)]
                hst = [ps.enter_context(nc.sbuf_tensor(f"hst_{i}", [128, 512], F32))
                       for i in range(4)]
                g_ps = [ps.enter_context(nc.psum_tensor(f"g_ps{i}", [128, 512], F32))
                        for i in range(2)]
                u_ps = [ps.enter_context(nc.psum_tensor(f"u_ps{i}", [128, 512], F32))
                        for i in range(2)]
                dum5_ps = ps.enter_context(nc.psum_tensor("p5_dum", [32, 1], F32))

                def p5dummy(eng):
                    eng.matmul(dum5_ps[:], ones128[:, 0:32], onescol[:],
                               start=True, stop=True).then_inc(spe.h, 1)

                plan = Plan()
                wgu_fetch, gmm_pe, umm_pe = {}, {}, {}
                silu_act, ht_dve, hw_sync = {}, {}, {}

                wg_r = wg_in.rearrange("p (f n) -> p f n", f=NF)
                wu_r = wu_in.rearrange("p (f n) -> p f n", f=NF)

                for js in range(16):
                    def wguf(eng, js=js):
                        if js >= 2:
                            eng.wait_ge(spe.h, umm_pe[(js - 2, NT - 1)])
                        srcg = wg_r[:, :, js * 512:(js + 1) * 512]
                        srcu = wu_r[:, :, js * 512:(js + 1) * 512]
                        dstg = wgb[js % 2][:].rearrange("p (f n) -> p f n", f=NF)
                        dstu = wub[js % 2][:].rearrange("p (f n) -> p f n", f=NF)
                        eng.dma_start(out=dstg[:, 0:8], in_=srcg[:, 0:8]).then_inc(s_w[js % 2].h, 16)
                        eng.dma_start(out=dstg[:, 8:16], in_=srcg[:, 8:16]).then_inc(s_w[js % 2].h, 16)
                        eng.dma_start(out=dstu[:, 0:8], in_=srcu[:, 0:8]).then_inc(s_w[js % 2].h, 16)
                        eng.dma_start(out=dstu[:, 8:16], in_=srcu[:, 8:16]).then_inc(s_w[js % 2].h, 16)
                    plan.add("gpsimd", wguf)
                    wgu_fetch[js] = s_w[js % 2].inc(64)

                    for t in range(NT):
                        idx = js * NT + t

                        def gmm(eng, js=js, t=t, idx=idx):
                            eng.wait_ge(s_w[js % 2].h, wgu_fetch[js])
                            if idx >= 2:
                                pj, pt = divmod(idx - 2, NT)
                                eng.wait_ge(sact.h, silu_act[(pj, pt)])
                            ins = None
                            for f in range(NF):
                                ins = eng.matmul(
                                    g_ps[t % 2][:],
                                    xn2[:, f * TOK + t * 128: f * TOK + t * 128 + 128],
                                    wgb[js % 2][:, f * 512:(f + 1) * 512],
                                    start=(f == 0), stop=(f == NF - 1))
                            ins.then_inc(spe.h, 1)
                            p5dummy(eng)
                        spe.inc()
                        gmm_pe[(js, t)] = spe.inc()
                        plan.add("pe", gmm)

                        def umm(eng, js=js, t=t, idx=idx):
                            if idx >= 2:
                                pj, pt = divmod(idx - 2, NT)
                                eng.wait_ge(sdve.h, ht_dve[(pj, pt)])
                            ins = None
                            for f in range(NF):
                                ins = eng.matmul(
                                    u_ps[t % 2][:],
                                    xn2[:, f * TOK + t * 128: f * TOK + t * 128 + 128],
                                    wub[js % 2][:, f * 512:(f + 1) * 512],
                                    start=(f == 0), stop=(f == NF - 1))
                            ins.then_inc(spe.h, 1)
                            p5dummy(eng)
                        spe.inc()
                        umm_pe[(js, t)] = spe.inc()
                        plan.add("pe", umm)

                        def siluf(eng, js=js, t=t, idx=idx):
                            eng.wait_ge(spe.h, gmm_pe[(js, t)])
                            if idx >= 2:
                                pj, pt = divmod(idx - 2, NT)
                                eng.wait_ge(sdve.h, ht_dve[(pj, pt)])
                            eng.activation(out=sgb[t % 2][:], in_=g_ps[t % 2][:],
                                           func=AF.Silu).then_inc(sact.h, 1)
                        silu_act[(js, t)] = sact.inc()
                        plan.add("act", siluf)

                        def htf(eng, js=js, t=t, idx=idx):
                            eng.wait_ge(spe.h, umm_pe[(js, t)])
                            eng.wait_ge(sact.h, silu_act[(js, t)])
                            if idx >= 4:
                                pj, pt = divmod(idx - 4, NT)
                                eng.wait_ge(s_hst[idx % 4].h, hw_sync[(pj, pt)])
                            eng.tensor_mul(out=hst[idx % 4][:], in0=sgb[t % 2][:],
                                           in1=u_ps[t % 2][:]).then_inc(sdve.h, 1)
                        ht_dve[(js, t)] = sdve.inc()
                        plan.add("dve", htf)

                        def hwf(eng, js=js, t=t, idx=idx):
                            eng.wait_ge(sdve.h, ht_dve[(js, t)])
                            eng.dma_start(out=h_dram.ap()[t, :, js * 512:(js + 1) * 512],
                                          in_=hst[idx % 4][:]).then_inc(s_hst[idx % 4].h, 16)
                        plan.add("sync", hwf)
                        hw_sync[(js, t)] = s_hst[idx % 4].inc(16)

                with nc.Block("p5a") as blk:
                    @blk.tensor
                    def _(eng):
                        plan.run("pe", eng)

                    @blk.gpsimd
                    def _(eng):
                        for i in range(8):
                            eng.dma_start(out=wd_bf.ap()[:, i * 16384:(i + 1) * 16384],
                                          in_=wd_in[:, i * 16384:(i + 1) * 16384]).then_inc(s_wdc.h, 16)
                        s_wdc.n += 8 * 16
                        plan.run("gpsimd", eng)

                    @blk.scalar
                    def _(eng):
                        plan.run("act", eng)

                    @blk.vector
                    def _(eng):
                        plan.run("dve", eng)

                    @blk.sync
                    def _(eng):
                        plan.run("sync", eng)
                p5a_hst = [s.n for s in s_hst]

        with contextlib.ExitStack() as ps:
            htile2 = [ps.enter_context(nc.sbuf_tensor(f"htile{i}", [128, II], F32))
                      for i in range(2)]
            hbf2 = [ps.enter_context(nc.sbuf_tensor(f"hbf{i}", [128, II], BF16))
                    for i in range(2)]
            t1 = ps.enter_context(nc.sbuf_tensor("t1", [128, 2048], F32))
            t2 = ps.enter_context(nc.sbuf_tensor("t2", [128, 2048], F32))
            t3 = ps.enter_context(nc.sbuf_tensor("t3", [128, 2048], F32))
            tstage = ps.enter_context(nc.sbuf_tensor("tstage", [128, 2048], BF16))
            tp_ps = [ps.enter_context(nc.psum_tensor(f"tp_ps{i}", [128, 128], BF16))
                     for i in range(6)]
            dum5b_ps = ps.enter_context(nc.psum_tensor("p5b_dum", [32, 1], F32))

            plan = Plan()
            hr_sync, msk_dve, sq_act5, hsw_sync = {}, {}, {}, {}
            tp_pe, tev_n = {}, {}

            def q_(t, k):
                return htile2[t % 2][:, k * 2048:(k + 1) * 2048]

            def qb_(t, k):
                return hbf2[t % 2][:, k * 2048:(k + 1) * 2048]

            for t in range(NT):
                def hrf(eng, t=t):
                    if t == 0:
                        for i in range(4):
                            eng.wait_ge(s_hst[i].h, p5a_hst[i])
                    if t >= 2:
                        eng.wait_ge(spe.h, tp_pe[(t - 2, 63)])
                    eng.dma_start(out=htile2[t % 2][:], in_=h_dram.ap()[t]).then_inc(sh.h, 16)
                plan.add("sync", hrf)
                hr_sync[t] = sh.inc(16)

                def castf(eng, t=t):
                    eng.wait_ge(sh.h, hr_sync[t])
                    eng.tensor_copy(out=hbf2[t % 2][:], in_=htile2[t % 2][:]).then_inc(sdve.h, 1)
                cast_v = sdve.inc()
                plan.add("dve", castf)

                def sqf(eng, t=t, cast_v=cast_v):
                    eng.wait_ge(sdve.h, cast_v)
                    eng.activation(out=htile2[t % 2][:], in_=htile2[t % 2][:],
                                   func=AF.Square).then_inc(sact.h, 1)
                sq_act5[t] = sact.inc()
                plan.add("act", sqf)

                def mskf(eng, t=t):
                    eng.wait_ge(sact.h, sq_act5[t])
                    eng.tensor_tensor(out=t1[:], in0=q_(t, 0), in1=q_(t, 1), op=ALU.min)
                    eng.tensor_tensor(out=t2[:], in0=q_(t, 2), in1=q_(t, 3), op=ALU.min)
                    eng.tensor_max(out=t1[:], in0=t1[:], in1=t2[:])
                    eng.tensor_max(out=t2[:], in0=q_(t, 0), in1=q_(t, 1))
                    eng.tensor_max(out=t3[:], in0=q_(t, 2), in1=q_(t, 3))
                    eng.tensor_tensor(out=t2[:], in0=t2[:], in1=t3[:], op=ALU.min)
                    eng.tensor_max(out=t2[:], in0=t1[:], in1=t2[:])
                    for k in range(4):
                        eng.tensor_tensor(out=q_(t, k), in0=q_(t, k), in1=t2[:], op=ALU.is_ge)
                        ins = eng.tensor_mul(out=qb_(t, k), in0=qb_(t, k), in1=q_(t, k))
                    ins.then_inc(sdve.h, 1)
                msk_dve[t] = sdve.inc()
                plan.add("dve", mskf)

                for ic in range(64):
                    def tpf(eng, t=t, ic=ic):
                        if ic == 0:
                            eng.wait_ge(sdve.h, msk_dve[t])
                            if t >= 1:
                                eng.wait_ge(shs.h, hsw_sync[(t - 1, 7)])
                        if ic >= 6:
                            kind, val = tev_n[(t, ic - 6)]
                            eng.wait_ge(sdve.h if kind == "dve" else sact.h, val)
                        eng.transpose(tp_ps[ic % 6][:],
                                      hbf2[t % 2][:, ic * 128:(ic + 1) * 128],
                                      ident[:]).then_inc(spe.h, 1)
                    tp_pe[(t, ic)] = spe.inc()
                    plan.add("pe", tpf)

                    eng_name = "dve" if ic % 2 == 0 else "act"

                    def tevac(eng, t=t, ic=ic, eng_name=eng_name):
                        eng.wait_ge(spe.h, tp_pe[(t, ic)] + 1)
                        if ic >= 16:
                            eng.wait_ge(shs.h, hsw_sync[(t, ic // 8 - 2)])
                        dst = tstage[:, (ic % 16) * 128:(ic % 16) * 128 + 128]
                        if eng_name == "dve":
                            eng.tensor_copy(out=dst, in_=tp_ps[ic % 6][:]).then_inc(sdve.h, 1)
                        else:
                            eng.activation(out=dst, in_=tp_ps[ic % 6][:],
                                           func=AF.Identity).then_inc(sact.h, 1)
                    if eng_name == "dve":
                        tev_n[(t, ic)] = ("dve", sdve.inc())
                    else:
                        tev_n[(t, ic)] = ("act", sact.inc())
                    plan.add(eng_name, tevac)

                    if ic % 8 == 7:
                        def tstore(eng, t=t, ic8=ic // 8):
                            for icc in range(ic8 * 8, ic8 * 8 + 8):
                                kind, val = tev_n[(t, icc)]
                                eng.wait_ge(sdve.h if kind == "dve" else sact.h, val)
                            src_ = tstage[:, (ic8 % 2) * 1024:(ic8 % 2) * 1024 + 1024]
                            eng.dma_start(
                                out=hs_fm_dram.ap()[ic8 * 8:(ic8 + 1) * 8, :,
                                                    t * 128:(t + 1) * 128].transpose([1, 0, 2]),
                                in_=src_.rearrange("p (i n) -> p i n", i=8)).then_inc(shs.h, 16)
                        plan.add("sync", tstore)
                        hsw_sync[(t, ic // 8)] = shs.inc(16)

                def tpdum(eng, t=t):
                    eng.matmul(dum5b_ps[:], ones128[:, 0:32], onescol[:],
                               start=True, stop=True).then_inc(spe.h, 1)
                spe.inc()
                plan.add("pe", tpdum)
            p5b_shs = shs.n

            with nc.Block("p5b") as blk:
                @blk.tensor
                def _(eng):
                    plan.run("pe", eng)

                @blk.scalar
                def _(eng):
                    plan.run("act", eng)

                @blk.vector
                def _(eng):
                    plan.run("dve", eng)

                @blk.sync
                def _(eng):
                    plan.run("sync", eng)

        with contextlib.ExitStack() as ps:
            hfm = [ps.enter_context(nc.sbuf_tensor(f"hfm_{i}", [128, 512], BF16))
                   for i in range(4)]
            wdb = [ps.enter_context(nc.sbuf_tensor(f"wdb_{i}", [128, 2048], BF16))
                   for i in range(3)]
            y_ps = [ps.enter_context(nc.psum_tensor(f"y_ps{i}", [128, 512], F32))
                    for i in range(8)]

            plan = Plan()
            hs_ts, wd_fetch, dmm_pe, yev_dve = {}, {}, {}, {}
            wd_slot_user = {}

            for half in range(2):
                for gh in range(2):
                    pi = half * 2 + gh
                    for ic in range(64):
                        gi = pi * 64 + ic

                        def hstr(eng, half=half, ic=ic, gi=gi):
                            if gi == 0:
                                eng.wait_ge(shs.h, p5b_shs)
                            if gi >= 4:
                                ppi, pic = divmod(gi - 4, 64)
                                eng.wait_ge(spe.h, dmm_pe[(ppi, pic)])
                            eng.dma_start(
                                out=hfm[ic % 4][:],
                                in_=hs_fm_dram.ap()[ic, :, half * 512:(half + 1) * 512]
                            ).then_inc(s_tp[ic % 4].h, 16)
                        plan.add("sync", hstr)
                        hs_ts[(pi, ic)] = s_tp[ic % 4].inc(16)

                        prev_user = wd_slot_user.get(ic % 3)

                        def wdf(eng, ic=ic, gi=gi, prev_user=prev_user):
                            if gi == 0:
                                eng.wait_ge(s_wdc.h, s_wdc.n)
                            if prev_user is not None:
                                eng.wait_ge(spe.h, dmm_pe[prev_user])
                            eng.dma_start(out=wdb[ic % 3][:],
                                          in_=wd_bf.ap()[:, ic * 2048:(ic + 1) * 2048]).then_inc(s_wd[ic % 3].h, 16)
                        plan.add("gpsimd", wdf)
                        wd_fetch[(pi, ic)] = s_wd[ic % 3].inc(16)
                        wd_slot_user[ic % 3] = (pi, ic)

                        def dmm(eng, gh=gh, ic=ic, pi=pi):
                            eng.wait_ge(s_tp[ic % 4].h, hs_ts[(pi, ic)])
                            eng.wait_ge(s_wd[ic % 3].h, wd_fetch[(pi, ic)])
                            if ic == 0 and pi >= 1:
                                eng.wait_ge(sdve.h, yev_dve[pi - 1])
                            ins = None
                            for gg in range(8):
                                g = gh * 8 + gg
                                ins = eng.matmul(y_ps[gg][:],
                                                 wdb[ic % 3][:, g * 128:(g + 1) * 128],
                                                 hfm[ic % 4][:],
                                                 start=(ic == 0), stop=(ic == 63))
                            ins.then_inc(spe.h, 1)
                        dmm_pe[(pi, ic)] = spe.inc()
                        plan.add("pe", dmm)

                    def yev(eng, half=half, gh=gh, pi=pi):
                        eng.wait_ge(spe.h, dmm_pe[(pi, 63)])
                        ins = None
                        for gg in range(8):
                            g = gh * 8 + gg
                            sl_ = stream[:, g * TOK + half * 512: g * TOK + half * 512 + 512]
                            ins = eng.tensor_add(out=sl_, in0=y_ps[gg][:], in1=sl_)
                        ins.then_inc(sdve.h, 1)
                    yev_dve[pi] = sdve.inc()
                    plan.add("dve", yev)

            with nc.Block("p5c") as blk:
                @blk.tensor
                def _(eng):
                    plan.run("pe", eng)

                @blk.vector
                def _(eng):
                    plan.run("dve", eng)

                @blk.sync
                def _(eng):
                    plan.run("sync", eng)

                @blk.gpsimd
                def _(eng):
                    plan.run("gpsimd", eng)

        if debug:
            with nc.Block("dbgy") as blk:
                @blk.sync
                def _(eng):
                    eng.dma_start(out=dbg_y, in_=stream[:]).then_inc(sout.h, 16)
                    sout.n += 16
                    eng.wait_ge(sout.h, sout.n)

        apply6 = emit_layernorm("p6", False, None, False,
                                lambda f: stream[:, f * TOK:(f + 1) * TOK], {})

        with nc.Block("p6s") as blk:
            @blk.sync
            def _(eng):
                for f in range(NF):
                    eng.wait_ge(sdve.h, apply6[f])
                    eng.dma_start(out=out_t[f * 128:(f + 1) * 128, :],
                                  in_=stream[:, f * TOK:(f + 1) * TOK]).then_inc(sout.h, 16)
                    sout.n += 16
                if debug:
                    eng.dma_start(out=dbg_ccin, in_=cc_in.ap()).then_inc(sout.h, 16)
                    eng.dma_start(out=dbg_q.rearrange("h p t -> (h p) t"),
                                  in_=q_dram.ap().rearrange("h p t -> (h p) t")).then_inc(sout.h, 16)
                    for t in range(NT):
                        eng.dma_start(out=dbg_h[t], in_=h_dram.ap()[t]).then_inc(sout.h, 16)
                    eng.dma_start(out=dbg_hs.rearrange("i p t -> (i p) t"),
                                  in_=hs_fm_dram.ap().rearrange("i p t -> (i p) t")).then_inc(sout.h, 16)
                    sout.n += 16 * (3 + NT)
                eng.wait_ge(sout.h, sout.n)

    return nc


_NC_CACHE = None


def _get_nc():
    global _NC_CACHE
    if _NC_CACHE is None:
        _NC_CACHE = build_program()
    return _NC_CACHE


def kernel(L0_attn_X, L0_attn_ln_g, L0_attn_ln_b, L0_attn_Wq, L0_attn_bq,
           L0_attn_Wk, L0_attn_bk, L0_attn_Wv, L0_attn_bv, L0_attn_Wo,
           L0_attn_bo, L0_ffn_w_gate, L0_ffn_w_up, L0_ffn_w_down):
    X = np.asarray(L0_attn_X)[0]
    Wq = np.asarray(L0_attn_Wq)
    Wk = np.asarray(L0_attn_Wk)
    Wv = np.asarray(L0_attn_Wv)
    Wo = np.asarray(L0_attn_Wo)
    Wg = np.asarray(L0_ffn_w_gate)
    Wu = np.asarray(L0_ffn_w_up)
    Wd = np.asarray(L0_ffn_w_down)

    def lhsT_layout(W):
        n_in = W.shape[0] // 128
        return np.ascontiguousarray(
            W.reshape(n_in, 128, 16, 128).transpose(1, 2, 0, 3).reshape(128, -1))

    def rhs_layout(W):
        return np.ascontiguousarray(
            W.reshape(16, 128, -1).transpose(1, 0, 2).reshape(128, -1))

    perm = 4 * (np.arange(II) % 2048) + np.arange(II) // 2048

    wq_dev = lhsT_layout(Wq)
    wk_dev = lhsT_layout(Wk)
    wo_dev = lhsT_layout(Wo)
    wv_dev = rhs_layout(Wv)
    wg_dev = rhs_layout(Wg[:, perm])
    wu_dev = rhs_layout(Wu[:, perm])
    wd_dev = np.ascontiguousarray(
        Wd[perm, :].reshape(64, 128, 16, 128).transpose(1, 0, 2, 3).reshape(128, -1))

    def colvec(b):
        return np.ascontiguousarray(np.asarray(b).reshape(16, 128).T)

    bq_dev = colvec(L0_attn_bq)
    bk_dev = colvec(L0_attn_bk)
    bo_dev = colvec(L0_attn_bo)
    lng_dev = colvec(L0_attn_ln_g)
    lnb_dev = colvec(L0_attn_ln_b)
    bv_dev = np.ascontiguousarray(np.asarray(L0_attn_bv)[None, :])

    p = np.arange(128)[:, None]
    v = np.arange(896)[None, :]
    in_maps = []
    for c in range(NCORES):
        Xc = np.ascontiguousarray(X[c::8, :].T)
        masks = np.zeros((8, 128, 896), dtype=ml_dtypes.bfloat16)
        for r in range(8):
            delta = 1 if r > c else 0
            masks[r] = ((v - 384 - p) >= delta).astype(ml_dtypes.bfloat16)
        in_maps.append({
            "x": Xc, "wq": wq_dev, "wk": wk_dev, "wv": wv_dev, "wo": wo_dev,
            "wg": wg_dev, "wu": wu_dev, "wd": wd_dev,
            "bq": bq_dev, "bk": bk_dev, "bo": bo_dev, "bvr": bv_dev,
            "lng": lng_dev, "lnb": lnb_dev, "masks": masks,
            "identw": np.eye(128, dtype=ml_dtypes.bfloat16),
        })

    nc = _get_nc()
    res = run_bass_kernel_spmd(nc, in_maps, core_ids=list(range(NCORES)))
    out = np.empty((S, DM), np.float32)
    for c in range(NCORES):
        out[c::8, :] = res.results[c]["out"].T
    return out[None]


# version 20 (final)
# speedup vs baseline: 1.2801x; 1.0738x over previous
import contextlib
import numpy as np
import ml_dtypes
import concourse.bass as bass
import concourse.mybir as mybir
from concourse.bass_utils import run_bass_kernel_spmd

F32 = mybir.dt.float32
BF16 = mybir.dt.bfloat16
AF = mybir.ActivationFunctionType
ALU = mybir.AluOpType

NCORES = 8
DM = 2048
NF = 16
NH = 16
S = 8192
TOK = 1024
NT = 8
II = 8192
EPS = 1e-5
SCALE = 1.0 / float(np.sqrt(128.0))
EXP_BIAS = -2.0


class Sem:
    def __init__(self, nc, es, name):
        self.h = es.enter_context(nc.semaphore(name))
        self.n = 0

    def inc(self, k=1):
        self.n += k
        return self.n


class Plan:
    def __init__(self):
        self.items = []

    def add(self, eng, fn):
        self.items.append((eng, fn))

    def run(self, name, engobj):
        for e, fn in self.items:
            if e == name:
                fn(engobj)


def build_program(debug=False):
    nc = bass.Bass("TRN2", target_bir_lowering=False)

    x_in = nc.dram_tensor("x", [DM, TOK], F32, kind="ExternalInput").ap()
    wq_in = nc.dram_tensor("wq", [128, NF * DM], F32, kind="ExternalInput").ap()
    wk_in = nc.dram_tensor("wk", [128, NF * DM], F32, kind="ExternalInput").ap()
    wv_in = nc.dram_tensor("wv", [128, NF * DM], F32, kind="ExternalInput").ap()
    wo_in = nc.dram_tensor("wo", [128, NF * DM], F32, kind="ExternalInput").ap()
    wg_in = nc.dram_tensor("wg", [128, NF * II], F32, kind="ExternalInput").ap()
    wu_in = nc.dram_tensor("wu", [128, NF * II], F32, kind="ExternalInput").ap()
    wd_in = nc.dram_tensor("wd", [128, 64 * DM], F32, kind="ExternalInput").ap()
    bq_in = nc.dram_tensor("bq", [128, 16], F32, kind="ExternalInput").ap()
    bk_in = nc.dram_tensor("bk", [128, 16], F32, kind="ExternalInput").ap()
    bo_in = nc.dram_tensor("bo", [128, 16], F32, kind="ExternalInput").ap()
    bv_in = nc.dram_tensor("bvr", [1, DM], F32, kind="ExternalInput").ap()
    lng_in = nc.dram_tensor("lng", [128, 16], F32, kind="ExternalInput").ap()
    lnb_in = nc.dram_tensor("lnb", [128, 16], F32, kind="ExternalInput").ap()
    masks_in = nc.dram_tensor("masks", [8, 128, 896], BF16, kind="ExternalInput").ap()
    ident_in = nc.dram_tensor("identw", [128, 128], BF16, kind="ExternalInput").ap()
    out_t = nc.dram_tensor("out", [DM, TOK], F32, kind="ExternalOutput").ap()

    cc_in = nc.dram_tensor("cc_in", [4096, 1024], BF16, kind="Internal")
    cc_out = nc.dram_tensor("cc_out", [NCORES * 4096, 1024], BF16, kind="Internal")
    q_dram = nc.dram_tensor("q_dram", [NH, 128, TOK], BF16, kind="Internal")
    h_dram = nc.dram_tensor("h_dram", [NT, 128, II], F32, kind="Internal")
    hs_fm_dram = nc.dram_tensor("hs_fm", [64, 128, TOK], BF16, kind="Internal")
    wd_bf = nc.dram_tensor("wd_bf", [128, 64 * DM], BF16, kind="Internal")
    if debug:
        dbg_xn = nc.dram_tensor("dbg_xn", [128, NF * TOK], BF16, kind="ExternalOutput").ap()
        dbg_x2 = nc.dram_tensor("dbg_x2", [128, NF * TOK], F32, kind="ExternalOutput").ap()
        dbg_ccin = nc.dram_tensor("dbg_ccin", [4096, 1024], BF16, kind="ExternalOutput").ap()
        dbg_q = nc.dram_tensor("dbg_q", [NH, 128, TOK], BF16, kind="ExternalOutput").ap()
        dbg_h = nc.dram_tensor("dbg_h", [NT, 128, II], F32, kind="ExternalOutput").ap()
        dbg_hs = nc.dram_tensor("dbg_hs", [64, 128, TOK], BF16, kind="ExternalOutput").ap()
        dbg_y = nc.dram_tensor("dbg_y", [128, NF * TOK], F32, kind="ExternalOutput").ap()

    es = contextlib.ExitStack()
    with es:
        stream = es.enter_context(nc.sbuf_tensor("stream", [128, NF * TOK], F32))
        cpool = es.enter_context(nc.sbuf_tensor("cpool", [128, 8], F32))
        bq_sb = es.enter_context(nc.sbuf_tensor("bq_sb", [128, 16], F32))
        bk_sb = es.enter_context(nc.sbuf_tensor("bk_sb", [128, 16], F32))
        bo_sb = es.enter_context(nc.sbuf_tensor("bo_sb", [128, 16], F32))
        lng_sb = es.enter_context(nc.sbuf_tensor("lng_sb", [128, 16], F32))
        lnb_sb = es.enter_context(nc.sbuf_tensor("lnb_sb", [128, 16], F32))
        ones128 = es.enter_context(nc.sbuf_tensor("ones128", [128, 128], BF16))
        ident = es.enter_context(nc.sbuf_tensor("ident", [128, 128], BF16))
        onescol = es.enter_context(nc.sbuf_tensor("onescol", [128, 1], BF16))
        onesrow = es.enter_context(nc.sbuf_tensor("onesrow", [1, 128], F32))

        sx = Sem(nc, es, "sx")
        s_w = [Sem(nc, es, f"s_w{i}") for i in range(2)]
        s_st = [Sem(nc, es, f"s_st{i}") for i in range(2)]
        s_vst = [Sem(nc, es, f"s_vst{i}") for i in range(2)]
        s_qst = [Sem(nc, es, f"s_qst{i}") for i in range(2)]
        s_kv = [Sem(nc, es, f"s_kv{i}") for i in range(2)]
        s_hst = [Sem(nc, es, f"s_hst{i}") for i in range(4)]
        s_tp = [Sem(nc, es, f"s_tp{i}") for i in range(4)]
        s_wd = [Sem(nc, es, f"s_wd{i}") for i in range(3)]
        s_wdc = Sem(nc, es, "s_wdc")
        scast = Sem(nc, es, "scast")
        skvst = Sem(nc, es, "skvst")
        scc = Sem(nc, es, "scc")
        skvf = Sem(nc, es, "skvf")
        sh = Sem(nc, es, "sh")
        shs = Sem(nc, es, "shs")
        sout = Sem(nc, es, "sout")
        spe = Sem(nc, es, "spe")
        sact = Sem(nc, es, "sact")
        sdve = Sem(nc, es, "sdve")

        C_SCALE, C_EBIAS, C_EPS, C_NHALF, C_NONE = 0, 1, 2, 3, 4
        CVALS = {C_SCALE: SCALE, C_EBIAS: EXP_BIAS, C_EPS: EPS,
                 C_NHALF: -0.5, C_NONE: -1.0}

        def cp(i):
            return cpool[:, i:i + 1]

        def emit_layernorm(name, src_is_xb, xb_ap, with_gb, out_fn, pre_wait):
            with contextlib.ExitStack() as ps:
                s1_ps = ps.enter_context(nc.psum_tensor(name + "_s1", [128, TOK], F32))
                s2_ps = ps.enter_context(nc.psum_tensor(name + "_s2", [128, TOK], F32))
                dln_ps = ps.enter_context(nc.psum_tensor(name + "_dum", [32, 1], F32))
                xsq = ps.enter_context(nc.sbuf_tensor(name + "_xsq", [128, 2 * TOK], BF16))
                xbf = None
                if not src_is_xb:
                    xbf = ps.enter_context(nc.sbuf_tensor(name + "_xbf", [128, 2 * TOK], BF16))
                mu = ps.enter_context(nc.sbuf_tensor(name + "_mu", [128, TOK], F32))
                rstd = ps.enter_context(nc.sbuf_tensor(name + "_rstd", [128, TOK], F32))
                m2 = ps.enter_context(nc.sbuf_tensor(name + "_m2", [128, TOK], F32))
                ta = ps.enter_context(nc.sbuf_tensor(name + "_ta", [128, TOK], F32))

                plan = Plan()
                cast_dve = {}
                sq_act = {}
                mm_pe = {}
                for f in range(NF):
                    if xbf is not None:
                        def cast_fn(eng, ff=f):
                            if ff >= 2:
                                eng.wait_ge(spe.h, mm_pe[ff - 2])
                            eng.tensor_scalar(
                                out=xbf[:, (ff % 2) * TOK:(ff % 2 + 1) * TOK],
                                in0=stream[:, ff * TOK:(ff + 1) * TOK],
                                scalar1=1.0, scalar2=None,
                                op0=ALU.mult).then_inc(sdve.h, 1)
                        cast_dve[f] = sdve.inc()
                        plan.add("dve", cast_fn)

                    def sq_fn(eng, ff=f):
                        if xbf is not None:
                            eng.wait_ge(sdve.h, cast_dve[ff])
                        elif ff >= 2:
                            eng.wait_ge(spe.h, mm_pe[ff - 2])
                        src = (xbf[:, (ff % 2) * TOK:(ff % 2 + 1) * TOK]
                               if xbf is not None
                               else xb_ap[:, ff * TOK:(ff + 1) * TOK])
                        eng.activation(out=xsq[:, (ff % 2) * TOK:(ff % 2 + 1) * TOK],
                                       in_=src, func=AF.Square).then_inc(sact.h, 1)
                    sq_act[f] = sact.inc()
                    plan.add("act", sq_fn)

                    def mm_fn(eng, ff=f):
                        eng.wait_ge(sact.h, sq_act[ff])
                        src1 = (xbf[:, (ff % 2) * TOK:(ff % 2 + 1) * TOK]
                                if xbf is not None
                                else xb_ap[:, ff * TOK:(ff + 1) * TOK])
                        src2 = xsq[:, (ff % 2) * TOK:(ff % 2 + 1) * TOK]
                        st = (ff == 0)
                        sp = (ff == NF - 1)
                        eng.matmul(s1_ps[:, 0:512], ones128[:], src1[:, 0:512], start=st, stop=sp)
                        eng.matmul(s1_ps[:, 512:1024], ones128[:], src1[:, 512:1024], start=st, stop=sp)
                        eng.matmul(s2_ps[:, 0:512], ones128[:], src2[:, 0:512], start=st, stop=sp)
                        eng.matmul(s2_ps[:, 512:1024], ones128[:], src2[:, 512:1024],
                                   start=st, stop=sp).then_inc(spe.h, 1)
                        if sp:
                            eng.matmul(dln_ps[:], ones128[:, 0:32], onescol[:],
                                       start=True, stop=True).then_inc(spe.h, 1)
                    mm_pe[f] = spe.inc()
                    if f == NF - 1:
                        spe.inc()
                    plan.add("pe", mm_fn)

                stats_done = mm_pe[NF - 1] + 1

                def mu_fn(eng):
                    eng.wait_ge(spe.h, stats_done)
                    eng.tensor_scalar(out=mu[:], in0=s1_ps[:], scalar1=1.0 / 2048.0,
                                      scalar2=None, op0=ALU.mult).then_inc(sdve.h, 1)
                mu_dve = sdve.inc()
                plan.add("dve", mu_fn)

                def musq_fn(eng):
                    eng.wait_ge(sdve.h, mu_dve)
                    eng.activation(out=ta[:], in_=mu[:], func=AF.Square).then_inc(sact.h, 1)
                musq_act = sact.inc()
                plan.add("act", musq_fn)

                def var_fn(eng):
                    eng.wait_ge(sact.h, musq_act)
                    eng.scalar_tensor_tensor(
                        out=rstd[:], in0=s2_ps[:], scalar=1.0 / 2048.0, in1=ta[:],
                        op0=ALU.mult, op1=ALU.subtract).then_inc(sdve.h, 1)
                var_dve = sdve.inc()
                plan.add("dve", var_fn)

                rstd_act = sact.inc(2)

                def rstd_fn(eng):
                    eng.wait_ge(sdve.h, var_dve)
                    eng.activation(out=rstd[:], in_=rstd[:], func=AF.Ln,
                                   bias=cp(C_EPS)).then_inc(sact.h, 1)
                    eng.activation(out=rstd[:], in_=rstd[:], func=AF.Exp,
                                   scale=cp(C_NHALF)).then_inc(sact.h, 1)
                plan.add("act", rstd_fn)

                def m2_fn(eng):
                    eng.wait_ge(sact.h, rstd_act)
                    eng.tensor_mul(out=m2[:], in0=mu[:], in1=rstd[:]).then_inc(sdve.h, 1)
                m2_dve = sdve.inc()
                plan.add("dve", m2_fn)

                apply_dve = {}
                for f in range(NF):
                    def ap_fn(eng, ff=f):
                        if ff == 0:
                            eng.wait_ge(sdve.h, m2_dve)
                        eng.tensor_mul(out=ta[:], in0=stream[:, ff * TOK:(ff + 1) * TOK],
                                       in1=rstd[:])
                        if with_gb:
                            eng.tensor_sub(out=ta[:], in0=ta[:], in1=m2[:])
                            eng.tensor_scalar(out=out_fn(ff), in0=ta[:],
                                              scalar1=lng_sb[:, ff:ff + 1],
                                              scalar2=lnb_sb[:, ff:ff + 1],
                                              op0=ALU.mult, op1=ALU.add).then_inc(sdve.h, 1)
                        else:
                            eng.tensor_sub(out=out_fn(ff), in0=ta[:],
                                           in1=m2[:]).then_inc(sdve.h, 1)
                    apply_dve[f] = sdve.inc()
                    plan.add("dve", ap_fn)

                with nc.Block(name) as blk:
                    @blk.tensor
                    def _(eng):
                        for sem, val in pre_wait.get("pe", []):
                            eng.wait_ge(sem.h, val)
                        plan.run("pe", eng)

                    @blk.scalar
                    def _(eng):
                        for sem, val in pre_wait.get("act", []):
                            eng.wait_ge(sem.h, val)
                        plan.run("act", eng)

                    @blk.vector
                    def _(eng):
                        for sem, val in pre_wait.get("dve", []):
                            eng.wait_ge(sem.h, val)
                        plan.run("dve", eng)
                return apply_dve

        with nc.sbuf_tensor("xb", [128, NF * TOK], BF16) as xb:
            with nc.Block("p0") as blk:
                @blk.sync
                def _(eng):
                    src = x_in.rearrange("(f p) t -> p f t", p=128)
                    dst = stream[:].rearrange("p (f t) -> p f t", f=NF)
                    eng.dma_start(out=dst[:, 0:8], in_=src[:, 0:8]).then_inc(sx.h, 16)
                    eng.dma_start(out=dst[:, 8:16], in_=src[:, 8:16]).then_inc(sx.h, 16)
                    for ap_in, sb in ((bq_in, bq_sb), (bk_in, bk_sb), (bo_in, bo_sb),
                                      (lng_in, lng_sb), (lnb_in, lnb_sb), (ident_in, ident)):
                        eng.dma_start(out=sb[:], in_=ap_in).then_inc(sx.h, 16)

                @blk.gpsimd
                def _(eng):
                    src = x_in.rearrange("(f p) t -> p f t", p=128)
                    dst = xb[:].rearrange("p (f t) -> p f t", f=NF)
                    eng.dma_start(out=dst[:, 0:8], in_=src[:, 0:8]).then_inc(scast.h, 16)
                    eng.dma_start(out=dst[:, 8:16], in_=src[:, 8:16]).then_inc(scast.h, 16)
                    eng.memset(ones128[:], 1.0)
                    eng.memset(onescol[:], 1.0)
                    eng.memset(onesrow[:], 1.0)
                    for i, v in CVALS.items():
                        eng.memset(cpool[:, i:i + 1], v)
            sx.n += 8 * 16
            scast.n += 32
            p0_sx = sx.n
            p0_scast = scast.n

            emit_layernorm("p1", True, xb,
                           True, lambda f: xb[:, f * TOK:(f + 1) * TOK],
                           {"pe": [(scast, p0_scast)],
                            "act": [(scast, p0_scast)],
                            "dve": [(sx, p0_sx)]})

            with contextlib.ExitStack() as ps:
                bv_bc = ps.enter_context(nc.sbuf_tensor("bv_bc", [128, DM], F32))
                bv_row = ps.enter_context(nc.sbuf_tensor("bv_row", [1, DM], F32))
                wbuf = [ps.enter_context(nc.sbuf_tensor(f"w2_{i}", [128, NF * 512], BF16))
                        for i in range(2)]
                kstage = [ps.enter_context(nc.sbuf_tensor(f"kst_{i}", [128, TOK], BF16))
                          for i in range(2)]
                vstage = [ps.enter_context(nc.sbuf_tensor(f"vst_{i}", [128, 512], BF16))
                          for i in range(2)]
                pp = [ps.enter_context(nc.psum_tensor(f"p2_{i}", [128, 512], F32))
                      for i in range(2)]
                bv_ps = ps.enter_context(nc.psum_tensor("p2_bv", [128, DM], F32))
                dum2_ps = ps.enter_context(nc.psum_tensor("p2_dum", [32, 1], F32))

                def p2dummy(eng):
                    eng.matmul(dum2_ps[:], ones128[:, 0:32], onescol[:],
                               start=True, stop=True).then_inc(spe.h, 1)

                plan = Plan()
                cc_in_ap = cc_in.ap()
                cc_in_V = cc_in_ap[2048:4096, :].rearrange("(t two) n -> t (two n)", two=2)

                def bvload_fn(eng):
                    eng.dma_start(out=bv_row[:], in_=bv_in).then_inc(sx.h, 16)
                bv_sx = sx.inc(16)
                plan.add("sync", bvload_fn)

                def bvbc_fn(eng):
                    eng.wait_ge(sx.h, bv_sx)
                    for j in range(3):
                        eng.matmul(bv_ps[:, j * 512:(j + 1) * 512], onesrow[:],
                                   bv_row[:, j * 512:(j + 1) * 512], start=True, stop=True)
                    eng.matmul(bv_ps[:, 1536:2048], onesrow[:],
                               bv_row[:, 1536:2048], start=True, stop=True).then_inc(spe.h, 1)
                    p2dummy(eng)
                spe.inc()
                bv_pe = spe.inc()
                plan.add("pe", bvbc_fn)

                def bvcp_fn(eng):
                    eng.wait_ge(spe.h, bv_pe)
                    eng.tensor_copy(out=bv_bc[:], in_=bv_ps[:]).then_inc(sdve.h, 1)
                bv_dve = sdve.inc()
                plan.add("dve", bvcp_fn)

                kmm_pe, kevac_act, kdma = {}, {}, {}
                vmm_pe, vevac_dve, vdma = {}, {}, {}
                qmm_pe, qevac_act, qdma = {}, {}, {}
                wk_fetch, wv_fetch, wq_fetch = {}, {}, {}

                for g in range(16):
                    def wkf(eng, g=g):
                        if g >= 2:
                            eng.wait_ge(spe.h, kmm_pe[(g - 2, 1)])
                        eng.dma_start(out=wbuf[g % 2][:, 0:2048],
                                      in_=wk_in[:, g * 2048:(g + 1) * 2048]).then_inc(s_w[g % 2].h, 16)
                    plan.add("gpsimd", wkf)
                    wk_fetch[g] = s_w[g % 2].inc(16)

                    for half in range(2):
                        def kmm(eng, g=g, half=half):
                            eng.wait_ge(s_w[g % 2].h, wk_fetch[g])
                            idx = g * 2 + half
                            if idx >= 2:
                                pg, ph = divmod(idx - 2, 2)
                                eng.wait_ge(sact.h, kevac_act[(pg, ph)])
                            ins = None
                            for f in range(NF):
                                ins = eng.matmul(
                                    pp[half][:],
                                    wbuf[g % 2][:, f * 128:(f + 1) * 128],
                                    xb[:, f * TOK + half * 512: f * TOK + half * 512 + 512],
                                    start=(f == 0), stop=(f == NF - 1))
                            ins.then_inc(spe.h, 1)
                            p2dummy(eng)
                        spe.inc()
                        kmm_pe[(g, half)] = spe.inc()
                        plan.add("pe", kmm)

                        def kevac(eng, g=g, half=half):
                            eng.wait_ge(spe.h, kmm_pe[(g, half)])
                            if g >= 2 and half == 0:
                                eng.wait_ge(s_st[g % 2].h, kdma[g - 2])
                            eng.activation(out=kstage[g % 2][:, half * 512:half * 512 + 512],
                                           in_=pp[half][:], func=AF.Identity,
                                           bias=bk_sb[:, g:g + 1]).then_inc(sact.h, 1)
                        kevac_act[(g, half)] = sact.inc()
                        plan.add("act", kevac)

                    def kdma_fn(eng, g=g):
                        eng.wait_ge(sact.h, kevac_act[(g, 1)])
                        eng.dma_start(out=cc_in_ap[g * 128:(g + 1) * 128, :],
                                      in_=kstage[g % 2][:]).then_inc(s_st[g % 2].h, 16)
                    plan.add("sync", kdma_fn)
                    kdma[g] = s_st[g % 2].inc(16)

                wv_r = wv_in.rearrange("p (f n) -> p f n", f=NF)
                for j in range(4):
                    def wvf(eng, j=j):
                        if j >= 2:
                            eng.wait_ge(spe.h, vmm_pe[(j - 2, NT - 1)])
                        src = wv_r[:, :, j * 512:(j + 1) * 512]
                        dst = wbuf[j % 2][:].rearrange("p (f n) -> p f n", f=NF)
                        eng.dma_start(out=dst[:, 0:8], in_=src[:, 0:8]).then_inc(s_w[j % 2].h, 16)
                        eng.dma_start(out=dst[:, 8:16], in_=src[:, 8:16]).then_inc(s_w[j % 2].h, 16)
                    plan.add("gpsimd", wvf)
                    wv_fetch[j] = s_w[j % 2].inc(32)

                    for t in range(NT):
                        def vmm(eng, j=j, t=t):
                            eng.wait_ge(s_w[j % 2].h, wv_fetch[j])
                            idx = j * NT + t
                            if idx < 2:
                                eng.wait_ge(sact.h, kevac_act[(15, 1)])
                            else:
                                pj, pt = divmod(idx - 2, NT)
                                eng.wait_ge(sdve.h, vevac_dve[(pj, pt)])
                            ins = None
                            for f in range(NF):
                                ins = eng.matmul(
                                    pp[t % 2][:],
                                    xb[:, f * TOK + t * 128: f * TOK + t * 128 + 128],
                                    wbuf[j % 2][:, f * 512:(f + 1) * 512],
                                    start=(f == 0), stop=(f == NF - 1))
                            ins.then_inc(spe.h, 1)
                            p2dummy(eng)
                        spe.inc()
                        vmm_pe[(j, t)] = spe.inc()
                        plan.add("pe", vmm)

                        def vevac(eng, j=j, t=t):
                            eng.wait_ge(spe.h, vmm_pe[(j, t)])
                            idx = j * NT + t
                            if idx == 0:
                                eng.wait_ge(sdve.h, bv_dve)
                            if idx >= 2:
                                pj, pt = divmod(idx - 2, NT)
                                eng.wait_ge(s_vst[t % 2].h, vdma[(pj, pt)])
                            eng.tensor_add(out=vstage[t % 2][:], in0=pp[t % 2][:],
                                           in1=bv_bc[:, j * 512:(j + 1) * 512]).then_inc(sdve.h, 1)
                        vevac_dve[(j, t)] = sdve.inc()
                        plan.add("dve", vevac)

                        def vdma_fn(eng, j=j, t=t):
                            eng.wait_ge(sdve.h, vevac_dve[(j, t)])
                            eng.dma_start(
                                out=cc_in_V[t * 128:(t + 1) * 128, j * 512:(j + 1) * 512],
                                in_=vstage[t % 2][:]).then_inc(s_vst[t % 2].h, 16)
                        plan.add("sync", vdma_fn)
                        vdma[(j, t)] = s_vst[t % 2].inc(16)

                staged_vals = [(s_st[0], s_st[0].n), (s_st[1], s_st[1].n),
                               (s_vst[0], s_vst[0].n), (s_vst[1], s_vst[1].n)]

                def coll_fn(eng):
                    for sem, val in staged_vals:
                        eng.wait_ge(sem.h, val)
                    eng.collective_compute(
                        "AllGather", ALU.bypass,
                        replica_groups=[list(range(NCORES))],
                        ins=[cc_in.ap().opt()],
                        outs=[cc_out.ap().opt()],
                    ).then_inc(scc.h, 1)
                plan.add("gpsimd", coll_fn)
                scc.inc()

                for g in range(16):
                    def wqf(eng, g=g):
                        if g >= 2:
                            eng.wait_ge(spe.h, qmm_pe[(g - 2, 1)])
                        else:
                            eng.wait_ge(spe.h, vmm_pe[(3, NT - 1)])
                        eng.dma_start(out=wbuf[g % 2][:, 0:2048],
                                      in_=wq_in[:, g * 2048:(g + 1) * 2048]).then_inc(s_w[g % 2].h, 16)
                    plan.add("gpsimd", wqf)
                    wq_fetch[g] = s_w[g % 2].inc(16)

                    for half in range(2):
                        def qmm(eng, g=g, half=half):
                            eng.wait_ge(s_w[g % 2].h, wq_fetch[g])
                            idx = g * 2 + half
                            if idx < 2:
                                eng.wait_ge(sdve.h, vevac_dve[(3, NT - 1)])
                            else:
                                pg, ph = divmod(idx - 2, 2)
                                eng.wait_ge(sact.h, qevac_act[(pg, ph)])
                            ins = None
                            for f in range(NF):
                                ins = eng.matmul(
                                    pp[half][:],
                                    wbuf[g % 2][:, f * 128:(f + 1) * 128],
                                    xb[:, f * TOK + half * 512: f * TOK + half * 512 + 512],
                                    start=(f == 0), stop=(f == NF - 1))
                            ins.then_inc(spe.h, 1)
                            p2dummy(eng)
                        spe.inc()
                        qmm_pe[(g, half)] = spe.inc()
                        plan.add("pe", qmm)

                        def qevac(eng, g=g, half=half):
                            eng.wait_ge(spe.h, qmm_pe[(g, half)])
                            if half == 0:
                                if g < 2:
                                    eng.wait_ge(s_st[g % 2].h, kdma[14 + g])
                                else:
                                    eng.wait_ge(s_qst[g % 2].h, qdma[g - 2])
                            eng.activation(
                                out=kstage[g % 2][:, half * 512:half * 512 + 512],
                                in_=pp[half][:], func=AF.Identity,
                                bias=bq_sb[:, g:g + 1]).then_inc(sact.h, 1)
                        qevac_act[(g, half)] = sact.inc()
                        plan.add("act", qevac)

                    def qdma_fn(eng, g=g):
                        eng.wait_ge(sact.h, qevac_act[(g, 1)])
                        eng.dma_start(out=q_dram.ap()[g],
                                      in_=kstage[g % 2][:]).then_inc(s_qst[g % 2].h, 16)
                    plan.add("sync", qdma_fn)
                    qdma[g] = s_qst[g % 2].inc(16)

                with nc.Block("p2") as blk:
                    @blk.tensor
                    def _(eng):
                        plan.run("pe", eng)

                    @blk.scalar
                    def _(eng):
                        plan.run("act", eng)

                    @blk.vector
                    def _(eng):
                        plan.run("dve", eng)

                    @blk.sync
                    def _(eng):
                        plan.run("sync", eng)

                    @blk.gpsimd
                    def _(eng):
                        plan.run("gpsimd", eng)
            p2_qst = (s_qst[0].n, s_qst[1].n)

            if debug:
                with nc.Block("dbgxn") as blk:
                    @blk.sync
                    def _(eng):
                        eng.dma_start(out=dbg_xn, in_=xb[:]).then_inc(sout.h, 16)
                        sout.n += 16
                        eng.wait_ge(sout.h, sout.n)

        with contextlib.ExitStack() as ps:
            kb_sb = [ps.enter_context(nc.sbuf_tensor(f"kb_{i}", [128, 8192], BF16))
                     for i in range(2)]
            vb_sb = [ps.enter_context(nc.sbuf_tensor(f"vb_{i}", [128, 8192], BF16))
                     for i in range(2)]
            qcur = [ps.enter_context(nc.sbuf_tensor(f"qc_{i}", [128, 512], BF16))
                    for i in range(2)]
            abuf = [ps.enter_context(nc.sbuf_tensor(f"ab_{i}", [128, 512], BF16))
                    for i in range(4)]
            av_sb = ps.enter_context(nc.sbuf_tensor("av_sb", [128, NH * 512], BF16))
            mask_sb = ps.enter_context(nc.sbuf_tensor("mask_sb", [128, 8 * 896], BF16))
            rdbc_sb = ps.enter_context(nc.sbuf_tensor("rdbc_sb", [128, 512], F32))
            wob = [ps.enter_context(nc.sbuf_tensor(f"wob_{i}", [128, 2048], BF16))
                   for i in range(2)]

            s_ps = [ps.enter_context(nc.psum_tensor(f"s_ps{i}", [128, 512], F32))
                    for i in range(3)]
            av_ps = [ps.enter_context(nc.psum_tensor(f"av_ps{i}", [128, 512], F32))
                     for i in range(2)]
            den_ps = [ps.enter_context(nc.psum_tensor(f"den_ps{i}", [128, 512], F32))
                      for i in range(2)]
            dum3_ps = ps.enter_context(nc.psum_tensor("dum3_ps", [32, 1], F32))

            plan = Plan()

            def mload(eng):
                src = masks_in.transpose([1, 0, 2])
                dst = mask_sb[:].rearrange("p (r n) -> p r n", r=8)
                eng.dma_start(out=dst[:, 0:4], in_=src[:, 0:4]).then_inc(sx.h, 16)
                eng.dma_start(out=dst[:, 4:8], in_=src[:, 4:8]).then_inc(sx.h, 16)
            plan.add("sync", mload)
            mask_sx = sx.inc(32)

            ccr = cc_out.ap().rearrange("(r x) n -> r x n", x=4096)
            vsrc_all = ccr[:, 2048:4096, :].rearrange("r (t two) n -> r t (two n)", two=2)

            exp_act, mask_dve = {}, {}
            headdone_pe, headfinal_pe, rden_act = {}, {}, {}
            avevac_dve = {}
            mask_info = {}

            def emit_mask(mgt):
                if mgt not in mask_info:
                    return
                mr, mkbr, masl = mask_info[mgt]

                def maskf(eng, mgt=mgt, mr=mr, mkbr=mkbr, masl=masl):
                    if mgt == 0:
                        eng.wait_ge(sx.h, mask_sx)
                    eng.wait_ge(sact.h, exp_act[mgt])
                    moff = mr * 896 + (384 - 128 * mkbr)
                    eng.tensor_mul(out=abuf[masl][:], in0=abuf[masl][:],
                                   in1=mask_sb[:, moff:moff + 512]).then_inc(sdve.h, 1)
                mask_dve[mgt] = sdve.inc()
                plan.add("dve", maskf)
            kvf_done, smm_pe = {}, {}
            wo_pe, woevac_dve, wof_cast, wo_cover = {}, {}, {}, {}

            def emit_kv_fetch(qb, h, hh):
                Tq = 512 * (qb + 1)
                ktn = 4 * (qb + 1)
                slot = hh % 2

                def qf(eng, qb=qb, h=h, slot=slot, hh=hh):
                    if hh >= 2:
                        eng.wait_ge(spe.h, headfinal_pe[hh - 2])
                    eng.dma_start(out=qcur[slot][:],
                                  in_=q_dram.ap()[h, :, qb * 512:(qb + 1) * 512]).then_inc(s_kv[slot].h, 16)
                plan.add("sync", qf)
                s_kv[slot].inc(16)

                def kf(eng, h=h, slot=slot, Tq=Tq):
                    ksrc = ccr[:, h * 128:(h + 1) * 128, 0:Tq].transpose([1, 0, 2])
                    kdst = kb_sb[slot][:, 0:8 * Tq].rearrange("p (r t) -> p r t", r=8)
                    eng.dma_start(out=kdst[:, 0:4], in_=ksrc[:, 0:4]).then_inc(s_kv[slot].h, 16)
                    eng.dma_start(out=kdst[:, 4:8], in_=ksrc[:, 4:8]).then_inc(s_kv[slot].h, 16)
                plan.add("sync", kf)
                s_kv[slot].inc(32)

                def vf(eng, h=h, slot=slot, Tq=Tq, ktn=ktn):
                    vsrc = vsrc_all[:, 0:Tq, h * 128:(h + 1) * 128]
                    vsrc2 = vsrc.rearrange("r (kt p) d -> p r kt d", p=128)
                    vdst = vb_sb[slot][:, 0:8 * Tq].rearrange("p (r kt d) -> p r kt d",
                                                              r=8, kt=ktn)
                    for r in range(8):
                        if ktn <= 4:
                            eng.dma_start(out=vdst[:, r:r + 1],
                                          in_=vsrc2[:, r:r + 1]).then_inc(s_kv[slot].h, 16)
                        else:
                            eng.dma_start(out=vdst[:, r:r + 1, 0:4],
                                          in_=vsrc2[:, r:r + 1, 0:4]).then_inc(s_kv[slot].h, 16)
                            eng.dma_start(out=vdst[:, r:r + 1, 4:ktn],
                                          in_=vsrc2[:, r:r + 1, 4:ktn]).then_inc(s_kv[slot].h, 16)
                plan.add("sync", vf)
                s_kv[slot].inc(16 * (8 if ktn <= 4 else 16))
                return s_kv[slot].n

            gt = 0
            avmm_den = {}

            for qb in range(2):
                ktn = 4 * (qb + 1)
                ntile = 8 * ktn
                for h in range(NH):
                    hh = qb * NH + h
                    kvf_done[hh] = emit_kv_fetch(qb, h, hh)

                    def mk_smm(hh, r, kt, ti, gt, sl, ktn):
                        def smm(eng):
                            if ti == 0:
                                eng.wait_ge(s_kv[hh % 2].h, kvf_done[hh])
                                if hh >= 2:
                                    eng.wait_ge(sdve.h, avevac_dve[hh - 2])
                                if hh >= 2:
                                    eng.wait_ge(sact.h, rden_act[hh - 2])
                            if gt >= 2:
                                eng.wait_ge(sact.h, exp_act[gt - 2])
                            off = (r * ktn + kt) * 128
                            eng.matmul(s_ps[sl][:], kb_sb[hh % 2][:, off:off + 128],
                                       qcur[hh % 2][:],
                                       start=True, stop=True).then_inc(spe.h, 1)
                            eng.matmul(dum3_ps[:], ones128[:, 0:32], onescol[:],
                                       start=True, stop=True).then_inc(spe.h, 1)
                        return smm

                    def mk_avmm(hh, ti, gt, asl, ntile, band, r, kt, ktn):
                        def avmm(eng):
                            if band:
                                eng.wait_ge(sdve.h, mask_dve[gt])
                            else:
                                eng.wait_ge(sact.h, exp_act[gt])
                            off = (r * ktn + kt) * 128
                            eng.matmul(av_ps[hh % 2][:], vb_sb[hh % 2][:, off:off + 128],
                                       abuf[asl][:], start=(ti == 0),
                                       stop=(ti == ntile - 1)).then_inc(spe.h, 1)
                            eng.matmul(den_ps[hh % 2][:], ones128[:], abuf[asl][:],
                                       start=(ti == 0),
                                       stop=(ti == ntile - 1)).then_inc(spe.h, 1)
                        return avmm

                    def add_avmm(args):
                        hh_, ti_, gt_, asl_, ntile_, band_, r_, kt_, ktn_ = args
                        spe.inc()
                        avmm_den[gt_] = spe.inc()
                        plan.add("pe", mk_avmm(hh_, ti_, gt_, asl_, ntile_, band_,
                                               r_, kt_, ktn_))

                    def mk_exp(pgt, pasl, ppsl, cover):
                        def expf(eng):
                            eng.wait_ge(spe.h, cover)
                            eng.activation(out=abuf[pasl][:], in_=s_ps[ppsl][:],
                                           func=AF.Exp, bias=cp(C_EBIAS),
                                           scale=cp(C_SCALE)).then_inc(sact.h, 1)
                        return expf

                    pend_av = []
                    for r in range(8):
                        for kt in range(ktn):
                            ti = r * ktn + kt
                            band = (kt >= 4 * qb)
                            kbr = kt - 4 * qb
                            sl = gt % 3
                            asl = gt % 4
                            if band:
                                mask_info[gt] = (r, kbr, asl)

                            spe.inc()
                            smm_pe[gt] = spe.inc()
                            plan.add("pe", mk_smm(hh, r, kt, ti, gt, sl, ktn))

                            exp_act[gt] = sact.inc()
                            plan.add("act", mk_exp(gt, asl, sl, smm_pe[gt]))
                            emit_mask(gt)

                            pend_av.append((hh, ti, gt, asl, ntile, band, r, kt, ktn))
                            if len(pend_av) > 2:
                                add_avmm(pend_av.pop(0))
                            gt += 1

                    for a in pend_av:
                        add_avmm(a)

                    def headdummy(eng):
                        eng.matmul(dum3_ps[:], ones128[:, 0:32], onescol[:],
                                   start=True, stop=True).then_inc(spe.h, 1)
                    headfinal_pe[hh] = spe.inc()
                    plan.add("pe", headdummy)

                    rden_act[hh] = sact.inc(2)

                    def rdenf(eng, hh=hh):
                        eng.wait_ge(spe.h, headfinal_pe[hh])
                        if hh >= 1:
                            eng.wait_ge(sdve.h, avevac_dve[hh - 1])
                        eng.activation(out=rdbc_sb[:], in_=den_ps[hh % 2][:],
                                       func=AF.Ln).then_inc(sact.h, 1)
                        eng.activation(out=rdbc_sb[:], in_=rdbc_sb[:], func=AF.Exp,
                                       scale=cp(C_NONE)).then_inc(sact.h, 1)
                    plan.add("act", rdenf)

                    def avevac(eng, hh=hh, h=h, qb=qb):
                        eng.wait_ge(sact.h, rden_act[hh])
                        if qb == 1 and h == 0:
                            eng.wait_ge(spe.h, wo_pe[(0, 15)])
                        eng.tensor_mul(out=av_sb[:, h * 512:(h + 1) * 512],
                                       in0=av_ps[hh % 2][:],
                                       in1=rdbc_sb[:]).then_inc(sdve.h, 1)
                    avevac_dve[hh] = sdve.inc()
                    plan.add("dve", avevac)

                for g in range(16):
                    def wof(eng, qb=qb, g=g):
                        idx = qb * 16 + g
                        if idx >= 2:
                            pqb, pg = divmod(idx - 2, 16)
                            eng.wait_ge(spe.h, wo_pe[(pqb, pg)])
                        eng.dma_start(out=wob[g % 2][:],
                                      in_=wo_in[:, g * 2048:(g + 1) * 2048]).then_inc(s_w[g % 2].h, 16)
                    plan.add("gpsimd", wof)
                    wof_cast[(qb, g)] = s_w[g % 2].inc(16)

                    def womm(eng, qb=qb, g=g):
                        eng.wait_ge(s_w[g % 2].h, wof_cast[(qb, g)])
                        if g == 0:
                            eng.wait_ge(sdve.h, avevac_dve[qb * NH + NH - 1])
                        if g >= 2:
                            eng.wait_ge(sdve.h, woevac_dve[(qb, g - 2)])
                        ins = None
                        for f in range(NF):
                            ins = eng.matmul(s_ps[g % 2][:],
                                             wob[g % 2][:, f * 128:(f + 1) * 128],
                                             av_sb[:, f * 512:(f + 1) * 512],
                                             start=(f == 0), stop=(f == NF - 1))
                        ins.then_inc(spe.h, 1)
                        eng.matmul(dum3_ps[:], ones128[:, 0:32], onescol[:],
                                   start=True, stop=True).then_inc(spe.h, 1)
                    wo_pe[(qb, g)] = spe.inc()
                    wo_cover[(qb, g)] = spe.inc()
                    plan.add("pe", womm)

                    def woevac(eng, qb=qb, g=g):
                        eng.wait_ge(spe.h, wo_cover[(qb, g)])
                        sl_ = stream[:, g * TOK + qb * 512: g * TOK + qb * 512 + 512]
                        eng.scalar_tensor_tensor(out=sl_, in0=s_ps[g % 2][:],
                                                 scalar=bo_sb[:, g:g + 1], in1=sl_,
                                                 op0=ALU.add, op1=ALU.add).then_inc(sdve.h, 1)
                    woevac_dve[(qb, g)] = sdve.inc()
                    plan.add("dve", woevac)

            with nc.Block("p3") as blk:
                @blk.tensor
                def _(eng):
                    eng.wait_ge(scc.h, 1)
                    plan.run("pe", eng)

                @blk.scalar
                def _(eng):
                    plan.run("act", eng)

                @blk.vector
                def _(eng):
                    plan.run("dve", eng)

                @blk.sync
                def _(eng):
                    eng.wait_ge(scc.h, 1)
                    eng.wait_ge(s_qst[0].h, p2_qst[0])
                    eng.wait_ge(s_qst[1].h, p2_qst[1])
                    plan.run("sync", eng)

                @blk.gpsimd
                def _(eng):
                    plan.run("gpsimd", eng)

        if debug:
            with nc.Block("dbgx2") as blk:
                @blk.sync
                def _(eng):
                    eng.dma_start(out=dbg_x2, in_=stream[:]).then_inc(sout.h, 16)
                    sout.n += 16
                    eng.wait_ge(sout.h, sout.n)

        with nc.sbuf_tensor("xn2", [128, NF * TOK], BF16) as xn2:
            emit_layernorm("p4", False, None, False,
                           lambda f: xn2[:, f * TOK:(f + 1) * TOK], {})

            with contextlib.ExitStack() as ps:
                wgb = [ps.enter_context(nc.sbuf_tensor(f"wgb_{i}", [128, NF * 512], BF16))
                       for i in range(2)]
                wub = [ps.enter_context(nc.sbuf_tensor(f"wub_{i}", [128, NF * 512], BF16))
                       for i in range(2)]
                sgb = [ps.enter_context(nc.sbuf_tensor(f"sgb_{i}", [128, 512], F32))
                       for i in range(2)]
                hst = [ps.enter_context(nc.sbuf_tensor(f"hst_{i}", [128, 512], F32))
                       for i in range(4)]
                g_ps = [ps.enter_context(nc.psum_tensor(f"g_ps{i}", [128, 512], F32))
                        for i in range(2)]
                u_ps = [ps.enter_context(nc.psum_tensor(f"u_ps{i}", [128, 512], F32))
                        for i in range(2)]
                dum5_ps = ps.enter_context(nc.psum_tensor("p5_dum", [32, 1], F32))

                def p5dummy(eng):
                    eng.matmul(dum5_ps[:], ones128[:, 0:32], onescol[:],
                               start=True, stop=True).then_inc(spe.h, 1)

                plan = Plan()
                wgu_fetch, gmm_pe, umm_pe = {}, {}, {}
                silu_act, ht_dve, hw_sync = {}, {}, {}

                wg_r = wg_in.rearrange("p (f n) -> p f n", f=NF)
                wu_r = wu_in.rearrange("p (f n) -> p f n", f=NF)

                for js in range(16):
                    def wguf(eng, js=js):
                        if js >= 2:
                            eng.wait_ge(spe.h, umm_pe[(js - 2, NT - 1)])
                        srcg = wg_r[:, :, js * 512:(js + 1) * 512]
                        srcu = wu_r[:, :, js * 512:(js + 1) * 512]
                        dstg = wgb[js % 2][:].rearrange("p (f n) -> p f n", f=NF)
                        dstu = wub[js % 2][:].rearrange("p (f n) -> p f n", f=NF)
                        eng.dma_start(out=dstg[:, 0:8], in_=srcg[:, 0:8]).then_inc(s_w[js % 2].h, 16)
                        eng.dma_start(out=dstg[:, 8:16], in_=srcg[:, 8:16]).then_inc(s_w[js % 2].h, 16)
                        eng.dma_start(out=dstu[:, 0:8], in_=srcu[:, 0:8]).then_inc(s_w[js % 2].h, 16)
                        eng.dma_start(out=dstu[:, 8:16], in_=srcu[:, 8:16]).then_inc(s_w[js % 2].h, 16)
                        if 2 <= js <= 9:
                            i = js - 2
                            eng.dma_start(out=wd_bf.ap()[:, i * 16384:(i + 1) * 16384],
                                          in_=wd_in[:, i * 16384:(i + 1) * 16384]).then_inc(s_wdc.h, 16)
                    plan.add("gpsimd", wguf)
                    wgu_fetch[js] = s_w[js % 2].inc(64)
                    if 2 <= js <= 9:
                        s_wdc.inc(16)

                    for t in range(NT):
                        idx = js * NT + t

                        def gmm(eng, js=js, t=t, idx=idx):
                            eng.wait_ge(s_w[js % 2].h, wgu_fetch[js])
                            if idx >= 2:
                                pj, pt = divmod(idx - 2, NT)
                                eng.wait_ge(sact.h, silu_act[(pj, pt)])
                            ins = None
                            for f in range(NF):
                                ins = eng.matmul(
                                    g_ps[t % 2][:],
                                    xn2[:, f * TOK + t * 128: f * TOK + t * 128 + 128],
                                    wgb[js % 2][:, f * 512:(f + 1) * 512],
                                    start=(f == 0), stop=(f == NF - 1))
                            ins.then_inc(spe.h, 1)
                            p5dummy(eng)
                        spe.inc()
                        gmm_pe[(js, t)] = spe.inc()
                        plan.add("pe", gmm)

                        def umm(eng, js=js, t=t, idx=idx):
                            if idx >= 2:
                                pj, pt = divmod(idx - 2, NT)
                                eng.wait_ge(sdve.h, ht_dve[(pj, pt)])
                            ins = None
                            for f in range(NF):
                                ins = eng.matmul(
                                    u_ps[t % 2][:],
                                    xn2[:, f * TOK + t * 128: f * TOK + t * 128 + 128],
                                    wub[js % 2][:, f * 512:(f + 1) * 512],
                                    start=(f == 0), stop=(f == NF - 1))
                            ins.then_inc(spe.h, 1)
                            p5dummy(eng)
                        spe.inc()
                        umm_pe[(js, t)] = spe.inc()
                        plan.add("pe", umm)

                        def siluf(eng, js=js, t=t, idx=idx):
                            eng.wait_ge(spe.h, gmm_pe[(js, t)])
                            if idx >= 2:
                                pj, pt = divmod(idx - 2, NT)
                                eng.wait_ge(sdve.h, ht_dve[(pj, pt)])
                            eng.activation(out=sgb[t % 2][:], in_=g_ps[t % 2][:],
                                           func=AF.Silu).then_inc(sact.h, 1)
                        silu_act[(js, t)] = sact.inc()
                        plan.add("act", siluf)

                        def htf(eng, js=js, t=t, idx=idx):
                            eng.wait_ge(spe.h, umm_pe[(js, t)])
                            eng.wait_ge(sact.h, silu_act[(js, t)])
                            if idx >= 4:
                                pj, pt = divmod(idx - 4, NT)
                                eng.wait_ge(s_hst[idx % 4].h, hw_sync[(pj, pt)])
                            eng.tensor_mul(out=hst[idx % 4][:], in0=sgb[t % 2][:],
                                           in1=u_ps[t % 2][:]).then_inc(sdve.h, 1)
                        ht_dve[(js, t)] = sdve.inc()
                        plan.add("dve", htf)

                        def hwf(eng, js=js, t=t, idx=idx):
                            eng.wait_ge(sdve.h, ht_dve[(js, t)])
                            eng.dma_start(out=h_dram.ap()[t, :, js * 512:(js + 1) * 512],
                                          in_=hst[idx % 4][:]).then_inc(s_hst[idx % 4].h, 16)
                        plan.add("sync", hwf)
                        hw_sync[(js, t)] = s_hst[idx % 4].inc(16)

                with nc.Block("p5a") as blk:
                    @blk.tensor
                    def _(eng):
                        plan.run("pe", eng)

                    @blk.gpsimd
                    def _(eng):
                        plan.run("gpsimd", eng)

                    @blk.scalar
                    def _(eng):
                        plan.run("act", eng)

                    @blk.vector
                    def _(eng):
                        plan.run("dve", eng)

                    @blk.sync
                    def _(eng):
                        plan.run("sync", eng)
                p5a_hst = [s.n for s in s_hst]

        with contextlib.ExitStack() as ps:
            htile2 = [ps.enter_context(nc.sbuf_tensor(f"htile{i}", [128, II], F32))
                      for i in range(2)]
            hbf2 = [ps.enter_context(nc.sbuf_tensor(f"hbf{i}", [128, II], BF16))
                    for i in range(2)]
            t1 = ps.enter_context(nc.sbuf_tensor("t1", [128, 2048], F32))
            t2 = ps.enter_context(nc.sbuf_tensor("t2", [128, 2048], F32))
            t3 = ps.enter_context(nc.sbuf_tensor("t3", [128, 2048], F32))
            tstage = ps.enter_context(nc.sbuf_tensor("tstage", [128, 2048], BF16))
            tp_ps = [ps.enter_context(nc.psum_tensor(f"tp_ps{i}", [128, 128], BF16))
                     for i in range(6)]
            dum5b_ps = ps.enter_context(nc.psum_tensor("p5b_dum", [32, 1], F32))

            plan = Plan()
            hr_sync, msk_dve, sq_act5, hsw_sync = {}, {}, {}, {}
            tp_pe, tev_n = {}, {}

            def q_(t, k):
                return htile2[t % 2][:, k * 2048:(k + 1) * 2048]

            def qb_(t, k):
                return hbf2[t % 2][:, k * 2048:(k + 1) * 2048]

            for t in range(NT):
                def hrf(eng, t=t):
                    if t == 0:
                        for i in range(4):
                            eng.wait_ge(s_hst[i].h, p5a_hst[i])
                    if t >= 2:
                        eng.wait_ge(spe.h, tp_pe[(t - 2, 63)])
                    eng.dma_start(out=htile2[t % 2][:], in_=h_dram.ap()[t]).then_inc(sh.h, 16)
                plan.add("sync", hrf)
                hr_sync[t] = sh.inc(16)

                def castf(eng, t=t):
                    eng.wait_ge(sh.h, hr_sync[t])
                    eng.tensor_copy(out=hbf2[t % 2][:], in_=htile2[t % 2][:]).then_inc(sdve.h, 1)
                cast_v = sdve.inc()
                plan.add("dve", castf)

                def sqf(eng, t=t, cast_v=cast_v):
                    eng.wait_ge(sdve.h, cast_v)
                    eng.activation(out=htile2[t % 2][:], in_=htile2[t % 2][:],
                                   func=AF.Square).then_inc(sact.h, 1)
                sq_act5[t] = sact.inc()
                plan.add("act", sqf)

                def mskf(eng, t=t):
                    eng.wait_ge(sact.h, sq_act5[t])
                    eng.tensor_tensor(out=t1[:], in0=q_(t, 0), in1=q_(t, 1), op=ALU.min)
                    eng.tensor_tensor(out=t2[:], in0=q_(t, 2), in1=q_(t, 3), op=ALU.min)
                    eng.tensor_max(out=t1[:], in0=t1[:], in1=t2[:])
                    eng.tensor_max(out=t2[:], in0=q_(t, 0), in1=q_(t, 1))
                    eng.tensor_max(out=t3[:], in0=q_(t, 2), in1=q_(t, 3))
                    eng.tensor_tensor(out=t2[:], in0=t2[:], in1=t3[:], op=ALU.min)
                    eng.tensor_max(out=t2[:], in0=t1[:], in1=t2[:])
                    for k in range(4):
                        eng.tensor_tensor(out=q_(t, k), in0=q_(t, k), in1=t2[:], op=ALU.is_ge)
                        ins = eng.tensor_mul(out=qb_(t, k), in0=qb_(t, k), in1=q_(t, k))
                    ins.then_inc(sdve.h, 1)
                msk_dve[t] = sdve.inc()
                plan.add("dve", mskf)

                for ic in range(64):
                    def tpf(eng, t=t, ic=ic):
                        if ic == 0:
                            eng.wait_ge(sdve.h, msk_dve[t])
                            if t >= 1:
                                eng.wait_ge(shs.h, hsw_sync[(t - 1, 7)])
                        if ic >= 6:
                            kind, val = tev_n[(t, ic - 6)]
                            eng.wait_ge(sdve.h if kind == "dve" else sact.h, val)
                        eng.transpose(tp_ps[ic % 6][:],
                                      hbf2[t % 2][:, ic * 128:(ic + 1) * 128],
                                      ident[:]).then_inc(spe.h, 1)
                    tp_pe[(t, ic)] = spe.inc()
                    plan.add("pe", tpf)

                    eng_name = "dve" if ic % 2 == 0 else "act"

                    def tevac(eng, t=t, ic=ic, eng_name=eng_name):
                        eng.wait_ge(spe.h, tp_pe[(t, ic)] + 1)
                        if ic >= 16:
                            eng.wait_ge(shs.h, hsw_sync[(t, ic // 8 - 2)])
                        dst = tstage[:, (ic % 16) * 128:(ic % 16) * 128 + 128]
                        if eng_name == "dve":
                            eng.tensor_copy(out=dst, in_=tp_ps[ic % 6][:]).then_inc(sdve.h, 1)
                        else:
                            eng.activation(out=dst, in_=tp_ps[ic % 6][:],
                                           func=AF.Identity).then_inc(sact.h, 1)
                    if eng_name == "dve":
                        tev_n[(t, ic)] = ("dve", sdve.inc())
                    else:
                        tev_n[(t, ic)] = ("act", sact.inc())
                    plan.add(eng_name, tevac)

                    if ic % 8 == 7:
                        def tstore(eng, t=t, ic8=ic // 8):
                            for icc in range(ic8 * 8, ic8 * 8 + 8):
                                kind, val = tev_n[(t, icc)]
                                eng.wait_ge(sdve.h if kind == "dve" else sact.h, val)
                            src_ = tstage[:, (ic8 % 2) * 1024:(ic8 % 2) * 1024 + 1024]
                            eng.dma_start(
                                out=hs_fm_dram.ap()[ic8 * 8:(ic8 + 1) * 8, :,
                                                    t * 128:(t + 1) * 128].transpose([1, 0, 2]),
                                in_=src_.rearrange("p (i n) -> p i n", i=8)).then_inc(shs.h, 16)
                        plan.add("sync", tstore)
                        hsw_sync[(t, ic // 8)] = shs.inc(16)

                def tpdum(eng, t=t):
                    eng.matmul(dum5b_ps[:], ones128[:, 0:32], onescol[:],
                               start=True, stop=True).then_inc(spe.h, 1)
                spe.inc()
                plan.add("pe", tpdum)
            p5b_shs = shs.n

            with nc.Block("p5b") as blk:
                @blk.tensor
                def _(eng):
                    plan.run("pe", eng)

                @blk.scalar
                def _(eng):
                    plan.run("act", eng)

                @blk.vector
                def _(eng):
                    plan.run("dve", eng)

                @blk.sync
                def _(eng):
                    plan.run("sync", eng)

        with contextlib.ExitStack() as ps:
            hfm = [ps.enter_context(nc.sbuf_tensor(f"hfm_{i}", [128, 512], BF16))
                   for i in range(4)]
            wdb = [ps.enter_context(nc.sbuf_tensor(f"wdb_{i}", [128, 2048], BF16))
                   for i in range(3)]
            y_ps = [ps.enter_context(nc.psum_tensor(f"y_ps{i}", [128, 512], F32))
                    for i in range(8)]

            plan = Plan()
            hs_ts, wd_fetch, dmm_pe, yev_dve = {}, {}, {}, {}
            wd_slot_user = {}

            for half in range(2):
                for gh in range(2):
                    pi = half * 2 + gh
                    for ic in range(64):
                        gi = pi * 64 + ic

                        def hstr(eng, half=half, ic=ic, gi=gi):
                            if gi == 0:
                                eng.wait_ge(shs.h, p5b_shs)
                            if gi >= 4:
                                ppi, pic = divmod(gi - 4, 64)
                                eng.wait_ge(spe.h, dmm_pe[(ppi, pic)])
                            eng.dma_start(
                                out=hfm[ic % 4][:],
                                in_=hs_fm_dram.ap()[ic, :, half * 512:(half + 1) * 512]
                            ).then_inc(s_tp[ic % 4].h, 16)
                        plan.add("sync", hstr)
                        hs_ts[(pi, ic)] = s_tp[ic % 4].inc(16)

                        prev_user = wd_slot_user.get(ic % 3)

                        def wdf(eng, ic=ic, gi=gi, prev_user=prev_user):
                            if gi == 0:
                                eng.wait_ge(s_wdc.h, s_wdc.n)
                            if prev_user is not None:
                                eng.wait_ge(spe.h, dmm_pe[prev_user])
                            eng.dma_start(out=wdb[ic % 3][:],
                                          in_=wd_bf.ap()[:, ic * 2048:(ic + 1) * 2048]).then_inc(s_wd[ic % 3].h, 16)
                        plan.add("gpsimd", wdf)
                        wd_fetch[(pi, ic)] = s_wd[ic % 3].inc(16)
                        wd_slot_user[ic % 3] = (pi, ic)

                        def dmm(eng, gh=gh, ic=ic, pi=pi):
                            eng.wait_ge(s_tp[ic % 4].h, hs_ts[(pi, ic)])
                            eng.wait_ge(s_wd[ic % 3].h, wd_fetch[(pi, ic)])
                            if ic == 0 and pi >= 1:
                                eng.wait_ge(sdve.h, yev_dve[pi - 1])
                            ins = None
                            for gg in range(8):
                                g = gh * 8 + gg
                                ins = eng.matmul(y_ps[gg][:],
                                                 wdb[ic % 3][:, g * 128:(g + 1) * 128],
                                                 hfm[ic % 4][:],
                                                 start=(ic == 0), stop=(ic == 63))
                            ins.then_inc(spe.h, 1)
                        dmm_pe[(pi, ic)] = spe.inc()
                        plan.add("pe", dmm)

                    def yev(eng, half=half, gh=gh, pi=pi):
                        eng.wait_ge(spe.h, dmm_pe[(pi, 63)])
                        ins = None
                        for gg in range(8):
                            g = gh * 8 + gg
                            sl_ = stream[:, g * TOK + half * 512: g * TOK + half * 512 + 512]
                            ins = eng.tensor_add(out=sl_, in0=y_ps[gg][:], in1=sl_)
                        ins.then_inc(sdve.h, 1)
                    yev_dve[pi] = sdve.inc()
                    plan.add("dve", yev)

            with nc.Block("p5c") as blk:
                @blk.tensor
                def _(eng):
                    plan.run("pe", eng)

                @blk.vector
                def _(eng):
                    plan.run("dve", eng)

                @blk.sync
                def _(eng):
                    plan.run("sync", eng)

                @blk.gpsimd
                def _(eng):
                    plan.run("gpsimd", eng)

        if debug:
            with nc.Block("dbgy") as blk:
                @blk.sync
                def _(eng):
                    eng.dma_start(out=dbg_y, in_=stream[:]).then_inc(sout.h, 16)
                    sout.n += 16
                    eng.wait_ge(sout.h, sout.n)

        apply6 = emit_layernorm("p6", False, None, False,
                                lambda f: stream[:, f * TOK:(f + 1) * TOK], {})

        with nc.Block("p6s") as blk:
            @blk.sync
            def _(eng):
                for f in range(NF):
                    eng.wait_ge(sdve.h, apply6[f])
                    eng.dma_start(out=out_t[f * 128:(f + 1) * 128, :],
                                  in_=stream[:, f * TOK:(f + 1) * TOK]).then_inc(sout.h, 16)
                    sout.n += 16
                if debug:
                    eng.dma_start(out=dbg_ccin, in_=cc_in.ap()).then_inc(sout.h, 16)
                    eng.dma_start(out=dbg_q.rearrange("h p t -> (h p) t"),
                                  in_=q_dram.ap().rearrange("h p t -> (h p) t")).then_inc(sout.h, 16)
                    for t in range(NT):
                        eng.dma_start(out=dbg_h[t], in_=h_dram.ap()[t]).then_inc(sout.h, 16)
                    eng.dma_start(out=dbg_hs.rearrange("i p t -> (i p) t"),
                                  in_=hs_fm_dram.ap().rearrange("i p t -> (i p) t")).then_inc(sout.h, 16)
                    sout.n += 16 * (3 + NT)
                eng.wait_ge(sout.h, sout.n)

    return nc


_NC_CACHE = None


def _get_nc():
    global _NC_CACHE
    if _NC_CACHE is None:
        _NC_CACHE = build_program()
    return _NC_CACHE


def kernel(L0_attn_X, L0_attn_ln_g, L0_attn_ln_b, L0_attn_Wq, L0_attn_bq,
           L0_attn_Wk, L0_attn_bk, L0_attn_Wv, L0_attn_bv, L0_attn_Wo,
           L0_attn_bo, L0_ffn_w_gate, L0_ffn_w_up, L0_ffn_w_down):
    X = np.asarray(L0_attn_X)[0]
    Wq = np.asarray(L0_attn_Wq)
    Wk = np.asarray(L0_attn_Wk)
    Wv = np.asarray(L0_attn_Wv)
    Wo = np.asarray(L0_attn_Wo)
    Wg = np.asarray(L0_ffn_w_gate)
    Wu = np.asarray(L0_ffn_w_up)
    Wd = np.asarray(L0_ffn_w_down)

    def lhsT_layout(W):
        n_in = W.shape[0] // 128
        return np.ascontiguousarray(
            W.reshape(n_in, 128, 16, 128).transpose(1, 2, 0, 3).reshape(128, -1))

    def rhs_layout(W):
        return np.ascontiguousarray(
            W.reshape(16, 128, -1).transpose(1, 0, 2).reshape(128, -1))

    perm = 4 * (np.arange(II) % 2048) + np.arange(II) // 2048

    wq_dev = lhsT_layout(Wq)
    wk_dev = lhsT_layout(Wk)
    wo_dev = lhsT_layout(Wo)
    wv_dev = rhs_layout(Wv)
    wg_dev = rhs_layout(Wg[:, perm])
    wu_dev = rhs_layout(Wu[:, perm])
    wd_dev = np.ascontiguousarray(
        Wd[perm, :].reshape(64, 128, 16, 128).transpose(1, 0, 2, 3).reshape(128, -1))

    def colvec(b):
        return np.ascontiguousarray(np.asarray(b).reshape(16, 128).T)

    bq_dev = colvec(L0_attn_bq)
    bk_dev = colvec(L0_attn_bk)
    bo_dev = colvec(L0_attn_bo)
    lng_dev = colvec(L0_attn_ln_g)
    lnb_dev = colvec(L0_attn_ln_b)
    bv_dev = np.ascontiguousarray(np.asarray(L0_attn_bv)[None, :])

    p = np.arange(128)[:, None]
    v = np.arange(896)[None, :]
    in_maps = []
    for c in range(NCORES):
        Xc = np.ascontiguousarray(X[c::8, :].T)
        masks = np.zeros((8, 128, 896), dtype=ml_dtypes.bfloat16)
        for r in range(8):
            delta = 1 if r > c else 0
            masks[r] = ((v - 384 - p) >= delta).astype(ml_dtypes.bfloat16)
        in_maps.append({
            "x": Xc, "wq": wq_dev, "wk": wk_dev, "wv": wv_dev, "wo": wo_dev,
            "wg": wg_dev, "wu": wu_dev, "wd": wd_dev,
            "bq": bq_dev, "bk": bk_dev, "bo": bo_dev, "bvr": bv_dev,
            "lng": lng_dev, "lnb": lnb_dev, "masks": masks,
            "identw": np.eye(128, dtype=ml_dtypes.bfloat16),
        })

    nc = _get_nc()
    res = run_bass_kernel_spmd(nc, in_maps, core_ids=list(range(NCORES)))
    out = np.empty((S, DM), np.float32)
    for c in range(NCORES):
        out[c::8, :] = res.results[c]["out"].T
    return out[None]


# version 21
# speedup vs baseline: 1.2825x; 1.0018x over previous
import contextlib
import numpy as np
import ml_dtypes
import concourse.bass as bass
import concourse.mybir as mybir
from concourse.bass_utils import run_bass_kernel_spmd

F32 = mybir.dt.float32
BF16 = mybir.dt.bfloat16
AF = mybir.ActivationFunctionType
ALU = mybir.AluOpType

NCORES = 8
DM = 2048
NF = 16
NH = 16
S = 8192
TOK = 1024
NT = 8
II = 8192
EPS = 1e-5
SCALE = 1.0 / float(np.sqrt(128.0))
EXP_BIAS = -2.0


class Sem:
    def __init__(self, nc, es, name):
        self.h = es.enter_context(nc.semaphore(name))
        self.n = 0

    def inc(self, k=1):
        self.n += k
        return self.n


class Plan:
    def __init__(self):
        self.items = []

    def add(self, eng, fn):
        self.items.append((eng, fn))

    def run(self, name, engobj):
        for e, fn in self.items:
            if e == name:
                fn(engobj)


def build_program(debug=False):
    nc = bass.Bass("TRN2", target_bir_lowering=False)

    x_in = nc.dram_tensor("x", [DM, TOK], F32, kind="ExternalInput").ap()
    wq_in = nc.dram_tensor("wq", [128, NF * DM], F32, kind="ExternalInput").ap()
    wk_in = nc.dram_tensor("wk", [128, NF * DM], F32, kind="ExternalInput").ap()
    wv_in = nc.dram_tensor("wv", [128, NF * DM], F32, kind="ExternalInput").ap()
    wo_in = nc.dram_tensor("wo", [128, NF * DM], F32, kind="ExternalInput").ap()
    wg_in = nc.dram_tensor("wg", [128, NF * II], F32, kind="ExternalInput").ap()
    wu_in = nc.dram_tensor("wu", [128, NF * II], F32, kind="ExternalInput").ap()
    wd_in = nc.dram_tensor("wd", [128, 64 * DM], F32, kind="ExternalInput").ap()
    bq_in = nc.dram_tensor("bq", [128, 16], F32, kind="ExternalInput").ap()
    bk_in = nc.dram_tensor("bk", [128, 16], F32, kind="ExternalInput").ap()
    bo_in = nc.dram_tensor("bo", [128, 16], F32, kind="ExternalInput").ap()
    bv_in = nc.dram_tensor("bvr", [1, DM], F32, kind="ExternalInput").ap()
    lng_in = nc.dram_tensor("lng", [128, 16], F32, kind="ExternalInput").ap()
    lnb_in = nc.dram_tensor("lnb", [128, 16], F32, kind="ExternalInput").ap()
    masks_in = nc.dram_tensor("masks", [8, 128, 896], BF16, kind="ExternalInput").ap()
    ident_in = nc.dram_tensor("identw", [128, 128], BF16, kind="ExternalInput").ap()
    out_t = nc.dram_tensor("out", [DM, TOK], F32, kind="ExternalOutput").ap()

    cc_in = nc.dram_tensor("cc_in", [4096, 1024], BF16, kind="Internal")
    cc_out = nc.dram_tensor("cc_out", [NCORES * 4096, 1024], BF16, kind="Internal")
    q_dram = nc.dram_tensor("q_dram", [NH, 128, TOK], BF16, kind="Internal")
    h_dram = nc.dram_tensor("h_dram", [NT, 128, II], F32, kind="Internal")
    hs_fm_dram = nc.dram_tensor("hs_fm", [64, 128, TOK], BF16, kind="Internal")
    wd_bf = nc.dram_tensor("wd_bf", [128, 64 * DM], BF16, kind="Internal")
    if debug:
        dbg_xn = nc.dram_tensor("dbg_xn", [128, NF * TOK], BF16, kind="ExternalOutput").ap()
        dbg_x2 = nc.dram_tensor("dbg_x2", [128, NF * TOK], F32, kind="ExternalOutput").ap()
        dbg_ccin = nc.dram_tensor("dbg_ccin", [4096, 1024], BF16, kind="ExternalOutput").ap()
        dbg_q = nc.dram_tensor("dbg_q", [NH, 128, TOK], BF16, kind="ExternalOutput").ap()
        dbg_h = nc.dram_tensor("dbg_h", [NT, 128, II], F32, kind="ExternalOutput").ap()
        dbg_hs = nc.dram_tensor("dbg_hs", [64, 128, TOK], BF16, kind="ExternalOutput").ap()
        dbg_y = nc.dram_tensor("dbg_y", [128, NF * TOK], F32, kind="ExternalOutput").ap()

    es = contextlib.ExitStack()
    with es:
        stream = es.enter_context(nc.sbuf_tensor("stream", [128, NF * TOK], F32))
        cpool = es.enter_context(nc.sbuf_tensor("cpool", [128, 8], F32))
        bq_sb = es.enter_context(nc.sbuf_tensor("bq_sb", [128, 16], F32))
        bk_sb = es.enter_context(nc.sbuf_tensor("bk_sb", [128, 16], F32))
        bo_sb = es.enter_context(nc.sbuf_tensor("bo_sb", [128, 16], F32))
        lng_sb = es.enter_context(nc.sbuf_tensor("lng_sb", [128, 16], F32))
        lnb_sb = es.enter_context(nc.sbuf_tensor("lnb_sb", [128, 16], F32))
        ones128 = es.enter_context(nc.sbuf_tensor("ones128", [128, 128], BF16))
        ident = es.enter_context(nc.sbuf_tensor("ident", [128, 128], BF16))
        onescol = es.enter_context(nc.sbuf_tensor("onescol", [128, 1], BF16))
        onesrow = es.enter_context(nc.sbuf_tensor("onesrow", [1, 128], F32))

        sx = Sem(nc, es, "sx")
        s_w = [Sem(nc, es, f"s_w{i}") for i in range(2)]
        s_st = [Sem(nc, es, f"s_st{i}") for i in range(2)]
        s_vst = [Sem(nc, es, f"s_vst{i}") for i in range(2)]
        s_qst = [Sem(nc, es, f"s_qst{i}") for i in range(2)]
        s_kv = [Sem(nc, es, f"s_kv{i}") for i in range(2)]
        s_hst = [Sem(nc, es, f"s_hst{i}") for i in range(4)]
        s_tp = [Sem(nc, es, f"s_tp{i}") for i in range(4)]
        s_wd = [Sem(nc, es, f"s_wd{i}") for i in range(3)]
        s_wdc = Sem(nc, es, "s_wdc")
        scast = Sem(nc, es, "scast")
        skvst = Sem(nc, es, "skvst")
        scc = Sem(nc, es, "scc")
        skvf = Sem(nc, es, "skvf")
        sh = Sem(nc, es, "sh")
        shs = Sem(nc, es, "shs")
        sout = Sem(nc, es, "sout")
        spe = Sem(nc, es, "spe")
        sact = Sem(nc, es, "sact")
        sdve = Sem(nc, es, "sdve")

        C_SCALE, C_EBIAS, C_EPS, C_NHALF, C_NONE = 0, 1, 2, 3, 4
        CVALS = {C_SCALE: SCALE, C_EBIAS: EXP_BIAS, C_EPS: EPS,
                 C_NHALF: -0.5, C_NONE: -1.0}

        def cp(i):
            return cpool[:, i:i + 1]

        def emit_layernorm(name, src_is_xb, xb_ap, with_gb, out_fn, pre_wait):
            with contextlib.ExitStack() as ps:
                s1_ps = ps.enter_context(nc.psum_tensor(name + "_s1", [128, TOK], F32))
                s2_ps = ps.enter_context(nc.psum_tensor(name + "_s2", [128, TOK], F32))
                dln_ps = ps.enter_context(nc.psum_tensor(name + "_dum", [32, 1], F32))
                xsq = ps.enter_context(nc.sbuf_tensor(name + "_xsq", [128, 2 * TOK], BF16))
                xbf = None
                if not src_is_xb:
                    xbf = ps.enter_context(nc.sbuf_tensor(name + "_xbf", [128, 2 * TOK], BF16))
                mu = ps.enter_context(nc.sbuf_tensor(name + "_mu", [128, TOK], F32))
                rstd = ps.enter_context(nc.sbuf_tensor(name + "_rstd", [128, TOK], F32))
                m2 = ps.enter_context(nc.sbuf_tensor(name + "_m2", [128, TOK], F32))
                ta = ps.enter_context(nc.sbuf_tensor(name + "_ta", [128, TOK], F32))

                plan = Plan()
                cast_dve = {}
                sq_act = {}
                mm_pe = {}
                for f in range(NF):
                    if xbf is not None:
                        def cast_fn(eng, ff=f):
                            if ff >= 2:
                                eng.wait_ge(spe.h, mm_pe[ff - 2])
                            eng.tensor_scalar(
                                out=xbf[:, (ff % 2) * TOK:(ff % 2 + 1) * TOK],
                                in0=stream[:, ff * TOK:(ff + 1) * TOK],
                                scalar1=1.0, scalar2=None,
                                op0=ALU.mult).then_inc(sdve.h, 1)
                        cast_dve[f] = sdve.inc()
                        plan.add("dve", cast_fn)

                    def sq_fn(eng, ff=f):
                        if xbf is not None:
                            eng.wait_ge(sdve.h, cast_dve[ff])
                        elif ff >= 2:
                            eng.wait_ge(spe.h, mm_pe[ff - 2])
                        src = (xbf[:, (ff % 2) * TOK:(ff % 2 + 1) * TOK]
                               if xbf is not None
                               else xb_ap[:, ff * TOK:(ff + 1) * TOK])
                        eng.activation(out=xsq[:, (ff % 2) * TOK:(ff % 2 + 1) * TOK],
                                       in_=src, func=AF.Square).then_inc(sact.h, 1)
                    sq_act[f] = sact.inc()
                    plan.add("act", sq_fn)

                    def mm_fn(eng, ff=f):
                        eng.wait_ge(sact.h, sq_act[ff])
                        src1 = (xbf[:, (ff % 2) * TOK:(ff % 2 + 1) * TOK]
                                if xbf is not None
                                else xb_ap[:, ff * TOK:(ff + 1) * TOK])
                        src2 = xsq[:, (ff % 2) * TOK:(ff % 2 + 1) * TOK]
                        st = (ff == 0)
                        sp = (ff == NF - 1)
                        eng.matmul(s1_ps[:, 0:512], ones128[:], src1[:, 0:512], start=st, stop=sp)
                        eng.matmul(s1_ps[:, 512:1024], ones128[:], src1[:, 512:1024], start=st, stop=sp)
                        eng.matmul(s2_ps[:, 0:512], ones128[:], src2[:, 0:512], start=st, stop=sp)
                        eng.matmul(s2_ps[:, 512:1024], ones128[:], src2[:, 512:1024],
                                   start=st, stop=sp).then_inc(spe.h, 1)
                        if sp:
                            eng.matmul(dln_ps[:], ones128[:, 0:32], onescol[:],
                                       start=True, stop=True).then_inc(spe.h, 1)
                    mm_pe[f] = spe.inc()
                    if f == NF - 1:
                        spe.inc()
                    plan.add("pe", mm_fn)

                stats_done = mm_pe[NF - 1] + 1

                def mu_fn(eng):
                    eng.wait_ge(spe.h, stats_done)
                    eng.tensor_scalar(out=mu[:], in0=s1_ps[:], scalar1=1.0 / 2048.0,
                                      scalar2=None, op0=ALU.mult).then_inc(sdve.h, 1)
                mu_dve = sdve.inc()
                plan.add("dve", mu_fn)

                def musq_fn(eng):
                    eng.wait_ge(sdve.h, mu_dve)
                    eng.activation(out=ta[:], in_=mu[:], func=AF.Square).then_inc(sact.h, 1)
                musq_act = sact.inc()
                plan.add("act", musq_fn)

                def var_fn(eng):
                    eng.wait_ge(sact.h, musq_act)
                    eng.scalar_tensor_tensor(
                        out=rstd[:], in0=s2_ps[:], scalar=1.0 / 2048.0, in1=ta[:],
                        op0=ALU.mult, op1=ALU.subtract).then_inc(sdve.h, 1)
                var_dve = sdve.inc()
                plan.add("dve", var_fn)

                rstd_act = sact.inc(2)

                def rstd_fn(eng):
                    eng.wait_ge(sdve.h, var_dve)
                    eng.activation(out=rstd[:], in_=rstd[:], func=AF.Ln,
                                   bias=cp(C_EPS)).then_inc(sact.h, 1)
                    eng.activation(out=rstd[:], in_=rstd[:], func=AF.Exp,
                                   scale=cp(C_NHALF)).then_inc(sact.h, 1)
                plan.add("act", rstd_fn)

                def m2_fn(eng):
                    eng.wait_ge(sact.h, rstd_act)
                    eng.tensor_mul(out=m2[:], in0=mu[:], in1=rstd[:]).then_inc(sdve.h, 1)
                m2_dve = sdve.inc()
                plan.add("dve", m2_fn)

                apply_dve = {}
                for f in range(NF):
                    def ap_fn(eng, ff=f):
                        if ff == 0:
                            eng.wait_ge(sdve.h, m2_dve)
                        eng.tensor_mul(out=ta[:], in0=stream[:, ff * TOK:(ff + 1) * TOK],
                                       in1=rstd[:])
                        if with_gb:
                            eng.tensor_sub(out=ta[:], in0=ta[:], in1=m2[:])
                            eng.tensor_scalar(out=out_fn(ff), in0=ta[:],
                                              scalar1=lng_sb[:, ff:ff + 1],
                                              scalar2=lnb_sb[:, ff:ff + 1],
                                              op0=ALU.mult, op1=ALU.add).then_inc(sdve.h, 1)
                        else:
                            eng.tensor_sub(out=out_fn(ff), in0=ta[:],
                                           in1=m2[:]).then_inc(sdve.h, 1)
                    apply_dve[f] = sdve.inc()
                    plan.add("dve", ap_fn)

                with nc.Block(name) as blk:
                    @blk.tensor
                    def _(eng):
                        for sem, val in pre_wait.get("pe", []):
                            eng.wait_ge(sem.h, val)
                        plan.run("pe", eng)

                    @blk.scalar
                    def _(eng):
                        for sem, val in pre_wait.get("act", []):
                            eng.wait_ge(sem.h, val)
                        plan.run("act", eng)

                    @blk.vector
                    def _(eng):
                        for sem, val in pre_wait.get("dve", []):
                            eng.wait_ge(sem.h, val)
                        plan.run("dve", eng)
                return apply_dve

        with nc.sbuf_tensor("xb", [128, NF * TOK], BF16) as xb:
            with nc.Block("p0") as blk:
                @blk.sync
                def _(eng):
                    src = x_in.rearrange("(f p) t -> p f t", p=128)
                    dst = stream[:].rearrange("p (f t) -> p f t", f=NF)
                    eng.dma_start(out=dst[:, 0:8], in_=src[:, 0:8]).then_inc(sx.h, 16)
                    eng.dma_start(out=dst[:, 8:16], in_=src[:, 8:16]).then_inc(sx.h, 16)
                    for ap_in, sb in ((bq_in, bq_sb), (bk_in, bk_sb), (bo_in, bo_sb),
                                      (lng_in, lng_sb), (lnb_in, lnb_sb), (ident_in, ident)):
                        eng.dma_start(out=sb[:], in_=ap_in).then_inc(sx.h, 16)

                @blk.gpsimd
                def _(eng):
                    src = x_in.rearrange("(f p) t -> p f t", p=128)
                    dst = xb[:].rearrange("p (f t) -> p f t", f=NF)
                    eng.dma_start(out=dst[:, 0:8], in_=src[:, 0:8]).then_inc(scast.h, 16)
                    eng.dma_start(out=dst[:, 8:16], in_=src[:, 8:16]).then_inc(scast.h, 16)
                    eng.memset(ones128[:], 1.0)
                    eng.memset(onescol[:], 1.0)
                    eng.memset(onesrow[:], 1.0)
                    for i, v in CVALS.items():
                        eng.memset(cpool[:, i:i + 1], v)
            sx.n += 8 * 16
            scast.n += 32
            p0_sx = sx.n
            p0_scast = scast.n

            emit_layernorm("p1", True, xb,
                           True, lambda f: xb[:, f * TOK:(f + 1) * TOK],
                           {"pe": [(scast, p0_scast)],
                            "act": [(scast, p0_scast)],
                            "dve": [(sx, p0_sx)]})

            with contextlib.ExitStack() as ps:
                bv_bc = ps.enter_context(nc.sbuf_tensor("bv_bc", [128, DM], F32))
                bv_row = ps.enter_context(nc.sbuf_tensor("bv_row", [1, DM], F32))
                wbuf = [ps.enter_context(nc.sbuf_tensor(f"w2_{i}", [128, NF * 512], BF16))
                        for i in range(2)]
                kstage = [ps.enter_context(nc.sbuf_tensor(f"kst_{i}", [128, TOK], BF16))
                          for i in range(2)]
                vstage = [ps.enter_context(nc.sbuf_tensor(f"vst_{i}", [128, 512], BF16))
                          for i in range(2)]
                pp = [ps.enter_context(nc.psum_tensor(f"p2_{i}", [128, 512], F32))
                      for i in range(2)]
                bv_ps = ps.enter_context(nc.psum_tensor("p2_bv", [128, DM], F32))
                dum2_ps = ps.enter_context(nc.psum_tensor("p2_dum", [32, 1], F32))

                def p2dummy(eng):
                    eng.matmul(dum2_ps[:], ones128[:, 0:32], onescol[:],
                               start=True, stop=True).then_inc(spe.h, 1)

                plan = Plan()
                cc_in_ap = cc_in.ap()
                cc_in_V = cc_in_ap[2048:4096, :].rearrange("(t two) n -> t (two n)", two=2)

                def bvload_fn(eng):
                    eng.dma_start(out=bv_row[:], in_=bv_in).then_inc(sx.h, 16)
                bv_sx = sx.inc(16)
                plan.add("sync", bvload_fn)

                def bvbc_fn(eng):
                    eng.wait_ge(sx.h, bv_sx)
                    for j in range(3):
                        eng.matmul(bv_ps[:, j * 512:(j + 1) * 512], onesrow[:],
                                   bv_row[:, j * 512:(j + 1) * 512], start=True, stop=True)
                    eng.matmul(bv_ps[:, 1536:2048], onesrow[:],
                               bv_row[:, 1536:2048], start=True, stop=True).then_inc(spe.h, 1)
                    p2dummy(eng)
                spe.inc()
                bv_pe = spe.inc()
                plan.add("pe", bvbc_fn)

                def bvcp_fn(eng):
                    eng.wait_ge(spe.h, bv_pe)
                    eng.tensor_copy(out=bv_bc[:], in_=bv_ps[:]).then_inc(sdve.h, 1)
                bv_dve = sdve.inc()
                plan.add("dve", bvcp_fn)

                kmm_pe, kevac_act, kdma = {}, {}, {}
                vmm_pe, vevac_dve, vdma = {}, {}, {}
                qmm_pe, qevac_act, qdma = {}, {}, {}
                wk_fetch, wv_fetch, wq_fetch = {}, {}, {}

                for g in range(16):
                    def wkf(eng, g=g):
                        if g >= 2:
                            eng.wait_ge(spe.h, kmm_pe[(g - 2, 1)])
                        eng.dma_start(out=wbuf[g % 2][:, 0:2048],
                                      in_=wk_in[:, g * 2048:(g + 1) * 2048]).then_inc(s_w[g % 2].h, 16)
                    plan.add("gpsimd", wkf)
                    wk_fetch[g] = s_w[g % 2].inc(16)

                    for half in range(2):
                        def kmm(eng, g=g, half=half):
                            eng.wait_ge(s_w[g % 2].h, wk_fetch[g])
                            idx = g * 2 + half
                            if idx >= 2:
                                pg, ph = divmod(idx - 2, 2)
                                eng.wait_ge(sact.h, kevac_act[(pg, ph)])
                            ins = None
                            for f in range(NF):
                                ins = eng.matmul(
                                    pp[half][:],
                                    wbuf[g % 2][:, f * 128:(f + 1) * 128],
                                    xb[:, f * TOK + half * 512: f * TOK + half * 512 + 512],
                                    start=(f == 0), stop=(f == NF - 1))
                            ins.then_inc(spe.h, 1)
                            p2dummy(eng)
                        spe.inc()
                        kmm_pe[(g, half)] = spe.inc()
                        plan.add("pe", kmm)

                        def kevac(eng, g=g, half=half):
                            eng.wait_ge(spe.h, kmm_pe[(g, half)])
                            if g >= 2 and half == 0:
                                eng.wait_ge(s_st[g % 2].h, kdma[g - 2])
                            eng.activation(out=kstage[g % 2][:, half * 512:half * 512 + 512],
                                           in_=pp[half][:], func=AF.Identity,
                                           bias=bk_sb[:, g:g + 1]).then_inc(sact.h, 1)
                        kevac_act[(g, half)] = sact.inc()
                        plan.add("act", kevac)

                    def kdma_fn(eng, g=g):
                        eng.wait_ge(sact.h, kevac_act[(g, 1)])
                        eng.dma_start(out=cc_in_ap[g * 128:(g + 1) * 128, :],
                                      in_=kstage[g % 2][:]).then_inc(s_st[g % 2].h, 16)
                    plan.add("sync", kdma_fn)
                    kdma[g] = s_st[g % 2].inc(16)

                wv_r = wv_in.rearrange("p (f n) -> p f n", f=NF)
                for j in range(4):
                    def wvf(eng, j=j):
                        if j >= 2:
                            eng.wait_ge(spe.h, vmm_pe[(j - 2, NT - 1)])
                        src = wv_r[:, :, j * 512:(j + 1) * 512]
                        dst = wbuf[j % 2][:].rearrange("p (f n) -> p f n", f=NF)
                        eng.dma_start(out=dst[:, 0:8], in_=src[:, 0:8]).then_inc(s_w[j % 2].h, 16)
                        eng.dma_start(out=dst[:, 8:16], in_=src[:, 8:16]).then_inc(s_w[j % 2].h, 16)
                    plan.add("gpsimd", wvf)
                    wv_fetch[j] = s_w[j % 2].inc(32)

                    for t in range(NT):
                        def vmm(eng, j=j, t=t):
                            eng.wait_ge(s_w[j % 2].h, wv_fetch[j])
                            idx = j * NT + t
                            if idx < 2:
                                eng.wait_ge(sact.h, kevac_act[(15, 1)])
                            else:
                                pj, pt = divmod(idx - 2, NT)
                                eng.wait_ge(sdve.h, vevac_dve[(pj, pt)])
                            ins = None
                            for f in range(NF):
                                ins = eng.matmul(
                                    pp[t % 2][:],
                                    xb[:, f * TOK + t * 128: f * TOK + t * 128 + 128],
                                    wbuf[j % 2][:, f * 512:(f + 1) * 512],
                                    start=(f == 0), stop=(f == NF - 1))
                            ins.then_inc(spe.h, 1)
                            p2dummy(eng)
                        spe.inc()
                        vmm_pe[(j, t)] = spe.inc()
                        plan.add("pe", vmm)

                        def vevac(eng, j=j, t=t):
                            eng.wait_ge(spe.h, vmm_pe[(j, t)])
                            idx = j * NT + t
                            if idx == 0:
                                eng.wait_ge(sdve.h, bv_dve)
                            if idx >= 2:
                                pj, pt = divmod(idx - 2, NT)
                                eng.wait_ge(s_vst[t % 2].h, vdma[(pj, pt)])
                            eng.tensor_add(out=vstage[t % 2][:], in0=pp[t % 2][:],
                                           in1=bv_bc[:, j * 512:(j + 1) * 512]).then_inc(sdve.h, 1)
                        vevac_dve[(j, t)] = sdve.inc()
                        plan.add("dve", vevac)

                        def vdma_fn(eng, j=j, t=t):
                            eng.wait_ge(sdve.h, vevac_dve[(j, t)])
                            eng.dma_start(
                                out=cc_in_V[t * 128:(t + 1) * 128, j * 512:(j + 1) * 512],
                                in_=vstage[t % 2][:]).then_inc(s_vst[t % 2].h, 16)
                        plan.add("sync", vdma_fn)
                        vdma[(j, t)] = s_vst[t % 2].inc(16)

                staged_vals = [(s_st[0], s_st[0].n), (s_st[1], s_st[1].n),
                               (s_vst[0], s_vst[0].n), (s_vst[1], s_vst[1].n)]

                def coll_fn(eng):
                    for sem, val in staged_vals:
                        eng.wait_ge(sem.h, val)
                    eng.collective_compute(
                        "AllGather", ALU.bypass,
                        replica_groups=[list(range(NCORES))],
                        ins=[cc_in.ap().opt()],
                        outs=[cc_out.ap().opt()],
                    ).then_inc(scc.h, 1)
                plan.add("gpsimd", coll_fn)
                scc.inc()

                for g in range(16):
                    def wqf(eng, g=g):
                        if g >= 2:
                            eng.wait_ge(spe.h, qmm_pe[(g - 2, 1)])
                        else:
                            eng.wait_ge(spe.h, vmm_pe[(3, NT - 1)])
                        eng.dma_start(out=wbuf[g % 2][:, 0:2048],
                                      in_=wq_in[:, g * 2048:(g + 1) * 2048]).then_inc(s_w[g % 2].h, 16)
                    plan.add("gpsimd", wqf)
                    wq_fetch[g] = s_w[g % 2].inc(16)

                    for half in range(2):
                        def qmm(eng, g=g, half=half):
                            eng.wait_ge(s_w[g % 2].h, wq_fetch[g])
                            idx = g * 2 + half
                            if idx < 2:
                                eng.wait_ge(sdve.h, vevac_dve[(3, NT - 1)])
                            else:
                                pg, ph = divmod(idx - 2, 2)
                                eng.wait_ge(sact.h, qevac_act[(pg, ph)])
                            ins = None
                            for f in range(NF):
                                ins = eng.matmul(
                                    pp[half][:],
                                    wbuf[g % 2][:, f * 128:(f + 1) * 128],
                                    xb[:, f * TOK + half * 512: f * TOK + half * 512 + 512],
                                    start=(f == 0), stop=(f == NF - 1))
                            ins.then_inc(spe.h, 1)
                            p2dummy(eng)
                        spe.inc()
                        qmm_pe[(g, half)] = spe.inc()
                        plan.add("pe", qmm)

                        def qevac(eng, g=g, half=half):
                            eng.wait_ge(spe.h, qmm_pe[(g, half)])
                            if half == 0:
                                if g < 2:
                                    eng.wait_ge(s_st[g % 2].h, kdma[14 + g])
                                else:
                                    eng.wait_ge(s_qst[g % 2].h, qdma[g - 2])
                            eng.activation(
                                out=kstage[g % 2][:, half * 512:half * 512 + 512],
                                in_=pp[half][:], func=AF.Identity,
                                bias=bq_sb[:, g:g + 1]).then_inc(sact.h, 1)
                        qevac_act[(g, half)] = sact.inc()
                        plan.add("act", qevac)

                    def qdma_fn(eng, g=g):
                        eng.wait_ge(sact.h, qevac_act[(g, 1)])
                        eng.dma_start(out=q_dram.ap()[g],
                                      in_=kstage[g % 2][:]).then_inc(s_qst[g % 2].h, 16)
                    plan.add("sync", qdma_fn)
                    qdma[g] = s_qst[g % 2].inc(16)

                with nc.Block("p2") as blk:
                    @blk.tensor
                    def _(eng):
                        plan.run("pe", eng)

                    @blk.scalar
                    def _(eng):
                        plan.run("act", eng)

                    @blk.vector
                    def _(eng):
                        plan.run("dve", eng)

                    @blk.sync
                    def _(eng):
                        plan.run("sync", eng)

                    @blk.gpsimd
                    def _(eng):
                        plan.run("gpsimd", eng)
            p2_qst = (s_qst[0].n, s_qst[1].n)

            if debug:
                with nc.Block("dbgxn") as blk:
                    @blk.sync
                    def _(eng):
                        eng.dma_start(out=dbg_xn, in_=xb[:]).then_inc(sout.h, 16)
                        sout.n += 16
                        eng.wait_ge(sout.h, sout.n)

        with contextlib.ExitStack() as ps:
            kb_sb = [ps.enter_context(nc.sbuf_tensor(f"kb_{i}", [128, 8192], BF16))
                     for i in range(2)]
            vb_sb = [ps.enter_context(nc.sbuf_tensor(f"vb_{i}", [128, 8192], BF16))
                     for i in range(2)]
            qcur = [ps.enter_context(nc.sbuf_tensor(f"qc_{i}", [128, 512], BF16))
                    for i in range(2)]
            abuf = [ps.enter_context(nc.sbuf_tensor(f"ab_{i}", [128, 512], BF16))
                    for i in range(4)]
            av_sb = ps.enter_context(nc.sbuf_tensor("av_sb", [128, NH * 512], BF16))
            mask_sb = ps.enter_context(nc.sbuf_tensor("mask_sb", [128, 8 * 896], BF16))
            rdbc_sb = ps.enter_context(nc.sbuf_tensor("rdbc_sb", [128, 512], F32))
            wob = [ps.enter_context(nc.sbuf_tensor(f"wob_{i}", [128, 2048], BF16))
                   for i in range(2)]

            s_ps = [ps.enter_context(nc.psum_tensor(f"s_ps{i}", [128, 512], F32))
                    for i in range(3)]
            av_ps = [ps.enter_context(nc.psum_tensor(f"av_ps{i}", [128, 512], F32))
                     for i in range(2)]
            den_ps = [ps.enter_context(nc.psum_tensor(f"den_ps{i}", [128, 512], F32))
                      for i in range(2)]
            dum3_ps = ps.enter_context(nc.psum_tensor("dum3_ps", [32, 1], F32))

            plan = Plan()

            def mload(eng):
                src = masks_in.transpose([1, 0, 2])
                dst = mask_sb[:].rearrange("p (r n) -> p r n", r=8)
                eng.dma_start(out=dst[:, 0:4], in_=src[:, 0:4]).then_inc(sx.h, 16)
                eng.dma_start(out=dst[:, 4:8], in_=src[:, 4:8]).then_inc(sx.h, 16)
            plan.add("sync", mload)
            mask_sx = sx.inc(32)

            ccr = cc_out.ap().rearrange("(r x) n -> r x n", x=4096)
            vsrc_all = ccr[:, 2048:4096, :].rearrange("r (t two) n -> r t (two n)", two=2)

            exp_act, mask_dve = {}, {}
            headdone_pe, headfinal_pe, rden_act = {}, {}, {}
            avevac_dve = {}
            mask_info = {}

            def emit_mask(mgt):
                if mgt not in mask_info:
                    return
                mr, mkbr, masl = mask_info[mgt]

                def maskf(eng, mgt=mgt, mr=mr, mkbr=mkbr, masl=masl):
                    if mgt == 0:
                        eng.wait_ge(sx.h, mask_sx)
                    eng.wait_ge(sact.h, exp_act[mgt])
                    moff = mr * 896 + (384 - 128 * mkbr)
                    eng.tensor_mul(out=abuf[masl][:], in0=abuf[masl][:],
                                   in1=mask_sb[:, moff:moff + 512]).then_inc(sdve.h, 1)
                mask_dve[mgt] = sdve.inc()
                plan.add("dve", maskf)
            kvf_done, smm_pe = {}, {}
            wo_pe, woevac_dve, wof_cast, wo_cover = {}, {}, {}, {}

            def emit_kv_fetch(qb, h, hh):
                Tq = 512 * (qb + 1)
                ktn = 4 * (qb + 1)
                slot = hh % 2

                def qf(eng, qb=qb, h=h, slot=slot, hh=hh):
                    if hh >= 2:
                        eng.wait_ge(spe.h, headfinal_pe[hh - 2])
                    eng.dma_start(out=qcur[slot][:],
                                  in_=q_dram.ap()[h, :, qb * 512:(qb + 1) * 512]).then_inc(s_kv[slot].h, 16)
                plan.add("sync", qf)
                s_kv[slot].inc(16)

                def kf(eng, h=h, slot=slot, Tq=Tq):
                    ksrc = ccr[:, h * 128:(h + 1) * 128, 0:Tq].transpose([1, 0, 2])
                    kdst = kb_sb[slot][:, 0:8 * Tq].rearrange("p (r t) -> p r t", r=8)
                    eng.dma_start(out=kdst[:, 0:4], in_=ksrc[:, 0:4]).then_inc(s_kv[slot].h, 16)
                    eng.dma_start(out=kdst[:, 4:8], in_=ksrc[:, 4:8]).then_inc(s_kv[slot].h, 16)
                plan.add("sync", kf)
                s_kv[slot].inc(32)

                def vf(eng, h=h, slot=slot, Tq=Tq, ktn=ktn):
                    vsrc = vsrc_all[:, 0:Tq, h * 128:(h + 1) * 128]
                    vsrc2 = vsrc.rearrange("r (kt p) d -> p r kt d", p=128)
                    vdst = vb_sb[slot][:, 0:8 * Tq].rearrange("p (r kt d) -> p r kt d",
                                                              r=8, kt=ktn)
                    for r in range(8):
                        if ktn <= 4:
                            eng.dma_start(out=vdst[:, r:r + 1],
                                          in_=vsrc2[:, r:r + 1]).then_inc(s_kv[slot].h, 16)
                        else:
                            eng.dma_start(out=vdst[:, r:r + 1, 0:4],
                                          in_=vsrc2[:, r:r + 1, 0:4]).then_inc(s_kv[slot].h, 16)
                            eng.dma_start(out=vdst[:, r:r + 1, 4:ktn],
                                          in_=vsrc2[:, r:r + 1, 4:ktn]).then_inc(s_kv[slot].h, 16)
                plan.add("sync", vf)
                s_kv[slot].inc(16 * (8 if ktn <= 4 else 16))
                return s_kv[slot].n

            gt = 0
            avmm_den = {}

            for qb in range(2):
                ktn = 4 * (qb + 1)
                ntile = 8 * ktn
                for h in range(NH):
                    hh = qb * NH + h
                    kvf_done[hh] = emit_kv_fetch(qb, h, hh)

                    def mk_smm(hh, r, kt, ti, gt, sl, ktn):
                        def smm(eng):
                            if ti == 0:
                                eng.wait_ge(s_kv[hh % 2].h, kvf_done[hh])
                                if hh >= 2:
                                    eng.wait_ge(sdve.h, avevac_dve[hh - 2])
                                if hh >= 2:
                                    eng.wait_ge(sact.h, rden_act[hh - 2])
                            if gt >= 2:
                                eng.wait_ge(sact.h, exp_act[gt - 2])
                            off = (r * ktn + kt) * 128
                            eng.matmul(s_ps[sl][:], kb_sb[hh % 2][:, off:off + 128],
                                       qcur[hh % 2][:],
                                       start=True, stop=True).then_inc(spe.h, 1)
                            eng.matmul(dum3_ps[:], ones128[:, 0:32], onescol[:],
                                       start=True, stop=True).then_inc(spe.h, 1)
                        return smm

                    def mk_avmm(hh, ti, gt, asl, ntile, band, r, kt, ktn):
                        def avmm(eng):
                            if band:
                                eng.wait_ge(sdve.h, mask_dve[gt])
                            else:
                                eng.wait_ge(sact.h, exp_act[gt])
                            off = (r * ktn + kt) * 128
                            eng.matmul(av_ps[hh % 2][:], vb_sb[hh % 2][:, off:off + 128],
                                       abuf[asl][:], start=(ti == 0),
                                       stop=(ti == ntile - 1)).then_inc(spe.h, 1)
                            eng.matmul(den_ps[hh % 2][:], ones128[:], abuf[asl][:],
                                       start=(ti == 0),
                                       stop=(ti == ntile - 1)).then_inc(spe.h, 1)
                        return avmm

                    def add_avmm(args):
                        hh_, ti_, gt_, asl_, ntile_, band_, r_, kt_, ktn_ = args
                        spe.inc()
                        avmm_den[gt_] = spe.inc()
                        plan.add("pe", mk_avmm(hh_, ti_, gt_, asl_, ntile_, band_,
                                               r_, kt_, ktn_))

                    def mk_exp(pgt, pasl, ppsl, cover):
                        def expf(eng):
                            eng.wait_ge(spe.h, cover)
                            eng.activation(out=abuf[pasl][:], in_=s_ps[ppsl][:],
                                           func=AF.Exp, bias=cp(C_EBIAS),
                                           scale=cp(C_SCALE)).then_inc(sact.h, 1)
                        return expf

                    pend_av = []
                    for r in range(8):
                        for kt in range(ktn):
                            ti = r * ktn + kt
                            band = (kt >= 4 * qb)
                            kbr = kt - 4 * qb
                            sl = gt % 3
                            asl = gt % 4
                            if band:
                                mask_info[gt] = (r, kbr, asl)

                            spe.inc()
                            smm_pe[gt] = spe.inc()
                            plan.add("pe", mk_smm(hh, r, kt, ti, gt, sl, ktn))

                            exp_act[gt] = sact.inc()
                            plan.add("act", mk_exp(gt, asl, sl, smm_pe[gt]))
                            emit_mask(gt)

                            pend_av.append((hh, ti, gt, asl, ntile, band, r, kt, ktn))
                            if len(pend_av) > 2:
                                add_avmm(pend_av.pop(0))
                            gt += 1

                    for a in pend_av:
                        add_avmm(a)

                    def headdummy(eng):
                        eng.matmul(dum3_ps[:], ones128[:, 0:32], onescol[:],
                                   start=True, stop=True).then_inc(spe.h, 1)
                    headfinal_pe[hh] = spe.inc()
                    plan.add("pe", headdummy)

                    rden_act[hh] = sact.inc(2)

                    def rdenf(eng, hh=hh):
                        eng.wait_ge(spe.h, headfinal_pe[hh])
                        if hh >= 1:
                            eng.wait_ge(sdve.h, avevac_dve[hh - 1])
                        eng.activation(out=rdbc_sb[:], in_=den_ps[hh % 2][:],
                                       func=AF.Ln).then_inc(sact.h, 1)
                        eng.activation(out=rdbc_sb[:], in_=rdbc_sb[:], func=AF.Exp,
                                       scale=cp(C_NONE)).then_inc(sact.h, 1)
                    plan.add("act", rdenf)

                    def avevac(eng, hh=hh, h=h, qb=qb):
                        eng.wait_ge(sact.h, rden_act[hh])
                        if qb == 1 and h == 0:
                            eng.wait_ge(spe.h, wo_pe[(0, 15)])
                        eng.tensor_mul(out=av_sb[:, h * 512:(h + 1) * 512],
                                       in0=av_ps[hh % 2][:],
                                       in1=rdbc_sb[:]).then_inc(sdve.h, 1)
                    avevac_dve[hh] = sdve.inc()
                    plan.add("dve", avevac)

                for g in range(16):
                    def wof(eng, qb=qb, g=g):
                        idx = qb * 16 + g
                        if idx >= 2:
                            pqb, pg = divmod(idx - 2, 16)
                            eng.wait_ge(spe.h, wo_pe[(pqb, pg)])
                        eng.dma_start(out=wob[g % 2][:],
                                      in_=wo_in[:, g * 2048:(g + 1) * 2048]).then_inc(s_w[g % 2].h, 16)
                    plan.add("gpsimd", wof)
                    wof_cast[(qb, g)] = s_w[g % 2].inc(16)

                    def womm(eng, qb=qb, g=g):
                        eng.wait_ge(s_w[g % 2].h, wof_cast[(qb, g)])
                        if g == 0:
                            eng.wait_ge(sdve.h, avevac_dve[qb * NH + NH - 1])
                        if g >= 2:
                            eng.wait_ge(sdve.h, woevac_dve[(qb, g - 2)])
                        ins = None
                        for f in range(NF):
                            ins = eng.matmul(s_ps[g % 2][:],
                                             wob[g % 2][:, f * 128:(f + 1) * 128],
                                             av_sb[:, f * 512:(f + 1) * 512],
                                             start=(f == 0), stop=(f == NF - 1))
                        ins.then_inc(spe.h, 1)
                        eng.matmul(dum3_ps[:], ones128[:, 0:32], onescol[:],
                                   start=True, stop=True).then_inc(spe.h, 1)
                    wo_pe[(qb, g)] = spe.inc()
                    wo_cover[(qb, g)] = spe.inc()
                    plan.add("pe", womm)

                    def woevac(eng, qb=qb, g=g):
                        eng.wait_ge(spe.h, wo_cover[(qb, g)])
                        sl_ = stream[:, g * TOK + qb * 512: g * TOK + qb * 512 + 512]
                        eng.scalar_tensor_tensor(out=sl_, in0=s_ps[g % 2][:],
                                                 scalar=bo_sb[:, g:g + 1], in1=sl_,
                                                 op0=ALU.add, op1=ALU.add).then_inc(sdve.h, 1)
                    woevac_dve[(qb, g)] = sdve.inc()
                    plan.add("dve", woevac)

            with nc.Block("p3") as blk:
                @blk.tensor
                def _(eng):
                    eng.wait_ge(scc.h, 1)
                    plan.run("pe", eng)

                @blk.scalar
                def _(eng):
                    plan.run("act", eng)

                @blk.vector
                def _(eng):
                    plan.run("dve", eng)

                @blk.sync
                def _(eng):
                    eng.wait_ge(scc.h, 1)
                    eng.wait_ge(s_qst[0].h, p2_qst[0])
                    eng.wait_ge(s_qst[1].h, p2_qst[1])
                    plan.run("sync", eng)

                @blk.gpsimd
                def _(eng):
                    plan.run("gpsimd", eng)

        if debug:
            with nc.Block("dbgx2") as blk:
                @blk.sync
                def _(eng):
                    eng.dma_start(out=dbg_x2, in_=stream[:]).then_inc(sout.h, 16)
                    sout.n += 16
                    eng.wait_ge(sout.h, sout.n)

        with nc.sbuf_tensor("xn2", [128, NF * TOK], BF16) as xn2:
            emit_layernorm("p4", False, None, False,
                           lambda f: xn2[:, f * TOK:(f + 1) * TOK], {})

            with contextlib.ExitStack() as ps:
                wgb = [ps.enter_context(nc.sbuf_tensor(f"wgb_{i}", [128, NF * 512], BF16))
                       for i in range(2)]
                wub = [ps.enter_context(nc.sbuf_tensor(f"wub_{i}", [128, NF * 512], BF16))
                       for i in range(2)]
                sgb = [ps.enter_context(nc.sbuf_tensor(f"sgb_{i}", [128, 512], F32))
                       for i in range(2)]
                hst = [ps.enter_context(nc.sbuf_tensor(f"hst_{i}", [128, 512], F32))
                       for i in range(4)]
                g_ps = [ps.enter_context(nc.psum_tensor(f"g_ps{i}", [128, 512], F32))
                        for i in range(2)]
                u_ps = [ps.enter_context(nc.psum_tensor(f"u_ps{i}", [128, 512], F32))
                        for i in range(2)]
                dum5_ps = ps.enter_context(nc.psum_tensor("p5_dum", [32, 1], F32))

                def p5dummy(eng):
                    eng.matmul(dum5_ps[:], ones128[:, 0:32], onescol[:],
                               start=True, stop=True).then_inc(spe.h, 1)

                plan = Plan()
                wgu_fetch, gmm_pe, umm_pe = {}, {}, {}
                silu_act, ht_dve, hw_sync = {}, {}, {}

                wg_r = wg_in.rearrange("p (f n) -> p f n", f=NF)
                wu_r = wu_in.rearrange("p (f n) -> p f n", f=NF)

                for js in range(16):
                    def wguf(eng, js=js):
                        if js >= 2:
                            eng.wait_ge(spe.h, umm_pe[(js - 2, NT - 1)])
                        srcg = wg_r[:, :, js * 512:(js + 1) * 512]
                        srcu = wu_r[:, :, js * 512:(js + 1) * 512]
                        dstg = wgb[js % 2][:].rearrange("p (f n) -> p f n", f=NF)
                        dstu = wub[js % 2][:].rearrange("p (f n) -> p f n", f=NF)
                        eng.dma_start(out=dstg[:, 0:8], in_=srcg[:, 0:8]).then_inc(s_w[js % 2].h, 16)
                        eng.dma_start(out=dstg[:, 8:16], in_=srcg[:, 8:16]).then_inc(s_w[js % 2].h, 16)
                        eng.dma_start(out=dstu[:, 0:8], in_=srcu[:, 0:8]).then_inc(s_w[js % 2].h, 16)
                        eng.dma_start(out=dstu[:, 8:16], in_=srcu[:, 8:16]).then_inc(s_w[js % 2].h, 16)
                        if 2 <= js <= 9:
                            i = js - 2
                            eng.dma_start(out=wd_bf.ap()[:, i * 16384:(i + 1) * 16384],
                                          in_=wd_in[:, i * 16384:(i + 1) * 16384]).then_inc(s_wdc.h, 16)
                    plan.add("gpsimd", wguf)
                    wgu_fetch[js] = s_w[js % 2].inc(64)
                    if 2 <= js <= 9:
                        s_wdc.inc(16)

                    for t in range(NT):
                        idx = js * NT + t

                        def gmm(eng, js=js, t=t, idx=idx):
                            eng.wait_ge(s_w[js % 2].h, wgu_fetch[js])
                            if idx >= 2:
                                pj, pt = divmod(idx - 2, NT)
                                eng.wait_ge(sact.h, silu_act[(pj, pt)])
                            ins = None
                            for f in range(NF):
                                ins = eng.matmul(
                                    g_ps[t % 2][:],
                                    xn2[:, f * TOK + t * 128: f * TOK + t * 128 + 128],
                                    wgb[js % 2][:, f * 512:(f + 1) * 512],
                                    start=(f == 0), stop=(f == NF - 1))
                            ins.then_inc(spe.h, 1)
                            p5dummy(eng)
                        spe.inc()
                        gmm_pe[(js, t)] = spe.inc()
                        plan.add("pe", gmm)

                        def umm(eng, js=js, t=t, idx=idx):
                            if idx >= 2:
                                pj, pt = divmod(idx - 2, NT)
                                eng.wait_ge(sdve.h, ht_dve[(pj, pt)])
                            ins = None
                            for f in range(NF):
                                ins = eng.matmul(
                                    u_ps[t % 2][:],
                                    xn2[:, f * TOK + t * 128: f * TOK + t * 128 + 128],
                                    wub[js % 2][:, f * 512:(f + 1) * 512],
                                    start=(f == 0), stop=(f == NF - 1))
                            ins.then_inc(spe.h, 1)
                            p5dummy(eng)
                        spe.inc()
                        umm_pe[(js, t)] = spe.inc()
                        plan.add("pe", umm)

                        def siluf(eng, js=js, t=t, idx=idx):
                            eng.wait_ge(spe.h, gmm_pe[(js, t)])
                            if idx >= 2:
                                pj, pt = divmod(idx - 2, NT)
                                eng.wait_ge(sdve.h, ht_dve[(pj, pt)])
                            eng.activation(out=sgb[t % 2][:], in_=g_ps[t % 2][:],
                                           func=AF.Silu).then_inc(sact.h, 1)
                        silu_act[(js, t)] = sact.inc()
                        plan.add("act", siluf)

                        def htf(eng, js=js, t=t, idx=idx):
                            eng.wait_ge(spe.h, umm_pe[(js, t)])
                            eng.wait_ge(sact.h, silu_act[(js, t)])
                            if idx >= 4:
                                pj, pt = divmod(idx - 4, NT)
                                eng.wait_ge(s_hst[idx % 4].h, hw_sync[(pj, pt)])
                            eng.tensor_mul(out=hst[idx % 4][:], in0=sgb[t % 2][:],
                                           in1=u_ps[t % 2][:]).then_inc(sdve.h, 1)
                        ht_dve[(js, t)] = sdve.inc()
                        plan.add("dve", htf)

                        def hwf(eng, js=js, t=t, idx=idx):
                            eng.wait_ge(sdve.h, ht_dve[(js, t)])
                            eng.dma_start(out=h_dram.ap()[t, :, js * 512:(js + 1) * 512],
                                          in_=hst[idx % 4][:]).then_inc(s_hst[idx % 4].h, 16)
                        plan.add("sync", hwf)
                        hw_sync[(js, t)] = s_hst[idx % 4].inc(16)

                with nc.Block("p5a") as blk:
                    @blk.tensor
                    def _(eng):
                        plan.run("pe", eng)

                    @blk.gpsimd
                    def _(eng):
                        plan.run("gpsimd", eng)

                    @blk.scalar
                    def _(eng):
                        plan.run("act", eng)

                    @blk.vector
                    def _(eng):
                        plan.run("dve", eng)

                    @blk.sync
                    def _(eng):
                        plan.run("sync", eng)
                p5a_hst = [s.n for s in s_hst]

        with contextlib.ExitStack() as ps:
            htile2 = [ps.enter_context(nc.sbuf_tensor(f"htile{i}", [128, II], F32))
                      for i in range(2)]
            hbf2 = [ps.enter_context(nc.sbuf_tensor(f"hbf{i}", [128, II], BF16))
                    for i in range(2)]
            t1 = ps.enter_context(nc.sbuf_tensor("t1", [128, 2048], F32))
            t2 = ps.enter_context(nc.sbuf_tensor("t2", [128, 2048], F32))
            t3 = ps.enter_context(nc.sbuf_tensor("t3", [128, 2048], F32))
            tstage = ps.enter_context(nc.sbuf_tensor("tstage", [128, 2048], BF16))
            tp_ps = [ps.enter_context(nc.psum_tensor(f"tp_ps{i}", [128, 128], BF16))
                     for i in range(6)]
            dum5b_ps = ps.enter_context(nc.psum_tensor("p5b_dum", [32, 1], F32))

            plan = Plan()
            hr_sync, msk_dve, sq_act5, hsw_sync = {}, {}, {}, {}
            tp_pe, tev_n = {}, {}

            def q_(t, k):
                return htile2[t % 2][:, k * 2048:(k + 1) * 2048]

            def qb_(t, k):
                return hbf2[t % 2][:, k * 2048:(k + 1) * 2048]

            for t in range(NT):
                def hrf(eng, t=t):
                    if t == 0:
                        for i in range(4):
                            eng.wait_ge(s_hst[i].h, p5a_hst[i])
                    if t >= 2:
                        eng.wait_ge(spe.h, tp_pe[(t - 2, 63)])
                    eng.dma_start(out=htile2[t % 2][:], in_=h_dram.ap()[t]).then_inc(sh.h, 16)
                plan.add("sync", hrf)
                hr_sync[t] = sh.inc(16)

                def castf(eng, t=t):
                    eng.wait_ge(sh.h, hr_sync[t])
                    eng.activation(out=hbf2[t % 2][:], in_=htile2[t % 2][:],
                                   func=AF.Identity).then_inc(sact.h, 1)
                sact.inc()
                plan.add("act", castf)

                def sqf(eng, t=t):
                    eng.activation(out=htile2[t % 2][:], in_=htile2[t % 2][:],
                                   func=AF.Square).then_inc(sact.h, 1)
                sq_act5[t] = sact.inc()
                plan.add("act", sqf)

                def mskf(eng, t=t):
                    eng.wait_ge(sact.h, sq_act5[t])
                    eng.tensor_tensor(out=t1[:], in0=q_(t, 0), in1=q_(t, 1), op=ALU.min)
                    eng.tensor_tensor(out=t2[:], in0=q_(t, 2), in1=q_(t, 3), op=ALU.min)
                    eng.tensor_max(out=t1[:], in0=t1[:], in1=t2[:])
                    eng.tensor_max(out=t2[:], in0=q_(t, 0), in1=q_(t, 1))
                    eng.tensor_max(out=t3[:], in0=q_(t, 2), in1=q_(t, 3))
                    eng.tensor_tensor(out=t2[:], in0=t2[:], in1=t3[:], op=ALU.min)
                    eng.tensor_max(out=t2[:], in0=t1[:], in1=t2[:])
                    for k in range(4):
                        eng.tensor_tensor(out=q_(t, k), in0=q_(t, k), in1=t2[:], op=ALU.is_ge)
                        ins = eng.tensor_mul(out=qb_(t, k), in0=qb_(t, k), in1=q_(t, k))
                    ins.then_inc(sdve.h, 1)
                msk_dve[t] = sdve.inc()
                plan.add("dve", mskf)

                for ic in range(64):
                    def tpf(eng, t=t, ic=ic):
                        if ic == 0:
                            eng.wait_ge(sdve.h, msk_dve[t])
                            if t >= 1:
                                eng.wait_ge(shs.h, hsw_sync[(t - 1, 7)])
                        if ic >= 6:
                            kind, val = tev_n[(t, ic - 6)]
                            eng.wait_ge(sdve.h if kind == "dve" else sact.h, val)
                        eng.transpose(tp_ps[ic % 6][:],
                                      hbf2[t % 2][:, ic * 128:(ic + 1) * 128],
                                      ident[:]).then_inc(spe.h, 1)
                    tp_pe[(t, ic)] = spe.inc()
                    plan.add("pe", tpf)

                    eng_name = "dve" if ic % 2 == 0 else "act"

                    def tevac(eng, t=t, ic=ic, eng_name=eng_name):
                        eng.wait_ge(spe.h, tp_pe[(t, ic)] + 1)
                        if ic >= 16:
                            eng.wait_ge(shs.h, hsw_sync[(t, ic // 8 - 2)])
                        dst = tstage[:, (ic % 16) * 128:(ic % 16) * 128 + 128]
                        if eng_name == "dve":
                            eng.tensor_copy(out=dst, in_=tp_ps[ic % 6][:]).then_inc(sdve.h, 1)
                        else:
                            eng.activation(out=dst, in_=tp_ps[ic % 6][:],
                                           func=AF.Identity).then_inc(sact.h, 1)
                    if eng_name == "dve":
                        tev_n[(t, ic)] = ("dve", sdve.inc())
                    else:
                        tev_n[(t, ic)] = ("act", sact.inc())
                    plan.add(eng_name, tevac)

                    if ic % 8 == 7:
                        def tstore(eng, t=t, ic8=ic // 8):
                            for icc in range(ic8 * 8, ic8 * 8 + 8):
                                kind, val = tev_n[(t, icc)]
                                eng.wait_ge(sdve.h if kind == "dve" else sact.h, val)
                            src_ = tstage[:, (ic8 % 2) * 1024:(ic8 % 2) * 1024 + 1024]
                            eng.dma_start(
                                out=hs_fm_dram.ap()[ic8 * 8:(ic8 + 1) * 8, :,
                                                    t * 128:(t + 1) * 128].transpose([1, 0, 2]),
                                in_=src_.rearrange("p (i n) -> p i n", i=8)).then_inc(shs.h, 16)
                        plan.add("sync", tstore)
                        hsw_sync[(t, ic // 8)] = shs.inc(16)

                def tpdum(eng, t=t):
                    eng.matmul(dum5b_ps[:], ones128[:, 0:32], onescol[:],
                               start=True, stop=True).then_inc(spe.h, 1)
                spe.inc()
                plan.add("pe", tpdum)
            p5b_shs = shs.n

            with nc.Block("p5b") as blk:
                @blk.tensor
                def _(eng):
                    plan.run("pe", eng)

                @blk.scalar
                def _(eng):
                    plan.run("act", eng)

                @blk.vector
                def _(eng):
                    plan.run("dve", eng)

                @blk.sync
                def _(eng):
                    plan.run("sync", eng)

        with contextlib.ExitStack() as ps:
            hfm_full = ps.enter_context(nc.sbuf_tensor("hfm_full", [128, 64 * 512], BF16))
            wdb = [ps.enter_context(nc.sbuf_tensor(f"wdb_{i}", [128, 2048], BF16))
                   for i in range(3)]
            y_ps = [ps.enter_context(nc.psum_tensor(f"y_ps{i}", [128, 512], F32))
                    for i in range(8)]

            plan = Plan()
            hs_ts, wd_fetch, dmm_pe, yev_dve = {}, {}, {}, {}
            wd_slot_user = {}
            half_load = {}

            for half in range(2):
                def hload(eng, half=half):
                    if half == 0:
                        eng.wait_ge(shs.h, p5b_shs)
                    else:
                        eng.wait_ge(spe.h, dmm_pe[(1, 63)])
                    src_ = hs_fm_dram.ap()[:, :, half * 512:(half + 1) * 512]
                    dst = hfm_full[:].rearrange("p (i n) -> p i n", i=64)
                    eng.dma_start(out=dst[:, 0:32], in_=src_[0:32].transpose([1, 0, 2])).then_inc(s_tp[0].h, 16)
                    eng.dma_start(out=dst[:, 32:64], in_=src_[32:64].transpose([1, 0, 2])).then_inc(s_tp[0].h, 16)
                plan.add("sync", hload)
                half_load[half] = s_tp[0].inc(32)

                for gh in range(2):
                    pi = half * 2 + gh
                    for ic in range(64):
                        gi = pi * 64 + ic

                        prev_user = wd_slot_user.get(ic % 3)

                        def wdf(eng, ic=ic, gi=gi, prev_user=prev_user):
                            if gi == 0:
                                eng.wait_ge(s_wdc.h, s_wdc.n)
                            if prev_user is not None:
                                eng.wait_ge(spe.h, dmm_pe[prev_user])
                            eng.dma_start(out=wdb[ic % 3][:],
                                          in_=wd_bf.ap()[:, ic * 2048:(ic + 1) * 2048]).then_inc(s_wd[ic % 3].h, 16)
                        plan.add("gpsimd", wdf)
                        wd_fetch[(pi, ic)] = s_wd[ic % 3].inc(16)
                        wd_slot_user[ic % 3] = (pi, ic)

                        def dmm(eng, gh=gh, ic=ic, pi=pi, half=half):
                            if ic == 0:
                                eng.wait_ge(s_tp[0].h, half_load[half])
                            eng.wait_ge(s_wd[ic % 3].h, wd_fetch[(pi, ic)])
                            if ic == 0 and pi >= 1:
                                eng.wait_ge(sdve.h, yev_dve[pi - 1])
                            ins = None
                            for gg in range(8):
                                g = gh * 8 + gg
                                ins = eng.matmul(y_ps[gg][:],
                                                 wdb[ic % 3][:, g * 128:(g + 1) * 128],
                                                 hfm_full[:, ic * 512:(ic + 1) * 512],
                                                 start=(ic == 0), stop=(ic == 63))
                            ins.then_inc(spe.h, 1)
                        dmm_pe[(pi, ic)] = spe.inc()
                        plan.add("pe", dmm)

                    def yev(eng, half=half, gh=gh, pi=pi):
                        eng.wait_ge(spe.h, dmm_pe[(pi, 63)])
                        ins = None
                        for gg in range(8):
                            g = gh * 8 + gg
                            sl_ = stream[:, g * TOK + half * 512: g * TOK + half * 512 + 512]
                            ins = eng.tensor_add(out=sl_, in0=y_ps[gg][:], in1=sl_)
                        ins.then_inc(sdve.h, 1)
                    yev_dve[pi] = sdve.inc()
                    plan.add("dve", yev)

            with nc.Block("p5c") as blk:
                @blk.tensor
                def _(eng):
                    plan.run("pe", eng)

                @blk.vector
                def _(eng):
                    plan.run("dve", eng)

                @blk.sync
                def _(eng):
                    plan.run("sync", eng)

                @blk.gpsimd
                def _(eng):
                    plan.run("gpsimd", eng)

        if debug:
            with nc.Block("dbgy") as blk:
                @blk.sync
                def _(eng):
                    eng.dma_start(out=dbg_y, in_=stream[:]).then_inc(sout.h, 16)
                    sout.n += 16
                    eng.wait_ge(sout.h, sout.n)

        apply6 = emit_layernorm("p6", False, None, False,
                                lambda f: stream[:, f * TOK:(f + 1) * TOK], {})

        with nc.Block("p6s") as blk:
            @blk.sync
            def _(eng):
                for f in range(NF):
                    eng.wait_ge(sdve.h, apply6[f])
                    eng.dma_start(out=out_t[f * 128:(f + 1) * 128, :],
                                  in_=stream[:, f * TOK:(f + 1) * TOK]).then_inc(sout.h, 16)
                    sout.n += 16
                if debug:
                    eng.dma_start(out=dbg_ccin, in_=cc_in.ap()).then_inc(sout.h, 16)
                    eng.dma_start(out=dbg_q.rearrange("h p t -> (h p) t"),
                                  in_=q_dram.ap().rearrange("h p t -> (h p) t")).then_inc(sout.h, 16)
                    for t in range(NT):
                        eng.dma_start(out=dbg_h[t], in_=h_dram.ap()[t]).then_inc(sout.h, 16)
                    eng.dma_start(out=dbg_hs.rearrange("i p t -> (i p) t"),
                                  in_=hs_fm_dram.ap().rearrange("i p t -> (i p) t")).then_inc(sout.h, 16)
                    sout.n += 16 * (3 + NT)
                eng.wait_ge(sout.h, sout.n)

    return nc


_NC_CACHE = None


def _get_nc():
    global _NC_CACHE
    if _NC_CACHE is None:
        _NC_CACHE = build_program()
    return _NC_CACHE


def kernel(L0_attn_X, L0_attn_ln_g, L0_attn_ln_b, L0_attn_Wq, L0_attn_bq,
           L0_attn_Wk, L0_attn_bk, L0_attn_Wv, L0_attn_bv, L0_attn_Wo,
           L0_attn_bo, L0_ffn_w_gate, L0_ffn_w_up, L0_ffn_w_down):
    X = np.asarray(L0_attn_X)[0]
    Wq = np.asarray(L0_attn_Wq)
    Wk = np.asarray(L0_attn_Wk)
    Wv = np.asarray(L0_attn_Wv)
    Wo = np.asarray(L0_attn_Wo)
    Wg = np.asarray(L0_ffn_w_gate)
    Wu = np.asarray(L0_ffn_w_up)
    Wd = np.asarray(L0_ffn_w_down)

    def lhsT_layout(W):
        n_in = W.shape[0] // 128
        return np.ascontiguousarray(
            W.reshape(n_in, 128, 16, 128).transpose(1, 2, 0, 3).reshape(128, -1))

    def rhs_layout(W):
        return np.ascontiguousarray(
            W.reshape(16, 128, -1).transpose(1, 0, 2).reshape(128, -1))

    perm = 4 * (np.arange(II) % 2048) + np.arange(II) // 2048

    wq_dev = lhsT_layout(Wq)
    wk_dev = lhsT_layout(Wk)
    wo_dev = lhsT_layout(Wo)
    wv_dev = rhs_layout(Wv)
    wg_dev = rhs_layout(Wg[:, perm])
    wu_dev = rhs_layout(Wu[:, perm])
    wd_dev = np.ascontiguousarray(
        Wd[perm, :].reshape(64, 128, 16, 128).transpose(1, 0, 2, 3).reshape(128, -1))

    def colvec(b):
        return np.ascontiguousarray(np.asarray(b).reshape(16, 128).T)

    bq_dev = colvec(L0_attn_bq)
    bk_dev = colvec(L0_attn_bk)
    bo_dev = colvec(L0_attn_bo)
    lng_dev = colvec(L0_attn_ln_g)
    lnb_dev = colvec(L0_attn_ln_b)
    bv_dev = np.ascontiguousarray(np.asarray(L0_attn_bv)[None, :])

    p = np.arange(128)[:, None]
    v = np.arange(896)[None, :]
    in_maps = []
    for c in range(NCORES):
        Xc = np.ascontiguousarray(X[c::8, :].T)
        masks = np.zeros((8, 128, 896), dtype=ml_dtypes.bfloat16)
        for r in range(8):
            delta = 1 if r > c else 0
            masks[r] = ((v - 384 - p) >= delta).astype(ml_dtypes.bfloat16)
        in_maps.append({
            "x": Xc, "wq": wq_dev, "wk": wk_dev, "wv": wv_dev, "wo": wo_dev,
            "wg": wg_dev, "wu": wu_dev, "wd": wd_dev,
            "bq": bq_dev, "bk": bk_dev, "bo": bo_dev, "bvr": bv_dev,
            "lng": lng_dev, "lnb": lnb_dev, "masks": masks,
            "identw": np.eye(128, dtype=ml_dtypes.bfloat16),
        })

    nc = _get_nc()
    res = run_bass_kernel_spmd(nc, in_maps, core_ids=list(range(NCORES)))
    out = np.empty((S, DM), np.float32)
    for c in range(NCORES):
        out[c::8, :] = res.results[c]["out"].T
    return out[None]


# version 22
# speedup vs baseline: 1.2936x; 1.0086x over previous
import contextlib
import numpy as np
import ml_dtypes
import concourse.bass as bass
import concourse.mybir as mybir
from concourse.bass_utils import run_bass_kernel_spmd

F32 = mybir.dt.float32
BF16 = mybir.dt.bfloat16
AF = mybir.ActivationFunctionType
ALU = mybir.AluOpType

NCORES = 8
DM = 2048
NF = 16
NH = 16
S = 8192
TOK = 1024
NT = 8
II = 8192
EPS = 1e-5
SCALE = 1.0 / float(np.sqrt(128.0))
EXP_BIAS = -2.0


class Sem:
    def __init__(self, nc, es, name):
        self.h = es.enter_context(nc.semaphore(name))
        self.n = 0

    def inc(self, k=1):
        self.n += k
        return self.n


class Plan:
    def __init__(self):
        self.items = []

    def add(self, eng, fn):
        self.items.append((eng, fn))

    def run(self, name, engobj):
        for e, fn in self.items:
            if e == name:
                fn(engobj)


def build_program(debug=False):
    nc = bass.Bass("TRN2", target_bir_lowering=False)

    x_in = nc.dram_tensor("x", [DM, TOK], F32, kind="ExternalInput").ap()
    wq_in = nc.dram_tensor("wq", [128, NF * DM], F32, kind="ExternalInput").ap()
    wk_in = nc.dram_tensor("wk", [128, NF * DM], F32, kind="ExternalInput").ap()
    wv_in = nc.dram_tensor("wv", [128, NF * DM], F32, kind="ExternalInput").ap()
    wo_in = nc.dram_tensor("wo", [128, NF * DM], F32, kind="ExternalInput").ap()
    wg_in = nc.dram_tensor("wg", [128, NF * II], F32, kind="ExternalInput").ap()
    wu_in = nc.dram_tensor("wu", [128, NF * II], F32, kind="ExternalInput").ap()
    wd_in = nc.dram_tensor("wd", [128, 64 * DM], F32, kind="ExternalInput").ap()
    bq_in = nc.dram_tensor("bq", [128, 16], F32, kind="ExternalInput").ap()
    bk_in = nc.dram_tensor("bk", [128, 16], F32, kind="ExternalInput").ap()
    bo_in = nc.dram_tensor("bo", [128, 16], F32, kind="ExternalInput").ap()
    bv_in = nc.dram_tensor("bvr", [1, DM], F32, kind="ExternalInput").ap()
    lng_in = nc.dram_tensor("lng", [128, 16], F32, kind="ExternalInput").ap()
    lnb_in = nc.dram_tensor("lnb", [128, 16], F32, kind="ExternalInput").ap()
    masks_in = nc.dram_tensor("masks", [8, 128, 896], BF16, kind="ExternalInput").ap()
    ident_in = nc.dram_tensor("identw", [128, 128], BF16, kind="ExternalInput").ap()
    out_t = nc.dram_tensor("out", [DM, TOK], F32, kind="ExternalOutput").ap()

    cc_in = nc.dram_tensor("cc_in", [4096, 1024], BF16, kind="Internal")
    cc_out = nc.dram_tensor("cc_out", [NCORES * 4096, 1024], BF16, kind="Internal")
    q_dram = nc.dram_tensor("q_dram", [NH, 128, TOK], BF16, kind="Internal")
    h_dram = nc.dram_tensor("h_dram", [NT, 128, II], F32, kind="Internal")
    hs_fm_dram = nc.dram_tensor("hs_fm", [64, 128, TOK], BF16, kind="Internal")
    wd_bf = nc.dram_tensor("wd_bf", [128, 64 * DM], BF16, kind="Internal")
    if debug:
        dbg_xn = nc.dram_tensor("dbg_xn", [128, NF * TOK], BF16, kind="ExternalOutput").ap()
        dbg_x2 = nc.dram_tensor("dbg_x2", [128, NF * TOK], F32, kind="ExternalOutput").ap()
        dbg_ccin = nc.dram_tensor("dbg_ccin", [4096, 1024], BF16, kind="ExternalOutput").ap()
        dbg_q = nc.dram_tensor("dbg_q", [NH, 128, TOK], BF16, kind="ExternalOutput").ap()
        dbg_h = nc.dram_tensor("dbg_h", [NT, 128, II], F32, kind="ExternalOutput").ap()
        dbg_hs = nc.dram_tensor("dbg_hs", [64, 128, TOK], BF16, kind="ExternalOutput").ap()
        dbg_y = nc.dram_tensor("dbg_y", [128, NF * TOK], F32, kind="ExternalOutput").ap()

    es = contextlib.ExitStack()
    with es:
        stream = es.enter_context(nc.sbuf_tensor("stream", [128, NF * TOK], F32))
        cpool = es.enter_context(nc.sbuf_tensor("cpool", [128, 8], F32))
        bq_sb = es.enter_context(nc.sbuf_tensor("bq_sb", [128, 16], F32))
        bk_sb = es.enter_context(nc.sbuf_tensor("bk_sb", [128, 16], F32))
        bo_sb = es.enter_context(nc.sbuf_tensor("bo_sb", [128, 16], F32))
        lng_sb = es.enter_context(nc.sbuf_tensor("lng_sb", [128, 16], F32))
        lnb_sb = es.enter_context(nc.sbuf_tensor("lnb_sb", [128, 16], F32))
        ones128 = es.enter_context(nc.sbuf_tensor("ones128", [128, 128], BF16))
        ident = es.enter_context(nc.sbuf_tensor("ident", [128, 128], BF16))
        onescol = es.enter_context(nc.sbuf_tensor("onescol", [128, 1], BF16))
        onesrow = es.enter_context(nc.sbuf_tensor("onesrow", [1, 128], F32))

        sx = Sem(nc, es, "sx")
        s_w = [Sem(nc, es, f"s_w{i}") for i in range(2)]
        s_st = [Sem(nc, es, f"s_st{i}") for i in range(2)]
        s_vst = [Sem(nc, es, f"s_vst{i}") for i in range(2)]
        s_qst = [Sem(nc, es, f"s_qst{i}") for i in range(2)]
        s_kv = [Sem(nc, es, f"s_kv{i}") for i in range(2)]
        s_hst = [Sem(nc, es, f"s_hst{i}") for i in range(4)]
        s_tp = [Sem(nc, es, f"s_tp{i}") for i in range(4)]
        s_wd = [Sem(nc, es, f"s_wd{i}") for i in range(6)]
        s_wdc = Sem(nc, es, "s_wdc")
        scast = Sem(nc, es, "scast")
        skvst = Sem(nc, es, "skvst")
        scc = Sem(nc, es, "scc")
        skvf = Sem(nc, es, "skvf")
        sh = Sem(nc, es, "sh")
        shs = Sem(nc, es, "shs")
        sout = Sem(nc, es, "sout")
        spe = Sem(nc, es, "spe")
        sact = Sem(nc, es, "sact")
        sdve = Sem(nc, es, "sdve")

        C_SCALE, C_EBIAS, C_EPS, C_NHALF, C_NONE = 0, 1, 2, 3, 4
        CVALS = {C_SCALE: SCALE, C_EBIAS: EXP_BIAS, C_EPS: EPS,
                 C_NHALF: -0.5, C_NONE: -1.0}

        def cp(i):
            return cpool[:, i:i + 1]

        def emit_layernorm(name, src_is_xb, xb_ap, with_gb, out_fn, pre_wait):
            with contextlib.ExitStack() as ps:
                s1_ps = ps.enter_context(nc.psum_tensor(name + "_s1", [128, TOK], F32))
                s2_ps = ps.enter_context(nc.psum_tensor(name + "_s2", [128, TOK], F32))
                dln_ps = ps.enter_context(nc.psum_tensor(name + "_dum", [32, 1], F32))
                xsq = ps.enter_context(nc.sbuf_tensor(name + "_xsq", [128, 2 * TOK], BF16))
                xbf = None
                if not src_is_xb:
                    xbf = ps.enter_context(nc.sbuf_tensor(name + "_xbf", [128, 2 * TOK], BF16))
                mu = ps.enter_context(nc.sbuf_tensor(name + "_mu", [128, TOK], F32))
                rstd = ps.enter_context(nc.sbuf_tensor(name + "_rstd", [128, TOK], F32))
                m2 = ps.enter_context(nc.sbuf_tensor(name + "_m2", [128, TOK], F32))
                ta = ps.enter_context(nc.sbuf_tensor(name + "_ta", [128, TOK], F32))

                plan = Plan()
                cast_dve = {}
                sq_act = {}
                mm_pe = {}
                for f in range(NF):
                    if xbf is not None:
                        def cast_fn(eng, ff=f):
                            if ff >= 2:
                                eng.wait_ge(spe.h, mm_pe[ff - 2])
                            eng.tensor_scalar(
                                out=xbf[:, (ff % 2) * TOK:(ff % 2 + 1) * TOK],
                                in0=stream[:, ff * TOK:(ff + 1) * TOK],
                                scalar1=1.0, scalar2=None,
                                op0=ALU.mult).then_inc(sdve.h, 1)
                        cast_dve[f] = sdve.inc()
                        plan.add("dve", cast_fn)

                    def sq_fn(eng, ff=f):
                        if xbf is not None:
                            eng.wait_ge(sdve.h, cast_dve[ff])
                        elif ff >= 2:
                            eng.wait_ge(spe.h, mm_pe[ff - 2])
                        src = (xbf[:, (ff % 2) * TOK:(ff % 2 + 1) * TOK]
                               if xbf is not None
                               else xb_ap[:, ff * TOK:(ff + 1) * TOK])
                        eng.activation(out=xsq[:, (ff % 2) * TOK:(ff % 2 + 1) * TOK],
                                       in_=src, func=AF.Square).then_inc(sact.h, 1)
                    sq_act[f] = sact.inc()
                    plan.add("act", sq_fn)

                    def mm_fn(eng, ff=f):
                        eng.wait_ge(sact.h, sq_act[ff])
                        src1 = (xbf[:, (ff % 2) * TOK:(ff % 2 + 1) * TOK]
                                if xbf is not None
                                else xb_ap[:, ff * TOK:(ff + 1) * TOK])
                        src2 = xsq[:, (ff % 2) * TOK:(ff % 2 + 1) * TOK]
                        st = (ff == 0)
                        sp = (ff == NF - 1)
                        eng.matmul(s1_ps[:, 0:512], ones128[:], src1[:, 0:512], start=st, stop=sp)
                        eng.matmul(s1_ps[:, 512:1024], ones128[:], src1[:, 512:1024], start=st, stop=sp)
                        eng.matmul(s2_ps[:, 0:512], ones128[:], src2[:, 0:512], start=st, stop=sp)
                        eng.matmul(s2_ps[:, 512:1024], ones128[:], src2[:, 512:1024],
                                   start=st, stop=sp).then_inc(spe.h, 1)
                        if sp:
                            eng.matmul(dln_ps[:], ones128[:, 0:32], onescol[:],
                                       start=True, stop=True).then_inc(spe.h, 1)
                    mm_pe[f] = spe.inc()
                    if f == NF - 1:
                        spe.inc()
                    plan.add("pe", mm_fn)

                stats_done = mm_pe[NF - 1] + 1

                def mu_fn(eng):
                    eng.wait_ge(spe.h, stats_done)
                    eng.tensor_scalar(out=mu[:], in0=s1_ps[:], scalar1=1.0 / 2048.0,
                                      scalar2=None, op0=ALU.mult).then_inc(sdve.h, 1)
                mu_dve = sdve.inc()
                plan.add("dve", mu_fn)

                def musq_fn(eng):
                    eng.wait_ge(sdve.h, mu_dve)
                    eng.activation(out=ta[:], in_=mu[:], func=AF.Square).then_inc(sact.h, 1)
                musq_act = sact.inc()
                plan.add("act", musq_fn)

                def var_fn(eng):
                    eng.wait_ge(sact.h, musq_act)
                    eng.scalar_tensor_tensor(
                        out=rstd[:], in0=s2_ps[:], scalar=1.0 / 2048.0, in1=ta[:],
                        op0=ALU.mult, op1=ALU.subtract).then_inc(sdve.h, 1)
                var_dve = sdve.inc()
                plan.add("dve", var_fn)

                rstd_act = sact.inc(2)

                def rstd_fn(eng):
                    eng.wait_ge(sdve.h, var_dve)
                    eng.activation(out=rstd[:], in_=rstd[:], func=AF.Ln,
                                   bias=cp(C_EPS)).then_inc(sact.h, 1)
                    eng.activation(out=rstd[:], in_=rstd[:], func=AF.Exp,
                                   scale=cp(C_NHALF)).then_inc(sact.h, 1)
                plan.add("act", rstd_fn)

                def m2_fn(eng):
                    eng.wait_ge(sact.h, rstd_act)
                    eng.tensor_mul(out=m2[:], in0=mu[:], in1=rstd[:]).then_inc(sdve.h, 1)
                m2_dve = sdve.inc()
                plan.add("dve", m2_fn)

                apply_dve = {}
                for f in range(NF):
                    def ap_fn(eng, ff=f):
                        if ff == 0:
                            eng.wait_ge(sdve.h, m2_dve)
                        eng.tensor_mul(out=ta[:], in0=stream[:, ff * TOK:(ff + 1) * TOK],
                                       in1=rstd[:])
                        if with_gb:
                            eng.tensor_sub(out=ta[:], in0=ta[:], in1=m2[:])
                            eng.tensor_scalar(out=out_fn(ff), in0=ta[:],
                                              scalar1=lng_sb[:, ff:ff + 1],
                                              scalar2=lnb_sb[:, ff:ff + 1],
                                              op0=ALU.mult, op1=ALU.add).then_inc(sdve.h, 1)
                        else:
                            eng.tensor_sub(out=out_fn(ff), in0=ta[:],
                                           in1=m2[:]).then_inc(sdve.h, 1)
                    apply_dve[f] = sdve.inc()
                    plan.add("dve", ap_fn)

                with nc.Block(name) as blk:
                    @blk.tensor
                    def _(eng):
                        for sem, val in pre_wait.get("pe", []):
                            eng.wait_ge(sem.h, val)
                        plan.run("pe", eng)

                    @blk.scalar
                    def _(eng):
                        for sem, val in pre_wait.get("act", []):
                            eng.wait_ge(sem.h, val)
                        plan.run("act", eng)

                    @blk.vector
                    def _(eng):
                        for sem, val in pre_wait.get("dve", []):
                            eng.wait_ge(sem.h, val)
                        plan.run("dve", eng)
                return apply_dve

        with nc.sbuf_tensor("xb", [128, NF * TOK], BF16) as xb:
            with nc.Block("p0") as blk:
                @blk.sync
                def _(eng):
                    src = x_in.rearrange("(f p) t -> p f t", p=128)
                    dst = stream[:].rearrange("p (f t) -> p f t", f=NF)
                    eng.dma_start(out=dst[:, 0:8], in_=src[:, 0:8]).then_inc(sx.h, 16)
                    eng.dma_start(out=dst[:, 8:16], in_=src[:, 8:16]).then_inc(sx.h, 16)
                    for ap_in, sb in ((bq_in, bq_sb), (bk_in, bk_sb), (bo_in, bo_sb),
                                      (lng_in, lng_sb), (lnb_in, lnb_sb), (ident_in, ident)):
                        eng.dma_start(out=sb[:], in_=ap_in).then_inc(sx.h, 16)

                @blk.gpsimd
                def _(eng):
                    src = x_in.rearrange("(f p) t -> p f t", p=128)
                    dst = xb[:].rearrange("p (f t) -> p f t", f=NF)
                    eng.dma_start(out=dst[:, 0:8], in_=src[:, 0:8]).then_inc(scast.h, 16)
                    eng.dma_start(out=dst[:, 8:16], in_=src[:, 8:16]).then_inc(scast.h, 16)
                    eng.memset(ones128[:], 1.0)
                    eng.memset(onescol[:], 1.0)
                    eng.memset(onesrow[:], 1.0)
                    for i, v in CVALS.items():
                        eng.memset(cpool[:, i:i + 1], v)
            sx.n += 8 * 16
            scast.n += 32
            p0_sx = sx.n
            p0_scast = scast.n

            emit_layernorm("p1", True, xb,
                           True, lambda f: xb[:, f * TOK:(f + 1) * TOK],
                           {"pe": [(scast, p0_scast)],
                            "act": [(scast, p0_scast)],
                            "dve": [(sx, p0_sx)]})

            with contextlib.ExitStack() as ps:
                bv_bc = ps.enter_context(nc.sbuf_tensor("bv_bc", [128, DM], F32))
                bv_row = ps.enter_context(nc.sbuf_tensor("bv_row", [1, DM], F32))
                wbuf = [ps.enter_context(nc.sbuf_tensor(f"w2_{i}", [128, NF * 512], BF16))
                        for i in range(2)]
                kstage = [ps.enter_context(nc.sbuf_tensor(f"kst_{i}", [128, TOK], BF16))
                          for i in range(2)]
                vstage = [ps.enter_context(nc.sbuf_tensor(f"vst_{i}", [128, 512], BF16))
                          for i in range(2)]
                pp = [ps.enter_context(nc.psum_tensor(f"p2_{i}", [128, 512], F32))
                      for i in range(2)]
                bv_ps = ps.enter_context(nc.psum_tensor("p2_bv", [128, DM], F32))
                dum2_ps = ps.enter_context(nc.psum_tensor("p2_dum", [32, 1], F32))

                def p2dummy(eng):
                    eng.matmul(dum2_ps[:], ones128[:, 0:32], onescol[:],
                               start=True, stop=True).then_inc(spe.h, 1)

                plan = Plan()
                cc_in_ap = cc_in.ap()
                cc_in_V = cc_in_ap[2048:4096, :].rearrange("(t two) n -> t (two n)", two=2)

                def bvload_fn(eng):
                    eng.dma_start(out=bv_row[:], in_=bv_in).then_inc(sx.h, 16)
                bv_sx = sx.inc(16)
                plan.add("sync", bvload_fn)

                def bvbc_fn(eng):
                    eng.wait_ge(sx.h, bv_sx)
                    for j in range(3):
                        eng.matmul(bv_ps[:, j * 512:(j + 1) * 512], onesrow[:],
                                   bv_row[:, j * 512:(j + 1) * 512], start=True, stop=True)
                    eng.matmul(bv_ps[:, 1536:2048], onesrow[:],
                               bv_row[:, 1536:2048], start=True, stop=True).then_inc(spe.h, 1)
                    p2dummy(eng)
                spe.inc()
                bv_pe = spe.inc()
                plan.add("pe", bvbc_fn)

                def bvcp_fn(eng):
                    eng.wait_ge(spe.h, bv_pe)
                    eng.tensor_copy(out=bv_bc[:], in_=bv_ps[:]).then_inc(sdve.h, 1)
                bv_dve = sdve.inc()
                plan.add("dve", bvcp_fn)

                kmm_pe, kevac_act, kdma = {}, {}, {}
                vmm_pe, vevac_dve, vdma = {}, {}, {}
                qmm_pe, qevac_act, qdma = {}, {}, {}
                wk_fetch, wv_fetch, wq_fetch = {}, {}, {}

                for g in range(16):
                    def wkf(eng, g=g):
                        if g >= 2:
                            eng.wait_ge(spe.h, kmm_pe[(g - 2, 1)])
                        eng.dma_start(out=wbuf[g % 2][:, 0:2048],
                                      in_=wk_in[:, g * 2048:(g + 1) * 2048]).then_inc(s_w[g % 2].h, 16)
                    plan.add("gpsimd", wkf)
                    wk_fetch[g] = s_w[g % 2].inc(16)

                    for half in range(2):
                        def kmm(eng, g=g, half=half):
                            eng.wait_ge(s_w[g % 2].h, wk_fetch[g])
                            idx = g * 2 + half
                            if idx >= 2:
                                pg, ph = divmod(idx - 2, 2)
                                eng.wait_ge(sact.h, kevac_act[(pg, ph)])
                            ins = None
                            for f in range(NF):
                                ins = eng.matmul(
                                    pp[half][:],
                                    wbuf[g % 2][:, f * 128:(f + 1) * 128],
                                    xb[:, f * TOK + half * 512: f * TOK + half * 512 + 512],
                                    start=(f == 0), stop=(f == NF - 1))
                            ins.then_inc(spe.h, 1)
                            p2dummy(eng)
                        spe.inc()
                        kmm_pe[(g, half)] = spe.inc()
                        plan.add("pe", kmm)

                        def kevac(eng, g=g, half=half):
                            eng.wait_ge(spe.h, kmm_pe[(g, half)])
                            if g >= 2 and half == 0:
                                eng.wait_ge(s_st[g % 2].h, kdma[g - 2])
                            eng.activation(out=kstage[g % 2][:, half * 512:half * 512 + 512],
                                           in_=pp[half][:], func=AF.Identity,
                                           bias=bk_sb[:, g:g + 1]).then_inc(sact.h, 1)
                        kevac_act[(g, half)] = sact.inc()
                        plan.add("act", kevac)

                    def kdma_fn(eng, g=g):
                        eng.wait_ge(sact.h, kevac_act[(g, 1)])
                        eng.dma_start(out=cc_in_ap[g * 128:(g + 1) * 128, :],
                                      in_=kstage[g % 2][:]).then_inc(s_st[g % 2].h, 16)
                    plan.add("sync", kdma_fn)
                    kdma[g] = s_st[g % 2].inc(16)

                wv_r = wv_in.rearrange("p (f n) -> p f n", f=NF)
                for j in range(4):
                    def wvf(eng, j=j):
                        if j >= 2:
                            eng.wait_ge(spe.h, vmm_pe[(j - 2, NT - 1)])
                        src = wv_r[:, :, j * 512:(j + 1) * 512]
                        dst = wbuf[j % 2][:].rearrange("p (f n) -> p f n", f=NF)
                        eng.dma_start(out=dst[:, 0:8], in_=src[:, 0:8]).then_inc(s_w[j % 2].h, 16)
                        eng.dma_start(out=dst[:, 8:16], in_=src[:, 8:16]).then_inc(s_w[j % 2].h, 16)
                    plan.add("gpsimd", wvf)
                    wv_fetch[j] = s_w[j % 2].inc(32)

                    for t in range(NT):
                        def vmm(eng, j=j, t=t):
                            eng.wait_ge(s_w[j % 2].h, wv_fetch[j])
                            idx = j * NT + t
                            if idx < 2:
                                eng.wait_ge(sact.h, kevac_act[(15, 1)])
                            else:
                                pj, pt = divmod(idx - 2, NT)
                                eng.wait_ge(sdve.h, vevac_dve[(pj, pt)])
                            ins = None
                            for f in range(NF):
                                ins = eng.matmul(
                                    pp[t % 2][:],
                                    xb[:, f * TOK + t * 128: f * TOK + t * 128 + 128],
                                    wbuf[j % 2][:, f * 512:(f + 1) * 512],
                                    start=(f == 0), stop=(f == NF - 1))
                            ins.then_inc(spe.h, 1)
                            p2dummy(eng)
                        spe.inc()
                        vmm_pe[(j, t)] = spe.inc()
                        plan.add("pe", vmm)

                        def vevac(eng, j=j, t=t):
                            eng.wait_ge(spe.h, vmm_pe[(j, t)])
                            idx = j * NT + t
                            if idx == 0:
                                eng.wait_ge(sdve.h, bv_dve)
                            if idx >= 2:
                                pj, pt = divmod(idx - 2, NT)
                                eng.wait_ge(s_vst[t % 2].h, vdma[(pj, pt)])
                            eng.tensor_add(out=vstage[t % 2][:], in0=pp[t % 2][:],
                                           in1=bv_bc[:, j * 512:(j + 1) * 512]).then_inc(sdve.h, 1)
                        vevac_dve[(j, t)] = sdve.inc()
                        plan.add("dve", vevac)

                        def vdma_fn(eng, j=j, t=t):
                            eng.wait_ge(sdve.h, vevac_dve[(j, t)])
                            eng.dma_start(
                                out=cc_in_V[t * 128:(t + 1) * 128, j * 512:(j + 1) * 512],
                                in_=vstage[t % 2][:]).then_inc(s_vst[t % 2].h, 16)
                        plan.add("sync", vdma_fn)
                        vdma[(j, t)] = s_vst[t % 2].inc(16)

                staged_vals = [(s_st[0], s_st[0].n), (s_st[1], s_st[1].n),
                               (s_vst[0], s_vst[0].n), (s_vst[1], s_vst[1].n)]

                def coll_fn(eng):
                    for sem, val in staged_vals:
                        eng.wait_ge(sem.h, val)
                    eng.collective_compute(
                        "AllGather", ALU.bypass,
                        replica_groups=[list(range(NCORES))],
                        ins=[cc_in.ap().opt()],
                        outs=[cc_out.ap().opt()],
                    ).then_inc(scc.h, 1)
                plan.add("gpsimd", coll_fn)
                scc.inc()

                for g in range(16):
                    def wqf(eng, g=g):
                        if g >= 2:
                            eng.wait_ge(spe.h, qmm_pe[(g - 2, 1)])
                        else:
                            eng.wait_ge(spe.h, vmm_pe[(3, NT - 1)])
                        eng.dma_start(out=wbuf[g % 2][:, 0:2048],
                                      in_=wq_in[:, g * 2048:(g + 1) * 2048]).then_inc(s_w[g % 2].h, 16)
                    plan.add("gpsimd", wqf)
                    wq_fetch[g] = s_w[g % 2].inc(16)

                    for half in range(2):
                        def qmm(eng, g=g, half=half):
                            eng.wait_ge(s_w[g % 2].h, wq_fetch[g])
                            idx = g * 2 + half
                            if idx < 2:
                                eng.wait_ge(sdve.h, vevac_dve[(3, NT - 1)])
                            else:
                                pg, ph = divmod(idx - 2, 2)
                                eng.wait_ge(sact.h, qevac_act[(pg, ph)])
                            ins = None
                            for f in range(NF):
                                ins = eng.matmul(
                                    pp[half][:],
                                    wbuf[g % 2][:, f * 128:(f + 1) * 128],
                                    xb[:, f * TOK + half * 512: f * TOK + half * 512 + 512],
                                    start=(f == 0), stop=(f == NF - 1))
                            ins.then_inc(spe.h, 1)
                            p2dummy(eng)
                        spe.inc()
                        qmm_pe[(g, half)] = spe.inc()
                        plan.add("pe", qmm)

                        def qevac(eng, g=g, half=half):
                            eng.wait_ge(spe.h, qmm_pe[(g, half)])
                            if half == 0:
                                if g < 2:
                                    eng.wait_ge(s_st[g % 2].h, kdma[14 + g])
                                else:
                                    eng.wait_ge(s_qst[g % 2].h, qdma[g - 2])
                            eng.activation(
                                out=kstage[g % 2][:, half * 512:half * 512 + 512],
                                in_=pp[half][:], func=AF.Identity,
                                bias=bq_sb[:, g:g + 1]).then_inc(sact.h, 1)
                        qevac_act[(g, half)] = sact.inc()
                        plan.add("act", qevac)

                    def qdma_fn(eng, g=g):
                        eng.wait_ge(sact.h, qevac_act[(g, 1)])
                        eng.dma_start(out=q_dram.ap()[g],
                                      in_=kstage[g % 2][:]).then_inc(s_qst[g % 2].h, 16)
                    plan.add("sync", qdma_fn)
                    qdma[g] = s_qst[g % 2].inc(16)

                with nc.Block("p2") as blk:
                    @blk.tensor
                    def _(eng):
                        plan.run("pe", eng)

                    @blk.scalar
                    def _(eng):
                        plan.run("act", eng)

                    @blk.vector
                    def _(eng):
                        plan.run("dve", eng)

                    @blk.sync
                    def _(eng):
                        plan.run("sync", eng)

                    @blk.gpsimd
                    def _(eng):
                        plan.run("gpsimd", eng)
            p2_qst = (s_qst[0].n, s_qst[1].n)

            if debug:
                with nc.Block("dbgxn") as blk:
                    @blk.sync
                    def _(eng):
                        eng.dma_start(out=dbg_xn, in_=xb[:]).then_inc(sout.h, 16)
                        sout.n += 16
                        eng.wait_ge(sout.h, sout.n)

        with contextlib.ExitStack() as ps:
            kb_sb = [ps.enter_context(nc.sbuf_tensor(f"kb_{i}", [128, 8192], BF16))
                     for i in range(2)]
            vb_sb = [ps.enter_context(nc.sbuf_tensor(f"vb_{i}", [128, 8192], BF16))
                     for i in range(2)]
            qcur = [ps.enter_context(nc.sbuf_tensor(f"qc_{i}", [128, 512], BF16))
                    for i in range(2)]
            abuf = [ps.enter_context(nc.sbuf_tensor(f"ab_{i}", [128, 512], BF16))
                    for i in range(4)]
            av_sb = ps.enter_context(nc.sbuf_tensor("av_sb", [128, NH * 512], BF16))
            mask_sb = ps.enter_context(nc.sbuf_tensor("mask_sb", [128, 8 * 896], BF16))
            rdbc_sb = ps.enter_context(nc.sbuf_tensor("rdbc_sb", [128, 512], F32))
            wob = [ps.enter_context(nc.sbuf_tensor(f"wob_{i}", [128, 2048], BF16))
                   for i in range(2)]

            s_ps = [ps.enter_context(nc.psum_tensor(f"s_ps{i}", [128, 512], F32))
                    for i in range(3)]
            av_ps = [ps.enter_context(nc.psum_tensor(f"av_ps{i}", [128, 512], F32))
                     for i in range(2)]
            den_ps = [ps.enter_context(nc.psum_tensor(f"den_ps{i}", [128, 512], F32))
                      for i in range(2)]
            dum3_ps = ps.enter_context(nc.psum_tensor("dum3_ps", [32, 1], F32))

            plan = Plan()

            def mload(eng):
                src = masks_in.transpose([1, 0, 2])
                dst = mask_sb[:].rearrange("p (r n) -> p r n", r=8)
                eng.dma_start(out=dst[:, 0:4], in_=src[:, 0:4]).then_inc(sx.h, 16)
                eng.dma_start(out=dst[:, 4:8], in_=src[:, 4:8]).then_inc(sx.h, 16)
            plan.add("sync", mload)
            mask_sx = sx.inc(32)

            ccr = cc_out.ap().rearrange("(r x) n -> r x n", x=4096)
            vsrc_all = ccr[:, 2048:4096, :].rearrange("r (t two) n -> r t (two n)", two=2)

            exp_act, mask_dve = {}, {}
            headdone_pe, headfinal_pe, rden_act = {}, {}, {}
            avevac_dve = {}
            mask_info = {}

            def emit_mask(mgt):
                if mgt not in mask_info:
                    return
                mr, mkbr, masl = mask_info[mgt]

                def maskf(eng, mgt=mgt, mr=mr, mkbr=mkbr, masl=masl):
                    if mgt == 0:
                        eng.wait_ge(sx.h, mask_sx)
                    eng.wait_ge(sact.h, exp_act[mgt])
                    moff = mr * 896 + (384 - 128 * mkbr)
                    eng.tensor_mul(out=abuf[masl][:], in0=abuf[masl][:],
                                   in1=mask_sb[:, moff:moff + 512]).then_inc(sdve.h, 1)
                mask_dve[mgt] = sdve.inc()
                plan.add("dve", maskf)
            kvf_done, smm_pe = {}, {}
            wo_pe, woevac_dve, wof_cast, wo_cover = {}, {}, {}, {}

            def emit_kv_fetch(qb, h, hh):
                Tq = 512 * (qb + 1)
                ktn = 4 * (qb + 1)
                slot = hh % 2

                def qf(eng, qb=qb, h=h, slot=slot, hh=hh):
                    if hh >= 2:
                        eng.wait_ge(spe.h, headfinal_pe[hh - 2])
                    eng.dma_start(out=qcur[slot][:],
                                  in_=q_dram.ap()[h, :, qb * 512:(qb + 1) * 512]).then_inc(s_kv[slot].h, 16)
                plan.add("sync", qf)
                s_kv[slot].inc(16)

                def kf(eng, h=h, slot=slot, Tq=Tq):
                    ksrc = ccr[:, h * 128:(h + 1) * 128, 0:Tq].transpose([1, 0, 2])
                    kdst = kb_sb[slot][:, 0:8 * Tq].rearrange("p (r t) -> p r t", r=8)
                    eng.dma_start(out=kdst[:, 0:4], in_=ksrc[:, 0:4]).then_inc(s_kv[slot].h, 16)
                    eng.dma_start(out=kdst[:, 4:8], in_=ksrc[:, 4:8]).then_inc(s_kv[slot].h, 16)
                plan.add("sync", kf)
                s_kv[slot].inc(32)

                def vf(eng, h=h, slot=slot, Tq=Tq, ktn=ktn):
                    vsrc = vsrc_all[:, 0:Tq, h * 128:(h + 1) * 128]
                    vsrc2 = vsrc.rearrange("r (kt p) d -> p r kt d", p=128)
                    vdst = vb_sb[slot][:, 0:8 * Tq].rearrange("p (r kt d) -> p r kt d",
                                                              r=8, kt=ktn)
                    for r in range(8):
                        if ktn <= 4:
                            eng.dma_start(out=vdst[:, r:r + 1],
                                          in_=vsrc2[:, r:r + 1]).then_inc(s_kv[slot].h, 16)
                        else:
                            eng.dma_start(out=vdst[:, r:r + 1, 0:4],
                                          in_=vsrc2[:, r:r + 1, 0:4]).then_inc(s_kv[slot].h, 16)
                            eng.dma_start(out=vdst[:, r:r + 1, 4:ktn],
                                          in_=vsrc2[:, r:r + 1, 4:ktn]).then_inc(s_kv[slot].h, 16)
                plan.add("sync", vf)
                s_kv[slot].inc(16 * (8 if ktn <= 4 else 16))
                return s_kv[slot].n

            gt = 0
            avmm_den = {}

            for qb in range(2):
                ktn = 4 * (qb + 1)
                ntile = 8 * ktn
                for h in range(NH):
                    hh = qb * NH + h
                    kvf_done[hh] = emit_kv_fetch(qb, h, hh)

                    def mk_smm(hh, r, kt, ti, gt, sl, ktn):
                        def smm(eng):
                            if ti == 0:
                                eng.wait_ge(s_kv[hh % 2].h, kvf_done[hh])
                                if hh >= 2:
                                    eng.wait_ge(sdve.h, avevac_dve[hh - 2])
                                if hh >= 2:
                                    eng.wait_ge(sact.h, rden_act[hh - 2])
                            if gt >= 2:
                                eng.wait_ge(sact.h, exp_act[gt - 2])
                            off = (r * ktn + kt) * 128
                            eng.matmul(s_ps[sl][:], kb_sb[hh % 2][:, off:off + 128],
                                       qcur[hh % 2][:],
                                       start=True, stop=True).then_inc(spe.h, 1)
                            eng.matmul(dum3_ps[:], ones128[:, 0:32], onescol[:],
                                       start=True, stop=True).then_inc(spe.h, 1)
                        return smm

                    def mk_avmm(hh, ti, gt, asl, ntile, band, r, kt, ktn):
                        def avmm(eng):
                            if band:
                                eng.wait_ge(sdve.h, mask_dve[gt])
                            else:
                                eng.wait_ge(sact.h, exp_act[gt])
                            off = (r * ktn + kt) * 128
                            eng.matmul(av_ps[hh % 2][:], vb_sb[hh % 2][:, off:off + 128],
                                       abuf[asl][:], start=(ti == 0),
                                       stop=(ti == ntile - 1)).then_inc(spe.h, 1)
                            eng.matmul(den_ps[hh % 2][:], ones128[:], abuf[asl][:],
                                       start=(ti == 0),
                                       stop=(ti == ntile - 1)).then_inc(spe.h, 1)
                        return avmm

                    def add_avmm(args):
                        hh_, ti_, gt_, asl_, ntile_, band_, r_, kt_, ktn_ = args
                        spe.inc()
                        avmm_den[gt_] = spe.inc()
                        plan.add("pe", mk_avmm(hh_, ti_, gt_, asl_, ntile_, band_,
                                               r_, kt_, ktn_))

                    def mk_exp(pgt, pasl, ppsl, cover):
                        def expf(eng):
                            eng.wait_ge(spe.h, cover)
                            eng.activation(out=abuf[pasl][:], in_=s_ps[ppsl][:],
                                           func=AF.Exp, bias=cp(C_EBIAS),
                                           scale=cp(C_SCALE)).then_inc(sact.h, 1)
                        return expf

                    pend_av = []
                    for r in range(8):
                        for kt in range(ktn):
                            ti = r * ktn + kt
                            band = (kt >= 4 * qb)
                            kbr = kt - 4 * qb
                            sl = gt % 3
                            asl = gt % 4
                            if band:
                                mask_info[gt] = (r, kbr, asl)

                            spe.inc()
                            smm_pe[gt] = spe.inc()
                            plan.add("pe", mk_smm(hh, r, kt, ti, gt, sl, ktn))

                            exp_act[gt] = sact.inc()
                            plan.add("act", mk_exp(gt, asl, sl, smm_pe[gt]))
                            emit_mask(gt)

                            pend_av.append((hh, ti, gt, asl, ntile, band, r, kt, ktn))
                            if len(pend_av) > 2:
                                add_avmm(pend_av.pop(0))
                            gt += 1

                    for a in pend_av:
                        add_avmm(a)

                    def headdummy(eng):
                        eng.matmul(dum3_ps[:], ones128[:, 0:32], onescol[:],
                                   start=True, stop=True).then_inc(spe.h, 1)
                    headfinal_pe[hh] = spe.inc()
                    plan.add("pe", headdummy)

                    rden_act[hh] = sact.inc(2)

                    def rdenf(eng, hh=hh):
                        eng.wait_ge(spe.h, headfinal_pe[hh])
                        if hh >= 1:
                            eng.wait_ge(sdve.h, avevac_dve[hh - 1])
                        eng.activation(out=rdbc_sb[:], in_=den_ps[hh % 2][:],
                                       func=AF.Ln).then_inc(sact.h, 1)
                        eng.activation(out=rdbc_sb[:], in_=rdbc_sb[:], func=AF.Exp,
                                       scale=cp(C_NONE)).then_inc(sact.h, 1)
                    plan.add("act", rdenf)

                    def avevac(eng, hh=hh, h=h, qb=qb):
                        eng.wait_ge(sact.h, rden_act[hh])
                        if qb == 1 and h == 0:
                            eng.wait_ge(spe.h, wo_pe[(0, 15)])
                        eng.tensor_mul(out=av_sb[:, h * 512:(h + 1) * 512],
                                       in0=av_ps[hh % 2][:],
                                       in1=rdbc_sb[:]).then_inc(sdve.h, 1)
                    avevac_dve[hh] = sdve.inc()
                    plan.add("dve", avevac)

                for g in range(16):
                    def wof(eng, qb=qb, g=g):
                        idx = qb * 16 + g
                        if idx >= 2:
                            pqb, pg = divmod(idx - 2, 16)
                            eng.wait_ge(spe.h, wo_pe[(pqb, pg)])
                        eng.dma_start(out=wob[g % 2][:],
                                      in_=wo_in[:, g * 2048:(g + 1) * 2048]).then_inc(s_w[g % 2].h, 16)
                    plan.add("gpsimd", wof)
                    wof_cast[(qb, g)] = s_w[g % 2].inc(16)

                    def womm(eng, qb=qb, g=g):
                        eng.wait_ge(s_w[g % 2].h, wof_cast[(qb, g)])
                        if g == 0:
                            eng.wait_ge(sdve.h, avevac_dve[qb * NH + NH - 1])
                        if g >= 2:
                            eng.wait_ge(sdve.h, woevac_dve[(qb, g - 2)])
                        ins = None
                        for f in range(NF):
                            ins = eng.matmul(s_ps[g % 2][:],
                                             wob[g % 2][:, f * 128:(f + 1) * 128],
                                             av_sb[:, f * 512:(f + 1) * 512],
                                             start=(f == 0), stop=(f == NF - 1))
                        ins.then_inc(spe.h, 1)
                        eng.matmul(dum3_ps[:], ones128[:, 0:32], onescol[:],
                                   start=True, stop=True).then_inc(spe.h, 1)
                    wo_pe[(qb, g)] = spe.inc()
                    wo_cover[(qb, g)] = spe.inc()
                    plan.add("pe", womm)

                    def woevac(eng, qb=qb, g=g):
                        eng.wait_ge(spe.h, wo_cover[(qb, g)])
                        sl_ = stream[:, g * TOK + qb * 512: g * TOK + qb * 512 + 512]
                        eng.scalar_tensor_tensor(out=sl_, in0=s_ps[g % 2][:],
                                                 scalar=bo_sb[:, g:g + 1], in1=sl_,
                                                 op0=ALU.add, op1=ALU.add).then_inc(sdve.h, 1)
                    woevac_dve[(qb, g)] = sdve.inc()
                    plan.add("dve", woevac)

            with nc.Block("p3") as blk:
                @blk.tensor
                def _(eng):
                    eng.wait_ge(scc.h, 1)
                    plan.run("pe", eng)

                @blk.scalar
                def _(eng):
                    plan.run("act", eng)

                @blk.vector
                def _(eng):
                    plan.run("dve", eng)

                @blk.sync
                def _(eng):
                    eng.wait_ge(scc.h, 1)
                    eng.wait_ge(s_qst[0].h, p2_qst[0])
                    eng.wait_ge(s_qst[1].h, p2_qst[1])
                    plan.run("sync", eng)

                @blk.gpsimd
                def _(eng):
                    plan.run("gpsimd", eng)

        if debug:
            with nc.Block("dbgx2") as blk:
                @blk.sync
                def _(eng):
                    eng.dma_start(out=dbg_x2, in_=stream[:]).then_inc(sout.h, 16)
                    sout.n += 16
                    eng.wait_ge(sout.h, sout.n)

        with nc.sbuf_tensor("xn2", [128, NF * TOK], BF16) as xn2:
            emit_layernorm("p4", False, None, False,
                           lambda f: xn2[:, f * TOK:(f + 1) * TOK], {})

            with contextlib.ExitStack() as ps:
                wgb = [ps.enter_context(nc.sbuf_tensor(f"wgb_{i}", [128, NF * 512], BF16))
                       for i in range(2)]
                wub = [ps.enter_context(nc.sbuf_tensor(f"wub_{i}", [128, NF * 512], BF16))
                       for i in range(2)]
                sgb = [ps.enter_context(nc.sbuf_tensor(f"sgb_{i}", [128, 512], F32))
                       for i in range(2)]
                hst = [ps.enter_context(nc.sbuf_tensor(f"hst_{i}", [128, 512], F32))
                       for i in range(4)]
                g_ps = [ps.enter_context(nc.psum_tensor(f"g_ps{i}", [128, 512], F32))
                        for i in range(2)]
                u_ps = [ps.enter_context(nc.psum_tensor(f"u_ps{i}", [128, 512], F32))
                        for i in range(2)]
                dum5_ps = ps.enter_context(nc.psum_tensor("p5_dum", [32, 1], F32))

                def p5dummy(eng):
                    eng.matmul(dum5_ps[:], ones128[:, 0:32], onescol[:],
                               start=True, stop=True).then_inc(spe.h, 1)

                plan = Plan()
                wgu_fetch, gmm_pe, umm_pe = {}, {}, {}
                silu_act, ht_dve, hw_sync = {}, {}, {}

                wg_r = wg_in.rearrange("p (f n) -> p f n", f=NF)
                wu_r = wu_in.rearrange("p (f n) -> p f n", f=NF)

                for js in range(16):
                    def wguf(eng, js=js):
                        if js >= 2:
                            eng.wait_ge(spe.h, umm_pe[(js - 2, NT - 1)])
                        srcg = wg_r[:, :, js * 512:(js + 1) * 512]
                        srcu = wu_r[:, :, js * 512:(js + 1) * 512]
                        dstg = wgb[js % 2][:].rearrange("p (f n) -> p f n", f=NF)
                        dstu = wub[js % 2][:].rearrange("p (f n) -> p f n", f=NF)
                        eng.dma_start(out=dstg[:, 0:8], in_=srcg[:, 0:8]).then_inc(s_w[js % 2].h, 16)
                        eng.dma_start(out=dstg[:, 8:16], in_=srcg[:, 8:16]).then_inc(s_w[js % 2].h, 16)
                        eng.dma_start(out=dstu[:, 0:8], in_=srcu[:, 0:8]).then_inc(s_w[js % 2].h, 16)
                        eng.dma_start(out=dstu[:, 8:16], in_=srcu[:, 8:16]).then_inc(s_w[js % 2].h, 16)
                        if 2 <= js <= 9:
                            i = js - 2
                            eng.dma_start(out=wd_bf.ap()[:, i * 16384:(i + 1) * 16384],
                                          in_=wd_in[:, i * 16384:(i + 1) * 16384]).then_inc(s_wdc.h, 16)
                    plan.add("gpsimd", wguf)
                    wgu_fetch[js] = s_w[js % 2].inc(64)
                    if 2 <= js <= 9:
                        s_wdc.inc(16)

                    for t in range(NT):
                        idx = js * NT + t

                        def gmm(eng, js=js, t=t, idx=idx):
                            eng.wait_ge(s_w[js % 2].h, wgu_fetch[js])
                            if idx >= 2:
                                pj, pt = divmod(idx - 2, NT)
                                eng.wait_ge(sact.h, silu_act[(pj, pt)])
                            ins = None
                            for f in range(NF):
                                ins = eng.matmul(
                                    g_ps[t % 2][:],
                                    xn2[:, f * TOK + t * 128: f * TOK + t * 128 + 128],
                                    wgb[js % 2][:, f * 512:(f + 1) * 512],
                                    start=(f == 0), stop=(f == NF - 1))
                            ins.then_inc(spe.h, 1)
                            p5dummy(eng)
                        spe.inc()
                        gmm_pe[(js, t)] = spe.inc()
                        plan.add("pe", gmm)

                        def umm(eng, js=js, t=t, idx=idx):
                            if idx >= 2:
                                pj, pt = divmod(idx - 2, NT)
                                eng.wait_ge(sdve.h, ht_dve[(pj, pt)])
                            ins = None
                            for f in range(NF):
                                ins = eng.matmul(
                                    u_ps[t % 2][:],
                                    xn2[:, f * TOK + t * 128: f * TOK + t * 128 + 128],
                                    wub[js % 2][:, f * 512:(f + 1) * 512],
                                    start=(f == 0), stop=(f == NF - 1))
                            ins.then_inc(spe.h, 1)
                            p5dummy(eng)
                        spe.inc()
                        umm_pe[(js, t)] = spe.inc()
                        plan.add("pe", umm)

                        def siluf(eng, js=js, t=t, idx=idx):
                            eng.wait_ge(spe.h, gmm_pe[(js, t)])
                            if idx >= 2:
                                pj, pt = divmod(idx - 2, NT)
                                eng.wait_ge(sdve.h, ht_dve[(pj, pt)])
                            eng.activation(out=sgb[t % 2][:], in_=g_ps[t % 2][:],
                                           func=AF.Silu).then_inc(sact.h, 1)
                        silu_act[(js, t)] = sact.inc()
                        plan.add("act", siluf)

                        def htf(eng, js=js, t=t, idx=idx):
                            eng.wait_ge(spe.h, umm_pe[(js, t)])
                            eng.wait_ge(sact.h, silu_act[(js, t)])
                            if idx >= 4:
                                pj, pt = divmod(idx - 4, NT)
                                eng.wait_ge(s_hst[idx % 4].h, hw_sync[(pj, pt)])
                            eng.tensor_mul(out=hst[idx % 4][:], in0=sgb[t % 2][:],
                                           in1=u_ps[t % 2][:]).then_inc(sdve.h, 1)
                        ht_dve[(js, t)] = sdve.inc()
                        plan.add("dve", htf)

                        def hwf(eng, js=js, t=t, idx=idx):
                            eng.wait_ge(sdve.h, ht_dve[(js, t)])
                            eng.dma_start(out=h_dram.ap()[t, :, js * 512:(js + 1) * 512],
                                          in_=hst[idx % 4][:]).then_inc(s_hst[idx % 4].h, 16)
                        plan.add("sync", hwf)
                        hw_sync[(js, t)] = s_hst[idx % 4].inc(16)

                with nc.Block("p5a") as blk:
                    @blk.tensor
                    def _(eng):
                        plan.run("pe", eng)

                    @blk.gpsimd
                    def _(eng):
                        plan.run("gpsimd", eng)

                    @blk.scalar
                    def _(eng):
                        plan.run("act", eng)

                    @blk.vector
                    def _(eng):
                        plan.run("dve", eng)

                    @blk.sync
                    def _(eng):
                        plan.run("sync", eng)
                p5a_hst = [s.n for s in s_hst]

        with contextlib.ExitStack() as ps:
            htile2 = [ps.enter_context(nc.sbuf_tensor(f"htile{i}", [128, II], F32))
                      for i in range(2)]
            hbf2 = [ps.enter_context(nc.sbuf_tensor(f"hbf{i}", [128, II], BF16))
                    for i in range(2)]
            t1 = ps.enter_context(nc.sbuf_tensor("t1", [128, 2048], F32))
            t2 = ps.enter_context(nc.sbuf_tensor("t2", [128, 2048], F32))
            t3 = ps.enter_context(nc.sbuf_tensor("t3", [128, 2048], F32))
            tstage = ps.enter_context(nc.sbuf_tensor("tstage", [128, 2048], BF16))
            tp_ps = [ps.enter_context(nc.psum_tensor(f"tp_ps{i}", [128, 128], BF16))
                     for i in range(6)]
            dum5b_ps = ps.enter_context(nc.psum_tensor("p5b_dum", [32, 1], F32))

            plan = Plan()
            hr_sync, msk_dve, sq_act5, hsw_sync = {}, {}, {}, {}
            tp_pe, tev_n = {}, {}

            def q_(t, k):
                return htile2[t % 2][:, k * 2048:(k + 1) * 2048]

            def qb_(t, k):
                return hbf2[t % 2][:, k * 2048:(k + 1) * 2048]

            for t in range(NT):
                def hrf(eng, t=t):
                    if t == 0:
                        for i in range(4):
                            eng.wait_ge(s_hst[i].h, p5a_hst[i])
                    if t >= 2:
                        eng.wait_ge(spe.h, tp_pe[(t - 2, 63)])
                    eng.dma_start(out=htile2[t % 2][:], in_=h_dram.ap()[t]).then_inc(sh.h, 16)
                plan.add("sync", hrf)
                hr_sync[t] = sh.inc(16)

                def castf(eng, t=t):
                    eng.wait_ge(sh.h, hr_sync[t])
                    eng.tensor_copy(out=hbf2[t % 2][:], in_=htile2[t % 2][:]).then_inc(sdve.h, 1)
                cast_v = sdve.inc()
                plan.add("dve", castf)

                def sqf(eng, t=t, cast_v=cast_v):
                    eng.wait_ge(sdve.h, cast_v)
                    eng.activation(out=htile2[t % 2][:], in_=htile2[t % 2][:],
                                   func=AF.Square).then_inc(sact.h, 1)
                sq_act5[t] = sact.inc()
                plan.add("act", sqf)

                def mskf(eng, t=t):
                    eng.wait_ge(sact.h, sq_act5[t])
                    eng.tensor_tensor(out=t1[:], in0=q_(t, 0), in1=q_(t, 1), op=ALU.min)
                    eng.tensor_tensor(out=t2[:], in0=q_(t, 2), in1=q_(t, 3), op=ALU.min)
                    eng.tensor_max(out=t1[:], in0=t1[:], in1=t2[:])
                    eng.tensor_max(out=t2[:], in0=q_(t, 0), in1=q_(t, 1))
                    eng.tensor_max(out=t3[:], in0=q_(t, 2), in1=q_(t, 3))
                    eng.tensor_tensor(out=t2[:], in0=t2[:], in1=t3[:], op=ALU.min)
                    eng.tensor_max(out=t2[:], in0=t1[:], in1=t2[:])
                    for k in range(4):
                        eng.tensor_tensor(out=q_(t, k), in0=q_(t, k), in1=t2[:], op=ALU.is_ge)
                        ins = eng.tensor_mul(out=qb_(t, k), in0=qb_(t, k), in1=q_(t, k))
                    ins.then_inc(sdve.h, 1)
                msk_dve[t] = sdve.inc()
                plan.add("dve", mskf)

                for ic in range(64):
                    def tpf(eng, t=t, ic=ic):
                        if ic == 0:
                            eng.wait_ge(sdve.h, msk_dve[t])
                            if t >= 1:
                                eng.wait_ge(shs.h, hsw_sync[(t - 1, 7)])
                        if ic >= 6:
                            kind, val = tev_n[(t, ic - 6)]
                            eng.wait_ge(sdve.h if kind == "dve" else sact.h, val)
                        eng.transpose(tp_ps[ic % 6][:],
                                      hbf2[t % 2][:, ic * 128:(ic + 1) * 128],
                                      ident[:]).then_inc(spe.h, 1)
                    tp_pe[(t, ic)] = spe.inc()
                    plan.add("pe", tpf)

                    eng_name = "dve" if ic % 2 == 0 else "act"

                    def tevac(eng, t=t, ic=ic, eng_name=eng_name):
                        eng.wait_ge(spe.h, tp_pe[(t, ic)] + 1)
                        if ic >= 16:
                            eng.wait_ge(shs.h, hsw_sync[(t, ic // 8 - 2)])
                        dst = tstage[:, (ic % 16) * 128:(ic % 16) * 128 + 128]
                        if eng_name == "dve":
                            eng.tensor_copy(out=dst, in_=tp_ps[ic % 6][:]).then_inc(sdve.h, 1)
                        else:
                            eng.activation(out=dst, in_=tp_ps[ic % 6][:],
                                           func=AF.Identity).then_inc(sact.h, 1)
                    if eng_name == "dve":
                        tev_n[(t, ic)] = ("dve", sdve.inc())
                    else:
                        tev_n[(t, ic)] = ("act", sact.inc())
                    plan.add(eng_name, tevac)

                    if ic % 8 == 7:
                        def tstore(eng, t=t, ic8=ic // 8):
                            for icc in range(ic8 * 8, ic8 * 8 + 8):
                                kind, val = tev_n[(t, icc)]
                                eng.wait_ge(sdve.h if kind == "dve" else sact.h, val)
                            src_ = tstage[:, (ic8 % 2) * 1024:(ic8 % 2) * 1024 + 1024]
                            eng.dma_start(
                                out=hs_fm_dram.ap()[ic8 * 8:(ic8 + 1) * 8, :,
                                                    t * 128:(t + 1) * 128].transpose([1, 0, 2]),
                                in_=src_.rearrange("p (i n) -> p i n", i=8)).then_inc(shs.h, 16)
                        plan.add("sync", tstore)
                        hsw_sync[(t, ic // 8)] = shs.inc(16)

                def tpdum(eng, t=t):
                    eng.matmul(dum5b_ps[:], ones128[:, 0:32], onescol[:],
                               start=True, stop=True).then_inc(spe.h, 1)
                spe.inc()
                plan.add("pe", tpdum)
            p5b_shs = shs.n

            with nc.Block("p5b") as blk:
                @blk.tensor
                def _(eng):
                    plan.run("pe", eng)

                @blk.scalar
                def _(eng):
                    plan.run("act", eng)

                @blk.vector
                def _(eng):
                    plan.run("dve", eng)

                @blk.sync
                def _(eng):
                    plan.run("sync", eng)

        with contextlib.ExitStack() as ps:
            hfm = [ps.enter_context(nc.sbuf_tensor(f"hfm_{i}", [128, 512], BF16))
                   for i in range(4)]
            wdb = [ps.enter_context(nc.sbuf_tensor(f"wdb_{i}", [128, 2048], BF16))
                   for i in range(6)]
            y_ps = [ps.enter_context(nc.psum_tensor(f"y_ps{i}", [128, 512], F32))
                    for i in range(8)]

            plan = Plan()
            hs_ts, wd_fetch, dmm_pe, yev_dve = {}, {}, {}, {}
            wd_slot_user = {}

            for half in range(2):
                for gh in range(2):
                    pi = half * 2 + gh
                    for ic in range(64):
                        gi = pi * 64 + ic

                        def hstr(eng, half=half, ic=ic, gi=gi):
                            if gi == 0:
                                eng.wait_ge(shs.h, p5b_shs)
                            if gi >= 4:
                                ppi, pic = divmod(gi - 4, 64)
                                eng.wait_ge(spe.h, dmm_pe[(ppi, pic)])
                            eng.dma_start(
                                out=hfm[ic % 4][:],
                                in_=hs_fm_dram.ap()[ic, :, half * 512:(half + 1) * 512]
                            ).then_inc(s_tp[ic % 4].h, 16)
                        plan.add("sync", hstr)
                        hs_ts[(pi, ic)] = s_tp[ic % 4].inc(16)

                        prev_user = wd_slot_user.get(ic % 6)

                        def wdf(eng, ic=ic, gi=gi, prev_user=prev_user):
                            if gi == 0:
                                eng.wait_ge(s_wdc.h, s_wdc.n)
                            if prev_user is not None:
                                eng.wait_ge(spe.h, dmm_pe[prev_user])
                            eng.dma_start(out=wdb[ic % 6][:],
                                          in_=wd_bf.ap()[:, ic * 2048:(ic + 1) * 2048]).then_inc(s_wd[ic % 6].h, 16)
                        plan.add("gpsimd", wdf)
                        wd_fetch[(pi, ic)] = s_wd[ic % 6].inc(16)
                        wd_slot_user[ic % 6] = (pi, ic)

                        def dmm(eng, gh=gh, ic=ic, pi=pi):
                            eng.wait_ge(s_tp[ic % 4].h, hs_ts[(pi, ic)])
                            eng.wait_ge(s_wd[ic % 6].h, wd_fetch[(pi, ic)])
                            if ic == 0 and pi >= 1:
                                eng.wait_ge(sdve.h, yev_dve[pi - 1])
                            ins = None
                            for gg in range(8):
                                g = gh * 8 + gg
                                ins = eng.matmul(y_ps[gg][:],
                                                 wdb[ic % 6][:, g * 128:(g + 1) * 128],
                                                 hfm[ic % 4][:],
                                                 start=(ic == 0), stop=(ic == 63))
                            ins.then_inc(spe.h, 1)
                        dmm_pe[(pi, ic)] = spe.inc()
                        plan.add("pe", dmm)

                    def yev(eng, half=half, gh=gh, pi=pi):
                        eng.wait_ge(spe.h, dmm_pe[(pi, 63)])
                        ins = None
                        for gg in range(8):
                            g = gh * 8 + gg
                            sl_ = stream[:, g * TOK + half * 512: g * TOK + half * 512 + 512]
                            ins = eng.tensor_add(out=sl_, in0=y_ps[gg][:], in1=sl_)
                        ins.then_inc(sdve.h, 1)
                    yev_dve[pi] = sdve.inc()
                    plan.add("dve", yev)

            with nc.Block("p5c") as blk:
                @blk.tensor
                def _(eng):
                    plan.run("pe", eng)

                @blk.vector
                def _(eng):
                    plan.run("dve", eng)

                @blk.sync
                def _(eng):
                    plan.run("sync", eng)

                @blk.gpsimd
                def _(eng):
                    plan.run("gpsimd", eng)

        if debug:
            with nc.Block("dbgy") as blk:
                @blk.sync
                def _(eng):
                    eng.dma_start(out=dbg_y, in_=stream[:]).then_inc(sout.h, 16)
                    sout.n += 16
                    eng.wait_ge(sout.h, sout.n)

        apply6 = emit_layernorm("p6", False, None, False,
                                lambda f: stream[:, f * TOK:(f + 1) * TOK], {})

        with nc.Block("p6s") as blk:
            @blk.sync
            def _(eng):
                for f in range(NF):
                    eng.wait_ge(sdve.h, apply6[f])
                    eng.dma_start(out=out_t[f * 128:(f + 1) * 128, :],
                                  in_=stream[:, f * TOK:(f + 1) * TOK]).then_inc(sout.h, 16)
                    sout.n += 16
                if debug:
                    eng.dma_start(out=dbg_ccin, in_=cc_in.ap()).then_inc(sout.h, 16)
                    eng.dma_start(out=dbg_q.rearrange("h p t -> (h p) t"),
                                  in_=q_dram.ap().rearrange("h p t -> (h p) t")).then_inc(sout.h, 16)
                    for t in range(NT):
                        eng.dma_start(out=dbg_h[t], in_=h_dram.ap()[t]).then_inc(sout.h, 16)
                    eng.dma_start(out=dbg_hs.rearrange("i p t -> (i p) t"),
                                  in_=hs_fm_dram.ap().rearrange("i p t -> (i p) t")).then_inc(sout.h, 16)
                    sout.n += 16 * (3 + NT)
                eng.wait_ge(sout.h, sout.n)

    return nc


_NC_CACHE = None


def _get_nc():
    global _NC_CACHE
    if _NC_CACHE is None:
        _NC_CACHE = build_program()
    return _NC_CACHE


def kernel(L0_attn_X, L0_attn_ln_g, L0_attn_ln_b, L0_attn_Wq, L0_attn_bq,
           L0_attn_Wk, L0_attn_bk, L0_attn_Wv, L0_attn_bv, L0_attn_Wo,
           L0_attn_bo, L0_ffn_w_gate, L0_ffn_w_up, L0_ffn_w_down):
    X = np.asarray(L0_attn_X)[0]
    Wq = np.asarray(L0_attn_Wq)
    Wk = np.asarray(L0_attn_Wk)
    Wv = np.asarray(L0_attn_Wv)
    Wo = np.asarray(L0_attn_Wo)
    Wg = np.asarray(L0_ffn_w_gate)
    Wu = np.asarray(L0_ffn_w_up)
    Wd = np.asarray(L0_ffn_w_down)

    def lhsT_layout(W):
        n_in = W.shape[0] // 128
        return np.ascontiguousarray(
            W.reshape(n_in, 128, 16, 128).transpose(1, 2, 0, 3).reshape(128, -1))

    def rhs_layout(W):
        return np.ascontiguousarray(
            W.reshape(16, 128, -1).transpose(1, 0, 2).reshape(128, -1))

    perm = 4 * (np.arange(II) % 2048) + np.arange(II) // 2048

    wq_dev = lhsT_layout(Wq)
    wk_dev = lhsT_layout(Wk)
    wo_dev = lhsT_layout(Wo)
    wv_dev = rhs_layout(Wv)
    wg_dev = rhs_layout(Wg[:, perm])
    wu_dev = rhs_layout(Wu[:, perm])
    wd_dev = np.ascontiguousarray(
        Wd[perm, :].reshape(64, 128, 16, 128).transpose(1, 0, 2, 3).reshape(128, -1))

    def colvec(b):
        return np.ascontiguousarray(np.asarray(b).reshape(16, 128).T)

    bq_dev = colvec(L0_attn_bq)
    bk_dev = colvec(L0_attn_bk)
    bo_dev = colvec(L0_attn_bo)
    lng_dev = colvec(L0_attn_ln_g)
    lnb_dev = colvec(L0_attn_ln_b)
    bv_dev = np.ascontiguousarray(np.asarray(L0_attn_bv)[None, :])

    p = np.arange(128)[:, None]
    v = np.arange(896)[None, :]
    in_maps = []
    for c in range(NCORES):
        Xc = np.ascontiguousarray(X[c::8, :].T)
        masks = np.zeros((8, 128, 896), dtype=ml_dtypes.bfloat16)
        for r in range(8):
            delta = 1 if r > c else 0
            masks[r] = ((v - 384 - p) >= delta).astype(ml_dtypes.bfloat16)
        in_maps.append({
            "x": Xc, "wq": wq_dev, "wk": wk_dev, "wv": wv_dev, "wo": wo_dev,
            "wg": wg_dev, "wu": wu_dev, "wd": wd_dev,
            "bq": bq_dev, "bk": bk_dev, "bo": bo_dev, "bvr": bv_dev,
            "lng": lng_dev, "lnb": lnb_dev, "masks": masks,
            "identw": np.eye(128, dtype=ml_dtypes.bfloat16),
        })

    nc = _get_nc()
    res = run_bass_kernel_spmd(nc, in_maps, core_ids=list(range(NCORES)))
    out = np.empty((S, DM), np.float32)
    for c in range(NCORES):
        out[c::8, :] = res.results[c]["out"].T
    return out[None]
